# version 07
# baseline (speedup 1.0000x reference)
import numpy as np
from contextlib import ExitStack
import concourse.bass as bass
import concourse.mybir as mybir
from concourse.bass_utils import run_bass_kernel_spmd

F32 = mybir.dt.float32
BF16 = mybir.dt.bfloat16
AF = mybir.ActivationFunctionType
ALU = mybir.AluOpType
AX = mybir.AxisListType

D, NCH, B, T, H = 1024, 8, 8, 2048, 16
OWN = 2048
NEG = -1.0e30
EPS = 1e-5


class P_:
    def __init__(self, nc, ctx):
        self.sem = {}
        self.cnt = {}
        for n in ("ld", "gp", "pe", "act", "dve", "pool", "st"):
            self.sem[n] = ctx.enter_context(nc.semaphore(n))
            self.cnt[n] = 0
        self.tagged = set()
        self._keep = []

    def inc(self, instr, name, amt=None):
        if id(instr) in self.tagged:
            return
        amt = amt if amt is not None else 1
        instr.then_inc(self.sem[name], amt)
        self.cnt[name] += amt

    def wv(self, eng, name, val):
        if val > 0:
            eng.wait_ge(self.sem[name], val)

    def w(self, eng, *names):
        for n in names:
            if self.cnt[n] > 0:
                eng.wait_ge(self.sem[n], self.cnt[n])

    def wall(self, eng):
        self.w(eng, "ld", "gp", "pe", "act", "dve", "pool", "st")


_PASSTHRU = ("wait_ge", "sem_inc", "nop", "dma_start", "dma_start_transpose",
             "indirect_dma_start", "trigger_dma", "collective_compute")


class SeqProxy:

    def __init__(self, eng, P, nm):
        self._eng, self._P, self._nm = eng, P, nm

    def __getattr__(self, name):
        f = getattr(self._eng, name)
        if name.startswith("_") or name in _PASSTHRU or not callable(f):
            return f

        def wrap(*args, **kw):
            P, nm = self._P, self._nm
            if P.cnt[nm] > 0:
                self._eng.wait_ge(P.sem[nm], P.cnt[nm])
            instr = f(*args, **kw)
            instr.then_inc(P.sem[nm], 1)
            P.cnt[nm] += 1
            P.tagged.add(id(instr))
            P._keep.append(instr)
            return instr
        return wrap


class BlkWrap:
    def __init__(self, blk, P):
        self._blk, self._P = blk, P

    @property
    def sync(self):
        return self._blk.sync

    @property
    def tensor(self):
        return self._blk.tensor

    @property
    def vector(self):
        return lambda fn: self._blk.vector(lambda e: fn(SeqProxy(e, self._P, "dve")))

    @property
    def scalar(self):
        return lambda fn: self._blk.scalar(lambda e: fn(SeqProxy(e, self._P, "act")))

    @property
    def gpsimd(self):
        return lambda fn: self._blk.gpsimd(lambda e: fn(SeqProxy(e, self._P, "pool")))


def build_nc(debug=False):
    nc = bass.Bass("TRN2", target_bir_lowering=False)

    d_xf = nc.dram_tensor("xf", [D, B * T], F32, kind="ExternalInput").ap()
    d_xo = nc.dram_tensor("xo", [D, OWN], F32, kind="ExternalInput").ap()
    d_wqkv = nc.dram_tensor("wqkv", [D, 3 * D], F32, kind="ExternalInput").ap()
    d_wout = nc.dram_tensor("wout", [D, D], F32, kind="ExternalInput").ap()
    d_w = {w: nc.dram_tensor(w, [D, D], F32, kind="ExternalInput").ap()
           for w in ("wf", "wi", "wc", "uf", "ui", "uc", "wo", "uo")}
    d_bv = {v: nc.dram_tensor(v, [128, NCH], F32, kind="ExternalInput").ap()
            for v in ("bfv", "biv", "bcv", "bov", "incv")}
    d_deny = nc.dram_tensor("deny", [128, 32], F32, kind="ExternalInput").ap()
    d_tri = nc.dram_tensor("trimask", [128, 128], F32, kind="ExternalInput").ap()
    d_iden = nc.dram_tensor("iden", [128, 128], F32, kind="ExternalInput").ap()
    d_yo = nc.dram_tensor("yo", [D, OWN], F32, kind="ExternalOutput").ap()

    dkind = "ExternalOutput" if debug else "Internal"
    d_wqkv_bf = nc.dram_tensor("wqkv_bf", [D, 3 * D], BF16)
    d_wout_bf = nc.dram_tensor("wout_bf", [D, D], BF16)
    d_wffn_bf = nc.dram_tensor("wffn_bf", [D, 8 * D], BF16)
    d_qpark = nc.dram_tensor("q_park", [D, OWN], BF16, kind=dkind)
    d_ypark = nc.dram_tensor("y_park", [D, OWN], F32, kind=dkind)
    if debug:
        d_txh = nc.dram_tensor("t_xhat", [128, NCH * OWN], BF16, kind="ExternalOutput").ap()
        d_tkb = nc.dram_tensor("t_kb", [128, NCH * T], BF16, kind="ExternalOutput").ap()
        d_ton = nc.dram_tensor("t_onn", [128, NCH * 256], BF16, kind="ExternalOutput").ap()
        d_tor = nc.dram_tensor("t_onraw", [128, NCH * 256], BF16, kind="ExternalOutput").ap()
        d_trc = nc.dram_tensor("t_rec", [2, 2048], F32, kind="ExternalOutput").ap()
        d_tfa = nc.dram_tensor("t_fac", [128, NCH * 256], F32, kind="ExternalOutput").ap()
        d_tyt = nc.dram_tensor("t_yt", [128, 256], BF16, kind="ExternalOutput").ap()
        d_txr = nc.dram_tensor("t_xr", [128, 256], F32, kind="ExternalOutput").ap()
        d_tpy = nc.dram_tensor("t_py", [128, 256], BF16, kind="ExternalOutput").ap()
        d_twt = nc.dram_tensor("t_wt", [128, 128], F32, kind="ExternalOutput").ap()
        d_tnb = nc.dram_tensor("t_nb", [128, 1], F32, kind="ExternalOutput").ap()
        d_tz = nc.dram_tensor("t_z", [128, NCH * OWN], BF16, kind="ExternalOutput").ap()
    d_rowb = nc.dram_tensor("row_bounce", [1, T], F32)
    d_recb = nc.dram_tensor("rec_bounce", [2, 2048], F32)

    xf_r = d_xf.rearrange("(c p) n -> p c n", p=128)
    xo_r = d_xo.rearrange("(c p) n -> p c n", p=128)
    wqkv_r = d_wqkv.rearrange("(c p) f -> p c f", p=128)
    wout_r = d_wout.rearrange("(c p) f -> p c f", p=128)
    w_r = {k: v.rearrange("(c p) f -> p c f", p=128) for k, v in d_w.items()}
    qpark_r = d_qpark.ap().rearrange("(c p) n -> p c n", p=128)
    wqkvbf_r = d_wqkv_bf.ap().rearrange("(c p) f -> p c f", p=128)
    woutbf_r = d_wout_bf.ap().rearrange("(c p) f -> p c f", p=128)
    wffnbf_r = d_wffn_bf.ap().rearrange("(c p) f -> p c f", p=128)
    FFN_ORDER = ("wf", "wi", "wc", "uf", "ui", "uc", "wo", "uo")

    with ExitStack() as ctx:
        sb = lambda n, sh, dt: ctx.enter_context(nc.sbuf_tensor(n, sh, dt))
        tri = sb("tri", [128, 128], F32)
        iden = sb("iden_s", [128, 128], F32)
        deny = sb("deny_s", [128, 32], F32)
        ones_bf = sb("ones_bf", [128, 1], BF16)
        ones_f1 = sb("ones_f1", [1, 128], F32)
        idy = sb("idy", [1, 1], F32)
        bv = {v: sb(v + "_s", [128, NCH], F32) for v in d_bv}
        ro_c = sb("ro_c", [128, 16], F32)
        lo_c = sb("lo_c", [128, 16], F32)
        co_c = sb("co_c", [128, 16], F32)
        qmax2 = sb("qmax2", [1, 1], F32)
        epsc = sb("epsc", [1, 1], F32)
        rw1 = sb("rw1", [1, T], F32)
        rw2 = sb("rw2", [1, T], F32)

        P = P_(nc, ctx)
        blk = BlkWrap(ctx.enter_context(nc.Block()), P)

        @blk.sync
        def _(s):
            P.inc(s.dma_start(out=tri[:], in_=d_tri), "ld", 16)
            P.inc(s.dma_start(out=iden[:], in_=d_iden), "ld", 16)
            P.inc(s.dma_start(out=deny[:], in_=d_deny), "ld", 16)
            for v in bv:
                P.inc(s.dma_start(out=bv[v][:], in_=d_bv[v]), "ld", 16)

        @blk.vector
        def _(v):
            v.memset(ones_bf[:], 1.0)
            v.memset(ones_f1[:], 1.0)
            v.memset(idy[:], 1.0)
            v.memset(epsc[:], EPS)
            P.inc(v.memset(qmax2[:], 0.0), "dve")

        with ExitStack() as abctx:
            xa_bf = abctx.enter_context(nc.sbuf_tensor("xa_bf", [128, NCH, OWN], BF16))
            mu_own_bf = abctx.enter_context(nc.sbuf_tensor("mu_own_bf", [1, OWN], BF16))

            with ExitStack() as actx:
                asb = lambda n, sh, dt: actx.enter_context(nc.sbuf_tensor(n, sh, dt))
                x2sl = asb("x2sl_a", [128, NCH, 512], BF16)
                wq = asb("wq_a", [128, NCH, 1024], BF16)
                nwq = asb("nwq_a", [1, 1024], BF16)
                qev = asb("qev_a", [128, 512], BF16)
                q2 = asb("q2_a", [128, 512], BF16)
                rbc = asb("rbc_a", [128, 512], F32)
                qscr = asb("qscr_a", [1, 4], F32)

                wstage = asb("wstage_a", [128, NCH, 1024], BF16)

                @blk.gpsimd
                def _(g):
                    P.inc(g.dma_start(out=xa_bf[:], in_=xo_r), "gp", 16)
                    g.wait_ge(P.sem["gp"], P.cnt["gp"])
                    P.inc(g.dma_start(out=wq[:], in_=wqkv_r[:, :, 0:1024]), "gp", 16)
                for seg in range(4):
                    srcv = (wqkv_r[:, :, seg * 1024:(seg + 1) * 1024] if seg < 3
                            else wout_r[:, :, 0:1024])
                    dstv = (d_wqkv_bf.ap().rearrange("(c p) f -> p c f", p=128)[
                                :, :, seg * 1024:(seg + 1) * 1024] if seg < 3
                            else woutbf_r[:, :, 0:1024])
                    @blk.gpsimd
                    def _(g, srcv=srcv, seg=seg):
                        P.w(g, "st")
                        P.inc(g.dma_start(out=wstage[:], in_=srcv), "gp", 16)
                    @blk.sync
                    def _(s, dstv=dstv):
                        P.w(s, "gp")
                        P.inc(s.dma_start(out=dstv, in_=wstage[:]), "st", 16)

                with nc.psum_tensor("psA_su", [1, 2048], F32) as psu, \
                     nc.psum_tensor("psA_sq", [1, 2048], F32) as psq:
                    for sl in range(4):
                        @blk.vector
                        def _(v, sl=sl):
                            P.w(v, "gp", "pe")
                            v.tensor_mul(x2sl[:], xa_bf[:, :, sl * 512:(sl + 1) * 512],
                                         xa_bf[:, :, sl * 512:(sl + 1) * 512])
                        @blk.tensor
                        def _(t, sl=sl):
                            P.w(t, "dve", "gp")
                            for c in range(NCH):
                                t.matmul(psu[:, sl * 512:(sl + 1) * 512], ones_bf[:],
                                         xa_bf[:, c, sl * 512:(sl + 1) * 512],
                                         start=(c == 0), stop=(c == NCH - 1))
                                i = t.matmul(psq[:, sl * 512:(sl + 1) * 512], ones_bf[:],
                                             x2sl[:, c, :], start=(c == 0), stop=(c == NCH - 1))
                            P.inc(i, "pe")
                    @blk.scalar
                    def _(sc):
                        P.w(sc, "pe")
                        sc.mul(rw1[:], psu[:], 1.0 / D)
                        sc.mul(rw2[:], psq[:], 1.0 / D)
                        sc.copy(mu_own_bf[:], rw1[:])
                @blk.vector
                def _(v):
                    P.w(v, "act")
                    for sl in range(4):
                        v.tensor_mul(rbc[0:1, :], rw1[:, sl * 512:(sl + 1) * 512],
                                     rw1[:, sl * 512:(sl + 1) * 512])
                        v.tensor_sub(rw2[:, sl * 512:(sl + 1) * 512],
                                     rw2[:, sl * 512:(sl + 1) * 512], rbc[0:1, 0:512])
                @blk.scalar
                def _(sc):
                    P.w(sc, "dve")
                    sc.activation(rw2[:], rw2[:], AF.Ln, bias=epsc[:])
                    sc.activation(rw2[:], rw2[:], AF.Exp, scale=-0.5)
                with nc.psum_tensor("psA_tr", [128, 16], F32) as ptr:
                    @blk.tensor
                    def _(t):
                        P.w(t, "act")
                        for jt in range(16):
                            i = t.transpose(ptr[:, jt:jt + 1], rw2[:, jt * 128:(jt + 1) * 128],
                                            idy[:])
                        P.inc(i, "pe")
                    @blk.vector
                    def _(v):
                        P.w(v, "pe")
                        v.tensor_copy(ro_c[:], ptr[:])
                        v.reciprocal(co_c[:], ptr[:])
                @blk.scalar
                def _(sc):
                    P.w(sc, "dve")
                    sc.activation(lo_c[:], ro_c[:], AF.Ln)
                    sc.mul(rw1[:], rw2[:], 0.125)
                @blk.gpsimd
                def _(g):
                    P.w(g, "act")
                    P.inc(g.dma_start(out=d_rowb.ap(), in_=rw1[:]), "gp", 16)
                with nc.psum_tensor("psA_ws", [1, 128], F32) as pws:
                    for fc in range(8):
                        @blk.tensor
                        def _(t, fc=fc):
                            P.w(t, "gp")
                            if fc:
                                P.w(t, "act")
                            for ec in range(NCH):
                                i = t.matmul(pws[:], ones_bf[:],
                                             wq[:, ec, fc * 128:(fc + 1) * 128],
                                             start=(ec == 0), stop=(ec == NCH - 1))
                            P.inc(i, "pe")
                        @blk.scalar
                        def _(sc, fc=fc):
                            P.w(sc, "pe")
                            sc.mul(nwq[:, fc * 128:(fc + 1) * 128], pws[:], -1.0)
                with nc.psum_tensor("psA_q", [128, 512], F32) as pq, \
                     nc.psum_tensor("psA_n0", [1, 512], F32) as pn0, \
                     nc.psum_tensor("psA_n1", [1, 512], F32) as pn1:
                    for sl in range(4):
                        @blk.gpsimd
                        def _(g, sl=sl):
                            P.w(g, "gp", "dve")
                            P.inc(g.dma_start(
                                out=rbc[:],
                                in_=d_rowb.ap().partition_broadcast(128)[:, 0, sl * 512:(sl + 1) * 512]),
                                "gp", 16)
                        for fc in range(8):
                            @blk.tensor
                            def _(t, fc=fc, sl=sl):
                                P.w(t, "gp", "act", "dve")
                                for c in range(NCH):
                                    t.matmul(pq[:], wq[:, c, fc * 128:(fc + 1) * 128],
                                             xa_bf[:, c, sl * 512:(sl + 1) * 512],
                                             start=(c == 0), stop=False)
                                i = t.matmul(pq[:], nwq[:, fc * 128:(fc + 1) * 128],
                                             mu_own_bf[:, sl * 512:(sl + 1) * 512],
                                             start=False, stop=True)
                                P.inc(i, "pe")
                            @blk.vector
                            def _(v, fc=fc, sl=sl):
                                P.w(v, "pe", "st")
                                v.tensor_mul(qev[:], pq[:], rbc[:])
                                v.tensor_mul(q2[:], qev[:], qev[:])
                            @blk.sync
                            def _(s, fc=fc, sl=sl):
                                P.w(s, "dve")
                                P.inc(s.dma_start(
                                    out=d_qpark.ap()[fc * 128:(fc + 1) * 128,
                                                     sl * 512:(sl + 1) * 512],
                                    in_=qev[:]), "st", 16)
                            @blk.tensor
                            def _(t, fc=fc):
                                P.w(t, "dve")
                                t.matmul(pn0[:], ones_bf[0:64, :], q2[0:64, :],
                                         start=True, stop=True)
                                P.inc(t.matmul(pn1[:], ones_bf[64:128, :], q2[64:128, :],
                                               start=True, stop=True), "pe")
                            @blk.vector
                            def _(v):
                                P.w(v, "pe", "gp")
                                v.tensor_reduce(qscr[:, 0:1], pn0[:], axis=AX.X, op=ALU.max)
                                v.tensor_reduce(qscr[:, 1:2], pn1[:], axis=AX.X, op=ALU.max)
                                v.tensor_max(qscr[:, 0:1], qscr[:, 0:1], qscr[:, 1:2])
                                v.tensor_max(qmax2[:], qmax2[:], qscr[:, 0:1])
            for dec in (blk.sync, blk.gpsimd, blk.tensor, blk.scalar, blk.vector):
                @dec
                def _(e):
                    P.wall(e)
            if debug:
                @blk.sync
                def _(s):
                    P.inc(s.dma_start(out=d_txh, in_=xa_bf[:].rearrange("p c n -> p (c n)")),
                          "st", 16)
                    s.wait_ge(P.sem["st"], P.cnt["st"])

            with ExitStack() as bctx:
                bsb = lambda n, sh, dt: bctx.enter_context(nc.sbuf_tensor(n, sh, dt))
                xb_bf = bsb("xb_bf", [128, NCH, T], BF16)
                kb = bsb("kb", [128, NCH, T], BF16)
                vb = bsb("vb", [128, 16, 1040], BF16)
                kob = bsb("kob", [128, NCH, 256], BF16)
                vob = bsb("vob", [128, 2, 1040], BF16)
                nwv = bsb("nwv", [1, 1024], BF16)
                nwk = bsb("nwk", [1, 128], BF16)
                mu_b_bf = bsb("mu_b_bf", [1, T], BF16)
                wt = bsb("wt", [128, 2, NCH, 128], BF16)
                qb = bsb("qb", [128, NCH, 256], BF16)
                scr8 = bsb("scr8", [128, NCH, 256], F32)
                yt8 = bsb("yt8", [128, 2, 256], F32)
                r_cb = bsb("r_cb", [128, 16], F32)
                l_cb = bsb("l_cb", [128, 16], F32)
                c_cb = bsb("c_cb", [128, 16], F32)
                btab = bsb("btab", [128, 32], F32)
                btabd = bsb("btabd", [128, 2], F32)
                negb = bsb("negb", [128, 1], F32)
                bscr = bsb("bscr", [1, 8], F32)


                wstage_b = bsb("wstage_b", [128, NCH, 512], BF16)
                wtuse = [0, 0]
                for seg in range(16):
                    w8 = FFN_ORDER[seg // 2]
                    half = seg % 2
                    @blk.gpsimd
                    def _(g, w8=w8, half=half):
                        P.w(g, "st")
                        P.inc(g.dma_start(out=wstage_b[:],
                                          in_=w_r[w8][:, :, half * 512:(half + 1) * 512]),
                              "gp", 16)
                    @blk.sync
                    def _(s, seg=seg, w8=w8, half=half):
                        P.w(s, "gp")
                        P.inc(s.dma_start(
                            out=wffnbf_r[:, :, seg * 512:(seg + 1) * 512],
                            in_=wstage_b[:]), "st", 16)
                for b in range(B):
                    bscope = ExitStack()
                    x2b = bscope.enter_context(nc.sbuf_tensor(f"x2b_{b}", [128, NCH, 128], BF16))
                    wvres = bscope.enter_context(
                        nc.sbuf_tensor(f"wvres_{b}", [128, NCH, 512], BF16))
                    @blk.gpsimd
                    def _(g, b=b):
                        P.w(g, "pe", "dve")
                        P.inc(g.dma_start(out=xb_bf[:],
                                          in_=xf_r[:, :, b * T:(b + 1) * T]), "gp", 16)
                    with nc.psum_tensor(f"psB_su{b}", [1, 2048], F32) as psu, \
                         nc.psum_tensor(f"psB_sq{b}", [1, 2048], F32) as psq:
                        for sl in range(16):
                            @blk.vector
                            def _(v, sl=sl):
                                P.w(v, "gp", "pe")
                                v.tensor_mul(x2b[:], xb_bf[:, :, sl * 128:(sl + 1) * 128],
                                             xb_bf[:, :, sl * 128:(sl + 1) * 128])
                            @blk.tensor
                            def _(t, sl=sl):
                                P.w(t, "dve", "gp")
                                if sl == 0:
                                    P.w(t, "act")
                                for c in range(NCH):
                                    t.matmul(psu[:, sl * 128:(sl + 1) * 128], ones_bf[:],
                                             xb_bf[:, c, sl * 128:(sl + 1) * 128],
                                             start=(c == 0), stop=(c == NCH - 1))
                                    i = t.matmul(psq[:, sl * 128:(sl + 1) * 128], ones_bf[:],
                                                 x2b[:, c, :], start=(c == 0), stop=(c == NCH - 1))
                                P.inc(i, "pe")
                        @blk.scalar
                        def _(sc):
                            P.w(sc, "pe")
                            sc.mul(rw1[:], psu[:], 1.0 / D)
                            sc.mul(rw2[:], psq[:], 1.0 / D)
                            sc.copy(mu_b_bf[:], rw1[:])
                    @blk.vector
                    def _(v):
                        P.w(v, "act")

                        s8f = scr8[:].rearrange("p c n -> p (c n)")
                        for sl in range(4):
                            v.tensor_mul(s8f[0:1, 0:512],
                                         rw1[:, sl * 512:(sl + 1) * 512],
                                         rw1[:, sl * 512:(sl + 1) * 512])
                            v.tensor_sub(rw2[:, sl * 512:(sl + 1) * 512],
                                         rw2[:, sl * 512:(sl + 1) * 512], s8f[0:1, 0:512])
                    @blk.scalar
                    def _(sc):
                        P.w(sc, "dve")
                        sc.activation(rw1[:], rw2[:], AF.Ln, bias=epsc[:])
                        sc.activation(rw2[:], rw1[:], AF.Exp, scale=-0.5)
                    with nc.psum_tensor(f"psB_tr{b}", [128, 16], F32) as ptrb:
                        @blk.tensor
                        def _(t):
                            P.w(t, "act")
                            for jt in range(16):
                                i = t.transpose(ptrb[:, jt:jt + 1],
                                                rw2[:, jt * 128:(jt + 1) * 128], idy[:])
                            P.inc(i, "pe")
                        @blk.vector
                        def _(v):
                            P.w(v, "pe")
                            v.tensor_copy(r_cb[:], ptrb[:])
                            v.reciprocal(c_cb[:], ptrb[:])
                    @blk.scalar
                    def _(sc):
                        P.w(sc, "dve")
                        sc.activation(l_cb[:], r_cb[:], AF.Ln)
                    @blk.vector
                    def _(v):
                        P.w(v, "act")
                        v.tensor_reduce(bscr[:, 1:2], rw2[:], axis=AX.X, op=ALU.max)
                        v.tensor_reduce(bscr[:, 2:3], rw1[:], axis=AX.X, op=ALU.min)
                        v.tensor_scalar_mul(bscr[:, 2:3], bscr[:, 2:3], -0.5)
                    for fi in range(8):
                        @blk.sync
                        def _(s, fi=fi):
                            P.wv(s, "pe", wtuse[fi % 2])
                            P.inc(s.dma_start(out=wt[:, fi % 2],
                                              in_=wqkvbf_r[:, :, 1024 + fi * 128:1024 + (fi + 1) * 128]),
                                  "ld", 16)
                        with nc.psum_tensor(f"psB_wk{b}_{fi}", [1, 128], F32) as pws2:
                            @blk.tensor
                            def _(t, fi=fi, pws2=pws2):
                                P.w(t, "ld", "act")
                                for ec in range(NCH):
                                    i = t.matmul(pws2[:], ones_bf[:], wt[:, fi % 2, ec, :],
                                                 start=(ec == 0), stop=(ec == NCH - 1))
                                P.inc(i, "pe")
                            @blk.scalar
                            def _(sc, pws2=pws2):
                                P.w(sc, "pe")
                                sc.mul(nwk[:], pws2[:], -1.0)
                        with nc.psum_tensor(f"psB_k{b}_{fi}", [128, 512], F32) as pk:
                            for tt in range(4):
                                @blk.tensor
                                def _(t, fi=fi, tt=tt, pk=pk, b=b):
                                    P.w(t, "act", "dve")
                                    for c in range(NCH):
                                        t.matmul(pk[:], wt[:, fi % 2, c, :],
                                                 xb_bf[:, c, tt * 512:(tt + 1) * 512],
                                                 start=(c == 0), stop=False)
                                    i = t.matmul(pk[:], nwk[:],
                                                 mu_b_bf[:, tt * 512:(tt + 1) * 512],
                                                 start=False, stop=True)
                                    P.inc(i, "pe")
                                @blk.vector
                                def _(v, fi=fi, tt=tt, pk=pk):
                                    P.w(v, "pe")
                                    v.tensor_copy(kb[:, fi, tt * 512:(tt + 1) * 512], pk[:])
                            with nc.psum_tensor(f"psB_ko{b}_{fi}", [128, 256], F32) as pko:
                                @blk.tensor
                                def _(t, fi=fi, b=b, pko=pko):
                                    P.w(t, "dve")
                                    for c in range(NCH):
                                        t.matmul(pko[:], wt[:, fi % 2, c, :],
                                                 xa_bf[:, c, b * 256:(b + 1) * 256],
                                                 start=(c == 0), stop=False)
                                    i = t.matmul(pko[:], nwk[:],
                                                 mu_own_bf[:, b * 256:(b + 1) * 256],
                                                 start=False, stop=True)
                                    P.inc(i, "pe")
                                @blk.vector
                                def _(v, fi=fi, pko=pko):
                                    P.w(v, "pe")
                                    v.tensor_copy(kob[:, fi, :], pko[:])
                            wtuse[fi % 2] = P.cnt["pe"]
                    with nc.psum_tensor(f"psB_v0_{b}", [128, 512], F32) as pv0, \
                         nc.psum_tensor(f"psB_v1_{b}", [128, 512], F32) as pv1, \
                         nc.psum_tensor(f"psB_wsv{b}", [1, 512], F32) as pwsv:
                        pvs = (pv0, pv1)
                        for ft in range(2):
                            @blk.sync
                            def _(s, ft=ft):
                                P.w(s, "pe")
                                P.inc(s.dma_start(out=wvres[:],
                                                  in_=wqkvbf_r[:, :, 2048 + ft * 512:
                                                               2048 + (ft + 1) * 512]), "ld", 16)
                            @blk.tensor
                            def _(t, ft=ft):
                                P.w(t, "ld", "act")
                                for ec in range(NCH):
                                    i = t.matmul(pwsv[:], ones_bf[:], wvres[:, ec, :],
                                                 start=(ec == 0), stop=(ec == NCH - 1))
                                P.inc(i, "pe")
                            @blk.scalar
                            def _(sc, ft=ft):
                                P.w(sc, "pe")
                                sc.mul(nwv[:, 0:512], pwsv[:], -1.0)
                            for jt in range(16):
                                pv = pvs[jt % 2]
                                @blk.tensor
                                def _(t, jt=jt, ft=ft, pv=pv, b=b):
                                    P.w(t, "dve", "act")
                                    for ec in range(NCH):
                                        t.matmul(pv[:],
                                                 xb_bf[:, ec, jt * 128:(jt + 1) * 128],
                                                 wvres[:, ec, :],
                                                 start=(ec == 0), stop=False)
                                    i = t.matmul(pv[:],
                                                 mu_b_bf[:, jt * 128:(jt + 1) * 128],
                                                 nwv[:, 0:512],
                                                 start=False, stop=True)
                                    P.inc(i, "pe")
                                @blk.vector
                                def _(v, jt=jt, ft=ft, pv=pv):
                                    P.w(v, "pe")
                                    dst = vb[:].rearrange("p jt (h e) -> p jt h e", h=16)[
                                        :, jt, ft * 8:(ft + 1) * 8, 0:64]
                                    v.tensor_copy(dst, pv[:].rearrange("p (h e) -> p h e", h=8))
                            for sj in range(2):
                                pv = pvs[sj]
                                @blk.tensor
                                def _(t, sj=sj, ft=ft, b=b, pv=pv):
                                    P.w(t, "dve")
                                    for ec in range(NCH):
                                        t.matmul(pv[:],
                                                 xa_bf[:, ec, b * 256 + sj * 128:
                                                        b * 256 + (sj + 1) * 128],
                                                 wvres[:, ec, :],
                                                 start=(ec == 0), stop=False)
                                    i = t.matmul(pv[:],
                                                 mu_own_bf[:, b * 256 + sj * 128:
                                                           b * 256 + (sj + 1) * 128],
                                                 nwv[:, 0:512],
                                                 start=False, stop=True)
                                    P.inc(i, "pe")
                                @blk.vector
                                def _(v, sj=sj, ft=ft, pv=pv):
                                    P.w(v, "pe")
                                    dst = vob[:].rearrange("p jt (h e) -> p jt h e", h=16)[
                                        :, sj, ft * 8:(ft + 1) * 8, 0:64]
                                    v.tensor_copy(dst, pv[:].rearrange("p (h e) -> p h e", h=8))
                        for jt in range(16):
                            @blk.vector
                            def _(v, jt=jt):
                                dst = vb[:].rearrange("p jt (h e) -> p jt h e", h=16)[:, jt, :, 64:65]
                                src2 = bass.AP(c_cb, jt, [list(c_cb.ap().ap[0]), [0, 16], [1, 1]])
                                v.tensor_copy(dst, src2)
                        for sj in range(2):
                            @blk.vector
                            def _(v, sj=sj, b=b):
                                dst = vob[:].rearrange("p jt (h e) -> p jt h e", h=16)[:, sj, :, 64:65]
                                src2 = bass.AP(co_c, b * 2 + sj, [list(co_c.ap().ap[0]), [0, 16], [1, 1]])
                                v.tensor_copy(dst, src2)

                    @blk.vector
                    def _(v):
                        P.w(v, "act", "pe")
                        P.inc(v.tensor_reduce(
                            negb[:].rearrange("p (a c) -> p a c", a=1), kb[:],
                            axis=AX.XY, op=ALU.max, apply_absolute_value=True), "dve")
                    with nc.psum_tensor(f"psB_km{b}", [1, 128], F32) as pkm, \
                         nc.psum_tensor(f"psB_nb{b}", [128, 1], F32) as pnb:
                        @blk.tensor
                        def _(t, pkm=pkm):
                            P.w(t, "dve")
                            P.inc(t.transpose(pkm[:], negb[:], iden[:]), "pe")
                        @blk.vector
                        def _(v, pkm=pkm):
                            P.w(v, "pe")
                            v.tensor_reduce(bscr[:, 0:1], pkm[:], axis=AX.X, op=ALU.max)
                            P.inc(v.tensor_mul(bscr[:, 0:1], bscr[:, 0:1], bscr[:, 1:2]), "dve")
                        @blk.scalar
                        def _(sc):
                            P.w(sc, "dve")
                            sc.activation(bscr[:, 3:4], qmax2[:], AF.Ln)
                            P.inc(sc.activation(bscr[:, 3:4], bscr[:, 3:4], AF.Exp, scale=0.5),
                                  "act")
                        @blk.vector
                        def _(v):
                            P.w(v, "act")
                            v.tensor_mul(bscr[:, 0:1], bscr[:, 0:1], bscr[:, 3:4])
                            v.tensor_scalar_mul(bscr[:, 0:1], bscr[:, 0:1], 8.0)
                            v.tensor_add(bscr[:, 0:1], bscr[:, 0:1], bscr[:, 2:3])
                            P.inc(v.tensor_scalar_mul(bscr[:, 0:1], bscr[:, 0:1], -1.0), "dve")
                        @blk.tensor
                        def _(t, pnb=pnb):
                            P.w(t, "dve")
                            P.inc(t.matmul(pnb[:], ones_f1[:], bscr[:, 0:1],
                                           start=True, stop=True), "pe")
                        @blk.vector
                        def _(v, pnb=pnb):
                            P.w(v, "pe")
                            P.inc(v.tensor_copy(negb[:], pnb[:]), "dve")
                    @blk.vector
                    def _(v, b=b):
                        P.w(v, "dve")
                        lview = bass.AP(l_cb, 0, [list(l_cb.ap().ap[0]), [0, 2], [1, 16]])
                        v.tensor_add(btab[:].rearrange("p (s k) -> p s k", s=2), lview,
                                     deny[:].rearrange("p (s k) -> p s k", s=2))
                        v.tensor_scalar_add(btab[:], btab[:], negb[:])
                        v.tensor_copy(btabd[:], lo_c[:, 2 * b:2 * b + 2])
                        P.inc(v.tensor_scalar_add(btabd[:], btabd[:], negb[:]), "dve")
                    @blk.sync
                    def _(s, b=b):
                        P.w(s, "st", "pe")
                        P.inc(s.dma_start(out=qb[:],
                                          in_=qpark_r[:, :, b * 256:(b + 1) * 256]), "ld", 16)
                    @blk.tensor
                    def _(t):
                        P.w(t, "ld", "dve", "act", "pool")
                    @blk.scalar
                    def _(sc):
                        P.w(sc, "dve", "pool")
                    bscope.close()

                    with ExitStack() as sctx:
                        ssb = lambda n, sh, dt: sctx.enter_context(nc.sbuf_tensor(n, sh, dt))
                        psb = [[ssb(f"psb{se}{h}_{b}", [128, 512], BF16) for h in range(2)]
                               for se in range(2)]
                        onraw = ssb(f"onraw_{b}", [128, NCH, 256], F32)
                        onn = ssb(f"onn_{b}", [128, NCH, 256], BF16)
                        rec0, rec1 = rw1, rw2
                        recs = (rec0, rec1)
                        with nc.psum_tensor(f"psB_sc00_{b}", [128, 512], F32) as sc00, \
                             nc.psum_tensor(f"psB_sc01_{b}", [128, 512], F32) as sc01, \
                             nc.psum_tensor(f"psB_sc10_{b}", [128, 512], F32) as sc10, \
                             nc.psum_tensor(f"psB_sc11_{b}", [128, 512], F32) as sc11, \
                             nc.psum_tensor(f"psB_av0_{b}", [128, 512], F32) as av0, \
                             nc.psum_tensor(f"psB_av1_{b}", [128, 512], F32) as av1:
                            sc = ((sc00, sc01), (sc10, sc11))
                            av = (av0, av1)
                            for hpg in range(2):
                                hps = [4 * hpg + i_ for i_ in range(4)]
                                for s_ in range(2):
                                    units = [(True, 0)] + [(False, k) for k in range(8 * (s_ + 1))]
                                    expcnt = [0, 0]
                                    for ui, (isdiag, k) in enumerate(units):
                                        se = ui % 2
                                        last_u = (ui == len(units) - 1)
                                        @blk.tensor
                                        def _(t, s_=s_, isdiag=isdiag, k=k, ui=ui, se=se,
                                              hps=hps, expcnt=expcnt):
                                            P.wv(t, "act", expcnt[se])
                                            if ui == 0:
                                                P.w(t, "dve", "act")
                                            for hi, hp in enumerate(hps):
                                                lt = (kob[:, hp, s_ * 128:(s_ + 1) * 128] if isdiag
                                                      else kb[:, hp, k * 128:(k + 1) * 128])
                                                t.matmul(sc[se][0][:, hi * 128:(hi + 1) * 128],
                                                         lt[0:64, :],
                                                         qb[0:64, hp, s_ * 128:(s_ + 1) * 128],
                                                         start=True, stop=True, tile_position=(0, 0))
                                                i = t.matmul(sc[se][1][:, hi * 128:(hi + 1) * 128],
                                                             lt[64:128, :],
                                                             qb[64:128, hp, s_ * 128:(s_ + 1) * 128],
                                                             start=True, stop=True,
                                                             tile_position=(64, 0))
                                            P.inc(i, "pe")
                                        if isdiag:
                                            @blk.vector
                                            def _(v, se=se):
                                                P.w(v, "pe")
                                                trib = bass.AP(tri, 0,
                                                               [list(tri.ap().ap[0]), [0, 4], [1, 128]])
                                                for h01 in range(2):
                                                    v.tensor_add(
                                                        sc[se][h01][:].rearrange("p (hp i) -> p hp i", hp=4),
                                                        sc[se][h01][:].rearrange("p (hp i) -> p hp i", hp=4),
                                                        trib)
                                        @blk.scalar
                                        def _(scl_, s_=s_, isdiag=isdiag, k=k, b=b, se=se,
                                              expcnt=expcnt):
                                            P.w(scl_, "pe")
                                            if isdiag:
                                                P.w(scl_, "dve")
                                                scalev = ro_c[:, 2 * b + s_: 2 * b + s_ + 1]
                                                bia = btabd[:, s_:s_ + 1]
                                            else:
                                                scalev = r_cb[:, k:k + 1]
                                                bia = btab[:, s_ * 16 + k: s_ * 16 + k + 1]
                                            for h01 in range(2):
                                                scl_.activation(psb[se][h01][:], sc[se][h01][:],
                                                                AF.Exp, bias=bia, scale=scalev)
                                            expcnt[se] = P.cnt["act"]
                                        @blk.tensor
                                        def _(t, s_=s_, isdiag=isdiag, k=k, ui=ui, se=se,
                                              hps=hps, last_u=last_u):
                                            P.w(t, "act")
                                            for h01 in range(2):
                                                for hi, hp in enumerate(hps):
                                                    h = 2 * hp + h01
                                                    vt = (vob[:, s_, h * 65:(h + 1) * 65] if isdiag
                                                          else vb[:, k, h * 65:(h + 1) * 65])
                                                    i = t.matmul(
                                                        av[h01][0:65, hi * 128:(hi + 1) * 128],
                                                        vt, psb[se][h01][:, hi * 128:(hi + 1) * 128],
                                                        start=(ui == 0 and hi == 0), stop=last_u)
                                            P.inc(i, "pe")
                                    @blk.vector
                                    def _(v, s_=s_, hps=hps):
                                        P.w(v, "pe")
                                        for h01 in range(2):
                                            for hi, hp in enumerate(hps):
                                                v.tensor_copy(
                                                    onraw[h01 * 64:(h01 + 1) * 64, hp,
                                                          s_ * 128:(s_ + 1) * 128],
                                                    av[h01][0:64, hi * 128:(hi + 1) * 128])
                                                off = hp * 256 + s_ * 128
                                                v.reciprocal(
                                                    recs[h01][:, off:off + 128],
                                                    av[h01][64:65, hi * 128:(hi + 1) * 128])
                        @blk.gpsimd
                        def _(g):
                            P.w(g, "dve", "gp")
                            P.inc(g.dma_start(out=d_recb.ap()[0:1, :], in_=rec0[:]), "gp", 16)
                            P.inc(g.dma_start(out=d_recb.ap()[1:2, :], in_=rec1[:]), "gp", 16)
                            g.wait_ge(P.sem["gp"], P.cnt["gp"])
                            for a in range(2):
                                rsrc = bass.AP(d_recb, a * 2048, [[0, 64], [1, 2048]])
                                rdst = scr8[a * 64:(a + 1) * 64, :, :].rearrange(
                                    "p c n -> p (c n)")
                                P.inc(g.dma_start(out=rdst, in_=rsrc), "gp", 16)
                        @blk.vector
                        def _(v):
                            P.w(v, "gp", "dve")
                            P.inc(v.tensor_mul(onn[:], onraw[:], scr8[:]), "dve")
                        if debug and b == 1:
                            @blk.sync
                            def _(s):
                                P.w(s, "dve")
                                P.inc(s.dma_start(out=d_ton,
                                                  in_=onn[:].rearrange("p c n -> p (c n)")),
                                      "st", 16)
                                P.inc(s.dma_start(out=d_tor,
                                                  in_=kob[:].rearrange("p c n -> p (c n)")),
                                      "st", 16)
                                P.inc(s.dma_start(out=d_tyt,
                                                  in_=vob[:, 0, 0:256]), "st", 16)
                                P.inc(s.dma_start(out=d_tpy,
                                                  in_=vob[:, 0, 256:512]), "st", 16)
                                P.inc(s.dma_start(out=d_tfa,
                                                  in_=scr8[:].rearrange("p c n -> p (c n)")),
                                      "st", 16)
                                P.inc(s.dma_start(out=d_trc[0:1, 0:2048], in_=bass.AP(ro_c, 0, [[16, 128], [1, 16]]).rearrange("p c -> (p c)")[None, :] if False else rec0[:]), "st", 16)
                                P.inc(s.dma_start(out=d_trc[1:2, :], in_=rec1[:]), "st", 16)
                                P.inc(s.dma_start(out=d_twt[:, 0:16], in_=ro_c[:]), "st", 16)
                                P.inc(s.dma_start(out=d_twt[:, 16:32], in_=c_cb[:]), "st", 16)
                                P.inc(s.dma_start(out=d_twt[:, 32:48], in_=co_c[:]), "st", 16)
                                P.inc(s.dma_start(out=d_twt[:, 48:64], in_=r_cb[:]), "st", 16)
                                s.wait_ge(P.sem["st"], P.cnt["st"])
                        @blk.sync
                        def _(s, b=b):
                            P.w(s, "dve", "act", "pool", "st")
                            P.inc(s.dma_start(out=scr8[:],
                                              in_=xo_r[:, :, b * 256:(b + 1) * 256]), "ld", 16)
                        with nc.psum_tensor(f"psB_y0_{b}", [128, 256], F32) as py0, \
                             nc.psum_tensor(f"psB_y1_{b}", [128, 256], F32) as py1:
                            pys = (py0, py1)
                            wtuse2 = wtuse
                            for fc in range(8):
                                py = pys[fc % 2]
                                @blk.sync
                                def _(s, fc=fc):
                                    P.wv(s, "pe", wtuse2[fc % 2])
                                    P.w(s, "dve") if fc == 0 else None
                                    P.inc(s.dma_start(out=wt[:, fc % 2],
                                                      in_=woutbf_r[:, :, fc * 128:(fc + 1) * 128]),
                                          "ld", 16)
                                @blk.tensor
                                def _(t, fc=fc, py=py):
                                    P.w(t, "ld", "dve")
                                    if fc == 0:
                                        P.w(t, "act")
                                    for ec in range(NCH):
                                        i = t.matmul(py[:], wt[:, fc % 2, ec, :], onn[:, ec, :],
                                                     start=(ec == 0), stop=(ec == NCH - 1))
                                    P.inc(i, "pe")
                                @blk.vector
                                def _(v, fc=fc, py=py):
                                    P.w(v, "pe", "ld", "st")
                                    v.tensor_add(yt8[:, fc % 2, :], py[:], scr8[:, fc, :])
                                wtuse2[fc % 2] = P.cnt["pe"]
                                if fc % 2 == 1:
                                    @blk.sync
                                    def _(s, fc=fc, b=b):
                                        P.w(s, "dve")
                                        P.inc(s.dma_start(
                                            out=d_ypark.ap().rearrange("(c p) n -> p c n", p=128)[
                                                :, fc - 1:fc + 1, b * 256:(b + 1) * 256],
                                            in_=yt8[:]), "st", 16)
                    @blk.scalar
                    def _(sc):
                        P.w(sc, "gp", "dve")
                    @blk.vector
                    def _(v):
                        P.w(v, "gp", "st")
                    if debug and b == 0:
                        @blk.sync
                        def _(s):
                            P.inc(s.dma_start(out=d_tkb, in_=kb[:].rearrange("p c n -> p (c n)")),
                                  "st", 16)
                            P.inc(s.dma_start(out=d_tnb, in_=negb[:]), "st", 16)
                            s.wait_ge(P.sem["st"], P.cnt["st"])
        ypark_r = d_ypark.ap().rearrange("(c p) n -> p c n", p=128)

        with ExitStack() as cctx:
            csb = lambda n, sh, dt: cctx.enter_context(nc.sbuf_tensor(n, sh, dt))
            z = csb("z", [128, NCH, OWN], BF16)
            cuo = csb("cuo", [128, NCH, 256], F32)
            wt_c = csb("wt_c", [128, 2, NCH, 128], BF16)
            ysl = csb("ysl", [128, NCH, 256], F32)
            ybf_c = csb("ybf_c", [128, NCH, 256], BF16)
            tmp_c = csb("tmp_c", [128, 256], F32)

            with nc.psum_tensor("psC_su", [1, 2048], F32) as psu, \
                 nc.psum_tensor("psC_sq", [1, 2048], F32) as psq:
                for sl in range(8):
                    @blk.sync
                    def _(s, sl=sl):
                        P.w(s, "dve", "pe")
                        P.inc(s.dma_start(out=ysl[:, :, :],
                                          in_=ypark_r[:, :, sl * 256:(sl + 1) * 256]), "ld", 16)
                    @blk.vector
                    def _(v, sl=sl):
                        P.w(v, "ld", "pe")
                        v.tensor_copy(ybf_c[:], ysl[:, :, :])
                    @blk.tensor
                    def _(t, sl=sl):
                        P.w(t, "dve")
                        for c in range(NCH):
                            i = t.matmul(psu[:, sl * 256:(sl + 1) * 256], ones_bf[:],
                                         ybf_c[:, c, :], start=(c == 0), stop=(c == NCH - 1))
                        P.inc(i, "pe")
                    @blk.vector
                    def _(v, sl=sl):
                        P.w(v, "pe")
                        v.tensor_mul(ybf_c[:], ysl[:, :, :], ysl[:, :, :])
                    @blk.tensor
                    def _(t, sl=sl):
                        P.w(t, "dve")
                        for c in range(NCH):
                            i = t.matmul(psq[:, sl * 256:(sl + 1) * 256], ones_bf[:],
                                         ybf_c[:, c, :], start=(c == 0), stop=(c == NCH - 1))
                        P.inc(i, "pe")
                @blk.scalar
                def _(sc):
                    P.w(sc, "pe")
                    sc.mul(rw1[:], psu[:], 1.0 / D)
                    sc.mul(rw2[:], psq[:], 1.0 / D)
            @blk.vector
            def _(v):
                P.w(v, "act")
                for sl in range(8):
                    v.tensor_mul(tmp_c[0:1, 0:256], rw1[:, sl * 256:(sl + 1) * 256],
                                 rw1[:, sl * 256:(sl + 1) * 256])
                    v.tensor_sub(rw2[:, sl * 256:(sl + 1) * 256],
                                 rw2[:, sl * 256:(sl + 1) * 256], tmp_c[0:1, 0:256])
            @blk.scalar
            def _(sc):
                P.w(sc, "dve")
                sc.activation(rw2[:], rw2[:], AF.Ln, bias=epsc[:])
                sc.activation(rw2[:], rw2[:], AF.Exp, scale=-0.5)
            @blk.gpsimd
            def _(g):
                P.w(g, "act", "gp")
                P.inc(g.dma_start(out=d_rowb.ap(), in_=rw1[:]), "gp", 16)
                g.wait_ge(P.sem["gp"], P.cnt["gp"])
                P.inc(g.dma_start(out=d_recb.ap()[0:1, :], in_=rw2[:]), "gp", 16)
            with nc.sbuf_tensor("mu2bc", [128, OWN], F32) as mu2bc, \
                 nc.sbuf_tensor("r2bc", [128, OWN], F32) as r2bc:
                @blk.gpsimd
                def _(g):
                    P.w(g, "gp")
                    P.inc(g.dma_start(out=mu2bc[:],
                                      in_=d_rowb.ap().partition_broadcast(128)[:, 0, :]),
                          "gp", 16)
                    P.inc(g.dma_start(out=r2bc[:],
                                      in_=bass.AP(d_recb, 0, [[0, 128], [1, OWN]])), "gp", 16)
                for sl in range(8):
                    @blk.sync
                    def _(s, sl=sl):
                        P.w(s, "dve", "pe")
                        P.inc(s.dma_start(out=ysl[:, :, :],
                                          in_=ypark_r[:, :, sl * 256:(sl + 1) * 256]), "ld", 16)
                    @blk.vector
                    def _(v, sl=sl):
                        P.w(v, "gp", "ld")
                        mub = bass.AP(mu2bc, sl * 256, [list(mu2bc.ap().ap[0]), [0, NCH], [1, 256]])
                        rbb = bass.AP(r2bc, sl * 256, [list(r2bc.ap().ap[0]), [0, NCH], [1, 256]])
                        v.tensor_sub(ysl[:, :, :], ysl[:, :, :], mub)
                        v.tensor_mul(z[:, :, sl * 256:(sl + 1) * 256], ysl[:, :, :], rbb)
            rctx = ExitStack()
            wres = {w: rctx.enter_context(nc.sbuf_tensor("w_" + w, [128, NCH, D], BF16))
                    for w in ("wf", "wi", "wc", "uf", "ui", "uc")}
            cst = rctx.enter_context(nc.sbuf_tensor("cst", [128, NCH, 256], F32))
            cbf = rctx.enter_context(nc.sbuf_tensor("cbf", [128, NCH, 256], BF16))
            gates = {g: rctx.enter_context(nc.sbuf_tensor("g_" + g, [128, NCH, 256], F32))
                     for g in ("f", "i", "c")}

            @blk.sync
            def _(s):
                P.w(s, "dve")
                for wi_, w8 in enumerate(("wf", "wi", "wc", "uf", "ui", "uc")):
                    P.inc(s.dma_start(out=wres[w8][:],
                                      in_=wffnbf_r[:, :, wi_ * 1024:(wi_ + 1) * 1024]),
                          "ld", 16)
            for c in range(NCH):
                @blk.vector
                def _(v, c=c):
                    P.w(v, "gp")
                    src = bass.AP(bv["incv"], c, [list(bv["incv"].ap().ap[0]), [0, 256]])
                    P.inc(v.tensor_copy(cst[:, c, :], src), "dve")
            @blk.vector
            def _(v):
                P.w(v, "dve", "pool")
                P.inc(v.tensor_copy(cbf[:], cst[:]), "dve")
            gnames = (("f", "wf", "uf", "bfv", AF.Sigmoid),
                      ("i", "wi", "ui", "biv", AF.Sigmoid),
                      ("c", "wc", "uc", "bcv", AF.Tanh))
            with nc.psum_tensor("psC_g0", [128, 256], F32) as pg0, \
                 nc.psum_tensor("psC_g1", [128, 256], F32) as pg1, \
                 nc.psum_tensor("psC_g2", [128, 256], F32) as pg2, \
                 nc.psum_tensor("psC_g3", [128, 256], F32) as pg3:
                pgs = (pg0, pg1, pg2, pg3)
                pguse = [0, 0, 0, 0]
                gi = 0
                for t_ in range(B):
                    for (gn, wn, un, bn, af) in gnames:
                        for mc in range(NCH):
                            bi = gi % 4
                            gi += 1
                            pg = pgs[bi]
                            @blk.tensor
                            def _(t, t_=t_, wn=wn, un=un, mc=mc, pg=pg, bi=bi):
                                P.wv(t, "act", pguse[bi])
                                if mc == 0 and gn == "f":
                                    P.w(t, "dve", "pool", "ld")
                                for ec in range(NCH):
                                    t.matmul(pg[:], wres[wn][:, ec, mc * 128:(mc + 1) * 128],
                                             z[:, ec, t_ * 256:(t_ + 1) * 256],
                                             start=(ec == 0), stop=False)
                                for ec in range(NCH):
                                    i = t.matmul(pg[:], wres[un][:, ec, mc * 128:(mc + 1) * 128],
                                                 cbf[:, ec, :],
                                                 start=False, stop=(ec == NCH - 1))
                                P.inc(i, "pe")
                            @blk.scalar
                            def _(sc, gn=gn, bn=bn, af=af, mc=mc, pg=pg, bi=bi):
                                P.w(sc, "pe")
                                sc.activation(gates[gn][:, mc, :], pg[:], af,
                                              bias=bv[bn][:, mc:mc + 1])
                                pguse[bi] = P.cnt["act"]
                    for mc in range(NCH):
                        dec = blk.vector if mc % 2 == 0 else blk.gpsimd
                        nm = "dve" if mc % 2 == 0 else "pool"
                        @dec
                        def _(e, mc=mc, nm=nm):
                            P.w(e, "act")
                            scratch = tmp_c[:, :] if nm == "dve" else ysl[:, 0, :]
                            e.tensor_mul(cst[:, mc, :], cst[:, mc, :], gates["f"][:, mc, :])
                            e.tensor_mul(scratch, gates["i"][:, mc, :], gates["c"][:, mc, :])
                            e.tensor_add(cst[:, mc, :], cst[:, mc, :], scratch)
                    @blk.vector
                    def _(v, t_=t_):
                        P.w(v, "dve", "pool")
                        v.tensor_copy(cbf[:], cst[:])
            wcuse = [0, 0]
            for mc in range(NCH):
                @blk.sync
                def _(s, mc=mc):
                    P.wv(s, "pe", wcuse[mc % 2])
                    if mc == 0:
                        P.w(s, "dve")
                    P.inc(s.dma_start(out=wt_c[:, mc % 2],
                                      in_=wffnbf_r[:, :, 7 * 1024 + mc * 128:
                                                   7 * 1024 + (mc + 1) * 128]), "ld", 16)
                with nc.psum_tensor(f"psC_uo_{mc}", [128, 256], F32) as pu:
                    @blk.tensor
                    def _(t, mc=mc, pu=pu):
                        P.w(t, "ld", "act", "dve", "pool")
                        for ec in range(NCH):
                            i = t.matmul(pu[:], wt_c[:, mc % 2, ec, :], cbf[:, ec, :],
                                         start=(ec == 0), stop=(ec == NCH - 1))
                        P.inc(i, "pe")
                    wcuse[mc % 2] = P.cnt["pe"]
                    @blk.vector
                    def _(v, mc=mc, pu=pu):
                        P.w(v, "pe")
                        v.tensor_copy(cuo[:, mc, :], pu[:])
            rctx.close()
            with ExitStack() as octx:
                wo_res = octx.enter_context(nc.sbuf_tensor("wo_res", [128, NCH, D], BF16))
                ystage = octx.enter_context(nc.sbuf_tensor("ystage", [128, NCH, 256], F32))
                yo_r = d_yo.rearrange("(c p) n -> p c n", p=128)
                @blk.sync
                def _(s):
                    P.w(s, "pe")
                    P.inc(s.dma_start(out=wo_res[:],
                                      in_=wffnbf_r[:, :, 6 * 1024:7 * 1024]), "ld", 16)
                for bt in range(B):
                    @blk.sync
                    def _(s, bt=bt):
                        P.w(s, "dve", "st")
                        P.inc(s.dma_start(out=ysl[:, :, :],
                                          in_=ypark_r[:, :, bt * 256:(bt + 1) * 256]),
                              "ld", 16)
                    for mc in range(NCH):
                        with nc.psum_tensor(f"psC_o_{bt}_{mc}", [128, 256], F32) as po:
                            @blk.tensor
                            def _(t, mc=mc, bt=bt, po=po):
                                P.w(t, "ld")
                                P.w(t, "dve")
                                for ec in range(NCH):
                                    i = t.matmul(po[:], wo_res[:, ec, mc * 128:(mc + 1) * 128],
                                                 z[:, ec, bt * 256:(bt + 1) * 256],
                                                 start=(ec == 0), stop=(ec == NCH - 1))
                                P.inc(i, "pe")
                            @blk.vector
                            def _(v, mc=mc, po=po):
                                P.w(v, "pe", "act")
                                P.inc(v.tensor_add(tmp_c[:], po[:], cuo[:, mc, :]), "dve")
                            @blk.scalar
                            def _(sc, mc=mc):
                                P.w(sc, "dve")
                                P.inc(sc.activation(tmp_c[:], tmp_c[:], AF.Tanh,
                                                    bias=bv["bov"][:, mc:mc + 1]), "act")
                            @blk.vector
                            def _(v, mc=mc):
                                P.w(v, "act", "ld")
                                P.inc(v.tensor_add(ystage[:, mc, :], tmp_c[:], ysl[:, mc, :]),
                                      "dve")
                    @blk.sync
                    def _(s, bt=bt):
                        P.w(s, "dve")
                        P.inc(s.dma_start(out=yo_r[:, :, bt * 256:(bt + 1) * 256],
                                          in_=ystage[:, :, :]), "st", 16)
        @blk.sync
        def _(s):
            P.wall(s)
    return nc


_NC_CACHE = {}


def kernel(L0_attn_query, L0_attn_key, L0_attn_value, L0_attn_in_proj_weight,
           L0_attn_out_proj_weight, L0_ffn_wf, L0_ffn_uf, L0_ffn_bf, L0_ffn_wi,
           L0_ffn_ui, L0_ffn_bi, L0_ffn_wc, L0_ffn_uc, L0_ffn_bc, L0_ffn_wo,
           L0_ffn_uo, L0_ffn_bo, L0_ffn_in_c):
    x = np.asarray(L0_attn_query, np.float32)
    xf = np.ascontiguousarray(x.transpose(2, 0, 1).reshape(D, B * T))
    wqkv = np.ascontiguousarray(np.asarray(L0_attn_in_proj_weight, np.float32).T)
    wout = np.ascontiguousarray(np.asarray(L0_attn_out_proj_weight, np.float32).T)
    wmats = {k: np.ascontiguousarray(np.asarray(v, np.float32).T) for k, v in
             dict(wf=L0_ffn_wf, wi=L0_ffn_wi, wc=L0_ffn_wc, uf=L0_ffn_uf,
                  ui=L0_ffn_ui, uc=L0_ffn_uc, wo=L0_ffn_wo, uo=L0_ffn_uo).items()}
    bvecs = {k: np.ascontiguousarray(np.asarray(v, np.float32).reshape(NCH, 128).T)
             for k, v in dict(bfv=L0_ffn_bf, biv=L0_ffn_bi, bcv=L0_ffn_bc,
                              bov=L0_ffn_bo, incv=L0_ffn_in_c).items()}
    jj = np.arange(128)
    trimask = np.where(jj[:, None] > jj[None, :], np.float32(NEG), np.float32(0.0))
    trimask = np.ascontiguousarray(trimask.astype(np.float32))

    in_maps = []
    for core in range(8):
        blks = (core, 8 + core)
        xo = np.empty((D, OWN), np.float32)
        for b in range(B):
            for s_, blkid in enumerate(blks):
                xo[:, b * 256 + s_ * 128: b * 256 + (s_ + 1) * 128] = \
                    x[b, blkid * 128:(blkid + 1) * 128, :].T
        deny = np.zeros((128, 32), np.float32)
        for s_, blkid in enumerate(blks):
            for k in range(16):
                if k >= blkid:
                    deny[:, s_ * 16 + k] = NEG
        m = dict(xf=xf, xo=np.ascontiguousarray(xo), wqkv=wqkv, wout=wout,
                 deny=deny, trimask=trimask, iden=np.eye(128, dtype=np.float32))
        m.update(wmats)
        m.update(bvecs)
        in_maps.append(m)

    dbg = bool(_NC_CACHE.get("debug"))
    key = ("nc", dbg)
    if key not in _NC_CACHE:
        _NC_CACHE[key] = build_nc(debug=dbg)
    nc = _NC_CACHE[key]
    res = run_bass_kernel_spmd(nc, in_maps, core_ids=list(range(8)))

    if dbg:
        _NC_CACHE["results"] = res.results
    out = np.empty((B, T, D), np.float32)
    for core in range(8):
        yo = res.results[core]["yo"]
        for b in range(B):
            for s_, blkid in enumerate((core, 8 + core)):
                out[b, blkid * 128:(blkid + 1) * 128, :] = \
                    yo[:, b * 256 + s_ * 128: b * 256 + (s_ + 1) * 128].T
    return out


# version 08
# speedup vs baseline: 1.1049x; 1.1049x over previous
import numpy as np
from contextlib import ExitStack
import concourse.bass as bass
import concourse.mybir as mybir
from concourse.bass_utils import run_bass_kernel_spmd

F32 = mybir.dt.float32
BF16 = mybir.dt.bfloat16
AF = mybir.ActivationFunctionType
ALU = mybir.AluOpType
AX = mybir.AxisListType

D, NCH, B, T, H = 1024, 8, 8, 2048, 16
OWN = 2048
NEG = -1.0e30
EPS = 1e-5


class P_:
    def __init__(self, nc, ctx):
        self.sem = {}
        self.cnt = {}
        for n in ("ld", "gp", "pe", "act", "dve", "pool", "st"):
            self.sem[n] = ctx.enter_context(nc.semaphore(n))
            self.cnt[n] = 0
        self.tagged = set()
        self._keep = []

    def inc(self, instr, name, amt=None):
        if id(instr) in self.tagged:
            return
        amt = amt if amt is not None else 1
        instr.then_inc(self.sem[name], amt)
        self.cnt[name] += amt

    def wv(self, eng, name, val):
        if val > 0:
            eng.wait_ge(self.sem[name], val)

    def w(self, eng, *names):
        for n in names:
            if self.cnt[n] > 0:
                eng.wait_ge(self.sem[n], self.cnt[n])

    def wall(self, eng):
        self.w(eng, "ld", "gp", "pe", "act", "dve", "pool", "st")


_PASSTHRU = ("wait_ge", "sem_inc", "nop", "dma_start", "dma_start_transpose",
             "indirect_dma_start", "trigger_dma", "collective_compute")


class SeqProxy:

    def __init__(self, eng, P, nm):
        self._eng, self._P, self._nm = eng, P, nm

    def __getattr__(self, name):
        f = getattr(self._eng, name)
        if name.startswith("_") or name in _PASSTHRU or not callable(f):
            return f

        def wrap(*args, **kw):
            P, nm = self._P, self._nm
            if P.cnt[nm] > 0:
                self._eng.wait_ge(P.sem[nm], P.cnt[nm])
            instr = f(*args, **kw)
            instr.then_inc(P.sem[nm], 1)
            P.cnt[nm] += 1
            P.tagged.add(id(instr))
            P._keep.append(instr)
            return instr
        return wrap


class BlkWrap:
    def __init__(self, blk, P):
        self._blk, self._P = blk, P

    @property
    def sync(self):
        return self._blk.sync

    @property
    def tensor(self):
        return self._blk.tensor

    @property
    def vector(self):
        return lambda fn: self._blk.vector(lambda e: fn(SeqProxy(e, self._P, "dve")))

    @property
    def scalar(self):
        return lambda fn: self._blk.scalar(lambda e: fn(SeqProxy(e, self._P, "act")))

    @property
    def gpsimd(self):
        return lambda fn: self._blk.gpsimd(lambda e: fn(SeqProxy(e, self._P, "pool")))


def build_nc(debug=False):
    nc = bass.Bass("TRN2", target_bir_lowering=False)

    d_xf = nc.dram_tensor("xf", [D, B * T], F32, kind="ExternalInput").ap()
    d_xo = nc.dram_tensor("xo", [D, OWN], F32, kind="ExternalInput").ap()
    d_wqkv = nc.dram_tensor("wqkv", [D, 3 * D], F32, kind="ExternalInput").ap()
    d_wout = nc.dram_tensor("wout", [D, D], F32, kind="ExternalInput").ap()
    d_w = {w: nc.dram_tensor(w, [D, D], F32, kind="ExternalInput").ap()
           for w in ("wf", "wi", "wc", "uf", "ui", "uc", "wo", "uo")}
    d_bv = {v: nc.dram_tensor(v, [128, NCH], F32, kind="ExternalInput").ap()
            for v in ("bfv", "biv", "bcv", "bov", "incv")}
    d_deny = nc.dram_tensor("deny", [128, 32], F32, kind="ExternalInput").ap()
    d_tri = nc.dram_tensor("trimask", [128, 128], F32, kind="ExternalInput").ap()
    d_iden = nc.dram_tensor("iden", [128, 128], F32, kind="ExternalInput").ap()
    d_yo = nc.dram_tensor("yo", [D, OWN], F32, kind="ExternalOutput").ap()

    dkind = "ExternalOutput" if debug else "Internal"
    d_wqkv_bf = nc.dram_tensor("wqkv_bf", [D, 3 * D], BF16)
    d_wout_bf = nc.dram_tensor("wout_bf", [D, D], BF16)
    d_wffn_bf = nc.dram_tensor("wffn_bf", [D, 8 * D], BF16)
    d_qpark = nc.dram_tensor("q_park", [D, OWN], BF16, kind=dkind)
    d_ypark = nc.dram_tensor("y_park", [D, OWN], F32, kind=dkind)
    if debug:
        d_txh = nc.dram_tensor("t_xhat", [128, NCH * OWN], BF16, kind="ExternalOutput").ap()
        d_tkb = nc.dram_tensor("t_kb", [128, NCH * T], BF16, kind="ExternalOutput").ap()
        d_ton = nc.dram_tensor("t_onn", [128, NCH * 256], BF16, kind="ExternalOutput").ap()
        d_tor = nc.dram_tensor("t_onraw", [128, NCH * 256], BF16, kind="ExternalOutput").ap()
        d_trc = nc.dram_tensor("t_rec", [2, 2048], F32, kind="ExternalOutput").ap()
        d_tfa = nc.dram_tensor("t_fac", [128, NCH * 256], F32, kind="ExternalOutput").ap()
        d_tyt = nc.dram_tensor("t_yt", [128, 256], BF16, kind="ExternalOutput").ap()
        d_txr = nc.dram_tensor("t_xr", [128, 256], F32, kind="ExternalOutput").ap()
        d_tpy = nc.dram_tensor("t_py", [128, 256], BF16, kind="ExternalOutput").ap()
        d_twt = nc.dram_tensor("t_wt", [128, 128], F32, kind="ExternalOutput").ap()
        d_tnb = nc.dram_tensor("t_nb", [128, 1], F32, kind="ExternalOutput").ap()
        d_tz = nc.dram_tensor("t_z", [128, NCH * OWN], BF16, kind="ExternalOutput").ap()
    d_rowb = nc.dram_tensor("row_bounce", [1, T], F32)
    d_recb = nc.dram_tensor("rec_bounce", [2, 2048], F32)

    xf_r = d_xf.rearrange("(c p) n -> p c n", p=128)
    xo_r = d_xo.rearrange("(c p) n -> p c n", p=128)
    wqkv_r = d_wqkv.rearrange("(c p) f -> p c f", p=128)
    wout_r = d_wout.rearrange("(c p) f -> p c f", p=128)
    w_r = {k: v.rearrange("(c p) f -> p c f", p=128) for k, v in d_w.items()}
    qpark_r = d_qpark.ap().rearrange("(c p) n -> p c n", p=128)
    wqkvbf_r = d_wqkv_bf.ap().rearrange("(c p) f -> p c f", p=128)
    woutbf_r = d_wout_bf.ap().rearrange("(c p) f -> p c f", p=128)
    wffnbf_r = d_wffn_bf.ap().rearrange("(c p) f -> p c f", p=128)
    FFN_ORDER = ("wf", "wi", "wc", "uf", "ui", "uc", "wo", "uo")

    with ExitStack() as ctx:
        sb = lambda n, sh, dt: ctx.enter_context(nc.sbuf_tensor(n, sh, dt))
        tri = sb("tri", [128, 128], F32)
        iden = sb("iden_s", [128, 128], F32)
        deny = sb("deny_s", [128, 32], F32)
        ones_bf = sb("ones_bf", [128, 1], BF16)
        ones_f1 = sb("ones_f1", [1, 128], F32)
        idy = sb("idy", [1, 1], F32)
        bv = {v: sb(v + "_s", [128, NCH], F32) for v in d_bv}
        ro_c = sb("ro_c", [128, 16], F32)
        lo_c = sb("lo_c", [128, 16], F32)
        co_c = sb("co_c", [128, 16], F32)
        qmax2 = sb("qmax2", [1, 1], F32)
        epsc = sb("epsc", [1, 1], F32)
        rw1 = sb("rw1", [1, T], F32)
        rw2 = sb("rw2", [1, T], F32)

        P = P_(nc, ctx)
        blk = BlkWrap(ctx.enter_context(nc.Block()), P)

        @blk.sync
        def _(s):
            P.inc(s.dma_start(out=tri[:], in_=d_tri), "ld", 16)
            P.inc(s.dma_start(out=iden[:], in_=d_iden), "ld", 16)
            P.inc(s.dma_start(out=deny[:], in_=d_deny), "ld", 16)
            for v in bv:
                P.inc(s.dma_start(out=bv[v][:], in_=d_bv[v]), "ld", 16)

        @blk.vector
        def _(v):
            v.memset(ones_bf[:], 1.0)
            v.memset(ones_f1[:], 1.0)
            v.memset(idy[:], 1.0)
            v.memset(epsc[:], EPS)
            P.inc(v.memset(qmax2[:], 0.0), "dve")

        with ExitStack() as abctx:
            xa_bf = abctx.enter_context(nc.sbuf_tensor("xa_bf", [128, NCH, OWN], BF16))
            mu_own_bf = abctx.enter_context(nc.sbuf_tensor("mu_own_bf", [1, OWN], BF16))

            with ExitStack() as actx:
                asb = lambda n, sh, dt: actx.enter_context(nc.sbuf_tensor(n, sh, dt))
                x2sl = asb("x2sl_a", [128, NCH, 512], BF16)
                wq = asb("wq_a", [128, NCH, 1024], BF16)
                nwq = asb("nwq_a", [1, 1024], BF16)
                qev = asb("qev_a", [128, 512], BF16)
                q2 = asb("q2_a", [128, 512], BF16)
                rbc = asb("rbc_a", [128, 512], F32)
                qscr = asb("qscr_a", [1, 4], F32)

                wstage = asb("wstage_a", [128, NCH, 1024], BF16)

                @blk.gpsimd
                def _(g):
                    P.inc(g.dma_start(out=xa_bf[:], in_=xo_r), "gp", 16)
                    g.wait_ge(P.sem["gp"], P.cnt["gp"])
                    P.inc(g.dma_start(out=wq[:], in_=wqkv_r[:, :, 0:1024]), "gp", 16)
                for seg in range(4):
                    srcv = (wqkv_r[:, :, seg * 1024:(seg + 1) * 1024] if seg < 3
                            else wout_r[:, :, 0:1024])
                    dstv = (d_wqkv_bf.ap().rearrange("(c p) f -> p c f", p=128)[
                                :, :, seg * 1024:(seg + 1) * 1024] if seg < 3
                            else woutbf_r[:, :, 0:1024])
                    @blk.gpsimd
                    def _(g, srcv=srcv, seg=seg):
                        P.w(g, "st")
                        P.inc(g.dma_start(out=wstage[:], in_=srcv), "gp", 16)
                    @blk.sync
                    def _(s, dstv=dstv):
                        P.w(s, "gp")
                        P.inc(s.dma_start(out=dstv, in_=wstage[:]), "st", 16)

                with nc.psum_tensor("psA_su", [1, 2048], F32) as psu, \
                     nc.psum_tensor("psA_sq", [1, 2048], F32) as psq:
                    for sl in range(4):
                        @blk.vector
                        def _(v, sl=sl):
                            P.w(v, "gp", "pe")
                            v.tensor_mul(x2sl[:], xa_bf[:, :, sl * 512:(sl + 1) * 512],
                                         xa_bf[:, :, sl * 512:(sl + 1) * 512])
                        @blk.tensor
                        def _(t, sl=sl):
                            P.w(t, "dve", "gp")
                            for c in range(NCH):
                                t.matmul(psu[:, sl * 512:(sl + 1) * 512], ones_bf[:],
                                         xa_bf[:, c, sl * 512:(sl + 1) * 512],
                                         start=(c == 0), stop=(c == NCH - 1))
                                i = t.matmul(psq[:, sl * 512:(sl + 1) * 512], ones_bf[:],
                                             x2sl[:, c, :], start=(c == 0), stop=(c == NCH - 1))
                            P.inc(i, "pe")
                    @blk.scalar
                    def _(sc):
                        P.w(sc, "pe")
                        sc.mul(rw1[:], psu[:], 1.0 / D)
                        sc.mul(rw2[:], psq[:], 1.0 / D)
                        sc.copy(mu_own_bf[:], rw1[:])
                @blk.vector
                def _(v):
                    P.w(v, "act")
                    for sl in range(4):
                        v.tensor_mul(rbc[0:1, :], rw1[:, sl * 512:(sl + 1) * 512],
                                     rw1[:, sl * 512:(sl + 1) * 512])
                        v.tensor_sub(rw2[:, sl * 512:(sl + 1) * 512],
                                     rw2[:, sl * 512:(sl + 1) * 512], rbc[0:1, 0:512])
                @blk.scalar
                def _(sc):
                    P.w(sc, "dve")
                    sc.activation(rw2[:], rw2[:], AF.Ln, bias=epsc[:])
                    sc.activation(rw2[:], rw2[:], AF.Exp, scale=-0.5)
                with nc.psum_tensor("psA_tr", [128, 16], F32) as ptr:
                    @blk.tensor
                    def _(t):
                        P.w(t, "act")
                        for jt in range(16):
                            i = t.transpose(ptr[:, jt:jt + 1], rw2[:, jt * 128:(jt + 1) * 128],
                                            idy[:])
                        P.inc(i, "pe")
                    @blk.vector
                    def _(v):
                        P.w(v, "pe")
                        v.tensor_copy(ro_c[:], ptr[:])
                        v.reciprocal(co_c[:], ptr[:])
                @blk.scalar
                def _(sc):
                    P.w(sc, "dve")
                    sc.activation(lo_c[:], ro_c[:], AF.Ln)
                    sc.mul(rw1[:], rw2[:], 0.125)
                @blk.gpsimd
                def _(g):
                    P.w(g, "act")
                    P.inc(g.dma_start(out=d_rowb.ap(), in_=rw1[:]), "gp", 16)
                with nc.psum_tensor("psA_ws", [1, 128], F32) as pws:
                    for fc in range(8):
                        @blk.tensor
                        def _(t, fc=fc):
                            P.w(t, "gp")
                            if fc:
                                P.w(t, "act")
                            for ec in range(NCH):
                                i = t.matmul(pws[:], ones_bf[:],
                                             wq[:, ec, fc * 128:(fc + 1) * 128],
                                             start=(ec == 0), stop=(ec == NCH - 1))
                            P.inc(i, "pe")
                        @blk.scalar
                        def _(sc, fc=fc):
                            P.w(sc, "pe")
                            sc.mul(nwq[:, fc * 128:(fc + 1) * 128], pws[:], -1.0)
                with nc.psum_tensor("psA_q", [128, 512], F32) as pq, \
                     nc.psum_tensor("psA_n0", [1, 512], F32) as pn0, \
                     nc.psum_tensor("psA_n1", [1, 512], F32) as pn1:
                    for sl in range(4):
                        @blk.gpsimd
                        def _(g, sl=sl):
                            P.w(g, "gp", "dve")
                            P.inc(g.dma_start(
                                out=rbc[:],
                                in_=d_rowb.ap().partition_broadcast(128)[:, 0, sl * 512:(sl + 1) * 512]),
                                "gp", 16)
                        for fc in range(8):
                            @blk.tensor
                            def _(t, fc=fc, sl=sl):
                                P.w(t, "gp", "act", "dve")
                                for c in range(NCH):
                                    t.matmul(pq[:], wq[:, c, fc * 128:(fc + 1) * 128],
                                             xa_bf[:, c, sl * 512:(sl + 1) * 512],
                                             start=(c == 0), stop=False)
                                i = t.matmul(pq[:], nwq[:, fc * 128:(fc + 1) * 128],
                                             mu_own_bf[:, sl * 512:(sl + 1) * 512],
                                             start=False, stop=True)
                                P.inc(i, "pe")
                            @blk.vector
                            def _(v, fc=fc, sl=sl):
                                P.w(v, "pe", "st")
                                v.tensor_mul(qev[:], pq[:], rbc[:])
                                v.tensor_mul(q2[:], qev[:], qev[:])
                            @blk.sync
                            def _(s, fc=fc, sl=sl):
                                P.w(s, "dve")
                                P.inc(s.dma_start(
                                    out=d_qpark.ap()[fc * 128:(fc + 1) * 128,
                                                     sl * 512:(sl + 1) * 512],
                                    in_=qev[:]), "st", 16)
                            @blk.tensor
                            def _(t, fc=fc):
                                P.w(t, "dve")
                                t.matmul(pn0[:], ones_bf[0:64, :], q2[0:64, :],
                                         start=True, stop=True)
                                P.inc(t.matmul(pn1[:], ones_bf[64:128, :], q2[64:128, :],
                                               start=True, stop=True), "pe")
                            @blk.vector
                            def _(v):
                                P.w(v, "pe", "gp")
                                v.tensor_reduce(qscr[:, 0:1], pn0[:], axis=AX.X, op=ALU.max)
                                v.tensor_reduce(qscr[:, 1:2], pn1[:], axis=AX.X, op=ALU.max)
                                v.tensor_max(qscr[:, 0:1], qscr[:, 0:1], qscr[:, 1:2])
                                v.tensor_max(qmax2[:], qmax2[:], qscr[:, 0:1])
            for dec in (blk.sync, blk.gpsimd, blk.tensor, blk.scalar, blk.vector):
                @dec
                def _(e):
                    P.wall(e)
            if debug:
                @blk.sync
                def _(s):
                    P.inc(s.dma_start(out=d_txh, in_=xa_bf[:].rearrange("p c n -> p (c n)")),
                          "st", 16)
                    s.wait_ge(P.sem["st"], P.cnt["st"])

            with ExitStack() as bctx:
                bsb = lambda n, sh, dt: bctx.enter_context(nc.sbuf_tensor(n, sh, dt))
                xb_bf = bsb("xb_bf", [128, NCH, T], BF16)
                kb = bsb("kb", [128, NCH, T], BF16)
                vb = bsb("vb", [128, 16, 1040], BF16)
                kob = bsb("kob", [128, NCH, 256], BF16)
                vob = bsb("vob", [128, 2, 1040], BF16)
                nwv = bsb("nwv", [1, 1024], BF16)
                nwk = bsb("nwk", [1, 128], BF16)
                mu_b_bf = bsb("mu_b_bf", [1, T], BF16)
                wt = bsb("wt", [128, 2, NCH, 128], BF16)
                qb = bsb("qb", [128, NCH, 256], BF16)
                scr8 = bsb("scr8", [128, NCH, 256], F32)
                yt8 = bsb("yt8", [128, 2, 256], F32)
                r_cb = bsb("r_cb", [128, 16], F32)
                l_cb = bsb("l_cb", [128, 16], F32)
                c_cb = bsb("c_cb", [128, 16], F32)
                btab = bsb("btab", [128, 32], F32)
                btabd = bsb("btabd", [128, 2], F32)
                negb = bsb("negb", [128, 1], F32)
                bscr = bsb("bscr", [1, 8], F32)


                wstage_b = bsb("wstage_b", [128, NCH, 512], BF16)
                wtuse = [0, 0]
                for seg in range(16):
                    w8 = FFN_ORDER[seg // 2]
                    half = seg % 2
                    @blk.gpsimd
                    def _(g, w8=w8, half=half):
                        P.w(g, "st")
                        P.inc(g.dma_start(out=wstage_b[:],
                                          in_=w_r[w8][:, :, half * 512:(half + 1) * 512]),
                              "gp", 16)
                    @blk.sync
                    def _(s, seg=seg, w8=w8, half=half):
                        P.w(s, "gp")
                        P.inc(s.dma_start(
                            out=wffnbf_r[:, :, seg * 512:(seg + 1) * 512],
                            in_=wstage_b[:]), "st", 16)
                for b in range(B):
                    bscope = ExitStack()
                    x2b = bscope.enter_context(nc.sbuf_tensor(f"x2b_{b}", [128, NCH, 128], BF16))
                    wvres = bscope.enter_context(
                        nc.sbuf_tensor(f"wvres_{b}", [128, NCH, 512], BF16))
                    @blk.gpsimd
                    def _(g, b=b):
                        P.w(g, "pe", "dve")
                        P.inc(g.dma_start(out=xb_bf[:],
                                          in_=xf_r[:, :, b * T:(b + 1) * T]), "gp", 16)
                    with nc.psum_tensor(f"psB_su{b}", [1, 2048], F32) as psu, \
                         nc.psum_tensor(f"psB_sq{b}", [1, 2048], F32) as psq:
                        for sl in range(16):
                            @blk.vector
                            def _(v, sl=sl):
                                P.w(v, "gp", "pe")
                                v.tensor_mul(x2b[:], xb_bf[:, :, sl * 128:(sl + 1) * 128],
                                             xb_bf[:, :, sl * 128:(sl + 1) * 128])
                            @blk.tensor
                            def _(t, sl=sl):
                                P.w(t, "dve", "gp")
                                if sl == 0:
                                    P.w(t, "act")
                                for c in range(NCH):
                                    t.matmul(psu[:, sl * 128:(sl + 1) * 128], ones_bf[:],
                                             xb_bf[:, c, sl * 128:(sl + 1) * 128],
                                             start=(c == 0), stop=(c == NCH - 1))
                                    i = t.matmul(psq[:, sl * 128:(sl + 1) * 128], ones_bf[:],
                                                 x2b[:, c, :], start=(c == 0), stop=(c == NCH - 1))
                                P.inc(i, "pe")
                        @blk.scalar
                        def _(sc):
                            P.w(sc, "pe")
                            sc.mul(rw1[:], psu[:], 1.0 / D)
                            sc.mul(rw2[:], psq[:], 1.0 / D)
                            sc.copy(mu_b_bf[:], rw1[:])
                    @blk.vector
                    def _(v):
                        P.w(v, "act")

                        s8f = scr8[:].rearrange("p c n -> p (c n)")
                        for sl in range(4):
                            v.tensor_mul(s8f[0:1, 0:512],
                                         rw1[:, sl * 512:(sl + 1) * 512],
                                         rw1[:, sl * 512:(sl + 1) * 512])
                            v.tensor_sub(rw2[:, sl * 512:(sl + 1) * 512],
                                         rw2[:, sl * 512:(sl + 1) * 512], s8f[0:1, 0:512])
                    @blk.scalar
                    def _(sc):
                        P.w(sc, "dve")
                        sc.activation(rw1[:], rw2[:], AF.Ln, bias=epsc[:])
                        sc.activation(rw2[:], rw1[:], AF.Exp, scale=-0.5)
                    with nc.psum_tensor(f"psB_tr{b}", [128, 16], F32) as ptrb:
                        @blk.tensor
                        def _(t):
                            P.w(t, "act")
                            for jt in range(16):
                                i = t.transpose(ptrb[:, jt:jt + 1],
                                                rw2[:, jt * 128:(jt + 1) * 128], idy[:])
                            P.inc(i, "pe")
                        @blk.vector
                        def _(v):
                            P.w(v, "pe")
                            v.tensor_copy(r_cb[:], ptrb[:])
                            v.reciprocal(c_cb[:], ptrb[:])
                    @blk.scalar
                    def _(sc):
                        P.w(sc, "dve")
                        sc.activation(l_cb[:], r_cb[:], AF.Ln)
                    @blk.vector
                    def _(v):
                        P.w(v, "act")
                        v.tensor_reduce(bscr[:, 1:2], rw2[:], axis=AX.X, op=ALU.max)
                        v.tensor_reduce(bscr[:, 2:3], rw1[:], axis=AX.X, op=ALU.min)
                        v.tensor_scalar_mul(bscr[:, 2:3], bscr[:, 2:3], -0.5)
                    for fi in range(8):
                        @blk.sync
                        def _(s, fi=fi):
                            P.wv(s, "pe", wtuse[fi % 2])
                            P.inc(s.dma_start(out=wt[:, fi % 2],
                                              in_=wqkvbf_r[:, :, 1024 + fi * 128:1024 + (fi + 1) * 128]),
                                  "ld", 16)
                        with nc.psum_tensor(f"psB_wk{b}_{fi}", [1, 128], F32) as pws2:
                            @blk.tensor
                            def _(t, fi=fi, pws2=pws2):
                                P.w(t, "ld", "act")
                                for ec in range(NCH):
                                    i = t.matmul(pws2[:], ones_bf[:], wt[:, fi % 2, ec, :],
                                                 start=(ec == 0), stop=(ec == NCH - 1))
                                P.inc(i, "pe")
                            @blk.scalar
                            def _(sc, pws2=pws2):
                                P.w(sc, "pe")
                                sc.mul(nwk[:], pws2[:], -1.0)
                        with nc.psum_tensor(f"psB_k{b}_{fi}", [128, 512], F32) as pk:
                            for tt in range(4):
                                @blk.tensor
                                def _(t, fi=fi, tt=tt, pk=pk, b=b):
                                    P.w(t, "act", "dve")
                                    for c in range(NCH):
                                        t.matmul(pk[:], wt[:, fi % 2, c, :],
                                                 xb_bf[:, c, tt * 512:(tt + 1) * 512],
                                                 start=(c == 0), stop=False)
                                    i = t.matmul(pk[:], nwk[:],
                                                 mu_b_bf[:, tt * 512:(tt + 1) * 512],
                                                 start=False, stop=True)
                                    P.inc(i, "pe")
                                @blk.vector
                                def _(v, fi=fi, tt=tt, pk=pk):
                                    P.w(v, "pe")
                                    v.tensor_copy(kb[:, fi, tt * 512:(tt + 1) * 512], pk[:])
                            with nc.psum_tensor(f"psB_ko{b}_{fi}", [128, 256], F32) as pko:
                                @blk.tensor
                                def _(t, fi=fi, b=b, pko=pko):
                                    P.w(t, "dve")
                                    for c in range(NCH):
                                        t.matmul(pko[:], wt[:, fi % 2, c, :],
                                                 xa_bf[:, c, b * 256:(b + 1) * 256],
                                                 start=(c == 0), stop=False)
                                    i = t.matmul(pko[:], nwk[:],
                                                 mu_own_bf[:, b * 256:(b + 1) * 256],
                                                 start=False, stop=True)
                                    P.inc(i, "pe")
                                @blk.vector
                                def _(v, fi=fi, pko=pko):
                                    P.w(v, "pe")
                                    v.tensor_copy(kob[:, fi, :], pko[:])
                            wtuse[fi % 2] = P.cnt["pe"]
                    with nc.psum_tensor(f"psB_v0_{b}", [128, 512], F32) as pv0, \
                         nc.psum_tensor(f"psB_v1_{b}", [128, 512], F32) as pv1, \
                         nc.psum_tensor(f"psB_wsv{b}", [1, 512], F32) as pwsv:
                        pvs = (pv0, pv1)
                        for ft in range(2):
                            @blk.sync
                            def _(s, ft=ft):
                                P.w(s, "pe")
                                P.inc(s.dma_start(out=wvres[:],
                                                  in_=wqkvbf_r[:, :, 2048 + ft * 512:
                                                               2048 + (ft + 1) * 512]), "ld", 16)
                            @blk.tensor
                            def _(t, ft=ft):
                                P.w(t, "ld", "act")
                                for ec in range(NCH):
                                    i = t.matmul(pwsv[:], ones_bf[:], wvres[:, ec, :],
                                                 start=(ec == 0), stop=(ec == NCH - 1))
                                P.inc(i, "pe")
                            @blk.scalar
                            def _(sc, ft=ft):
                                P.w(sc, "pe")
                                sc.mul(nwv[:, 0:512], pwsv[:], -1.0)
                            for jt in range(16):
                                pv = pvs[jt % 2]
                                @blk.tensor
                                def _(t, jt=jt, ft=ft, pv=pv, b=b):
                                    P.w(t, "dve", "act")
                                    for ec in range(NCH):
                                        t.matmul(pv[:],
                                                 xb_bf[:, ec, jt * 128:(jt + 1) * 128],
                                                 wvres[:, ec, :],
                                                 start=(ec == 0), stop=False)
                                    i = t.matmul(pv[:],
                                                 mu_b_bf[:, jt * 128:(jt + 1) * 128],
                                                 nwv[:, 0:512],
                                                 start=False, stop=True)
                                    P.inc(i, "pe")
                                @blk.vector
                                def _(v, jt=jt, ft=ft, pv=pv):
                                    P.w(v, "pe")
                                    dst = vb[:].rearrange("p jt (h e) -> p jt h e", h=16)[
                                        :, jt, ft * 8:(ft + 1) * 8, 0:64]
                                    v.tensor_copy(dst, pv[:].rearrange("p (h e) -> p h e", h=8))
                            for sj in range(2):
                                pv = pvs[sj]
                                @blk.tensor
                                def _(t, sj=sj, ft=ft, b=b, pv=pv):
                                    P.w(t, "dve")
                                    for ec in range(NCH):
                                        t.matmul(pv[:],
                                                 xa_bf[:, ec, b * 256 + sj * 128:
                                                        b * 256 + (sj + 1) * 128],
                                                 wvres[:, ec, :],
                                                 start=(ec == 0), stop=False)
                                    i = t.matmul(pv[:],
                                                 mu_own_bf[:, b * 256 + sj * 128:
                                                           b * 256 + (sj + 1) * 128],
                                                 nwv[:, 0:512],
                                                 start=False, stop=True)
                                    P.inc(i, "pe")
                                @blk.vector
                                def _(v, sj=sj, ft=ft, pv=pv):
                                    P.w(v, "pe")
                                    dst = vob[:].rearrange("p jt (h e) -> p jt h e", h=16)[
                                        :, sj, ft * 8:(ft + 1) * 8, 0:64]
                                    v.tensor_copy(dst, pv[:].rearrange("p (h e) -> p h e", h=8))
                        for jt in range(16):
                            @blk.vector
                            def _(v, jt=jt):
                                dst = vb[:].rearrange("p jt (h e) -> p jt h e", h=16)[:, jt, :, 64:65]
                                src2 = bass.AP(c_cb, jt, [list(c_cb.ap().ap[0]), [0, 16], [1, 1]])
                                v.tensor_copy(dst, src2)
                        for sj in range(2):
                            @blk.vector
                            def _(v, sj=sj, b=b):
                                dst = vob[:].rearrange("p jt (h e) -> p jt h e", h=16)[:, sj, :, 64:65]
                                src2 = bass.AP(co_c, b * 2 + sj, [list(co_c.ap().ap[0]), [0, 16], [1, 1]])
                                v.tensor_copy(dst, src2)

                    @blk.vector
                    def _(v):
                        P.w(v, "act", "pe")
                        P.inc(v.tensor_reduce(
                            negb[:].rearrange("p (a c) -> p a c", a=1), kb[:],
                            axis=AX.XY, op=ALU.max, apply_absolute_value=True), "dve")
                    with nc.psum_tensor(f"psB_km{b}", [1, 128], F32) as pkm, \
                         nc.psum_tensor(f"psB_nb{b}", [128, 1], F32) as pnb:
                        @blk.tensor
                        def _(t, pkm=pkm):
                            P.w(t, "dve")
                            P.inc(t.transpose(pkm[:], negb[:], iden[:]), "pe")
                        @blk.vector
                        def _(v, pkm=pkm):
                            P.w(v, "pe")
                            v.tensor_reduce(bscr[:, 0:1], pkm[:], axis=AX.X, op=ALU.max)
                            P.inc(v.tensor_mul(bscr[:, 0:1], bscr[:, 0:1], bscr[:, 1:2]), "dve")
                        @blk.scalar
                        def _(sc):
                            P.w(sc, "dve")
                            sc.activation(bscr[:, 3:4], qmax2[:], AF.Ln)
                            P.inc(sc.activation(bscr[:, 3:4], bscr[:, 3:4], AF.Exp, scale=0.5),
                                  "act")
                        @blk.vector
                        def _(v):
                            P.w(v, "act")
                            v.tensor_mul(bscr[:, 0:1], bscr[:, 0:1], bscr[:, 3:4])
                            v.tensor_scalar_mul(bscr[:, 0:1], bscr[:, 0:1], 8.0)
                            v.tensor_add(bscr[:, 0:1], bscr[:, 0:1], bscr[:, 2:3])
                            P.inc(v.tensor_scalar_mul(bscr[:, 0:1], bscr[:, 0:1], -1.0), "dve")
                        @blk.tensor
                        def _(t, pnb=pnb):
                            P.w(t, "dve")
                            P.inc(t.matmul(pnb[:], ones_f1[:], bscr[:, 0:1],
                                           start=True, stop=True), "pe")
                        @blk.vector
                        def _(v, pnb=pnb):
                            P.w(v, "pe")
                            P.inc(v.tensor_copy(negb[:], pnb[:]), "dve")
                    @blk.vector
                    def _(v, b=b):
                        P.w(v, "dve")
                        lview = bass.AP(l_cb, 0, [list(l_cb.ap().ap[0]), [0, 2], [1, 16]])
                        v.tensor_add(btab[:].rearrange("p (s k) -> p s k", s=2), lview,
                                     deny[:].rearrange("p (s k) -> p s k", s=2))
                        v.tensor_scalar_add(btab[:], btab[:], negb[:])
                        v.tensor_copy(btabd[:], lo_c[:, 2 * b:2 * b + 2])
                        P.inc(v.tensor_scalar_add(btabd[:], btabd[:], negb[:]), "dve")
                    @blk.sync
                    def _(s, b=b):
                        P.w(s, "st", "pe")
                        P.inc(s.dma_start(out=qb[:],
                                          in_=qpark_r[:, :, b * 256:(b + 1) * 256]), "ld", 16)
                    @blk.tensor
                    def _(t):
                        P.w(t, "ld", "dve", "act", "pool")
                    @blk.scalar
                    def _(sc):
                        P.w(sc, "dve", "pool")
                    bscope.close()

                    with ExitStack() as sctx:
                        ssb = lambda n, sh, dt: sctx.enter_context(nc.sbuf_tensor(n, sh, dt))
                        psb = [[ssb(f"psb{se}{h}_{b}", [128, 512], BF16) for h in range(2)]
                               for se in range(2)]
                        onraw = ssb(f"onraw_{b}", [128, NCH, 256], F32)
                        onn = ssb(f"onn_{b}", [128, NCH, 256], BF16)
                        rec0, rec1 = rw1, rw2
                        recs = (rec0, rec1)
                        with nc.psum_tensor(f"psB_sc00_{b}", [128, 512], F32) as sc00, \
                             nc.psum_tensor(f"psB_sc01_{b}", [128, 512], F32) as sc01, \
                             nc.psum_tensor(f"psB_sc10_{b}", [128, 512], F32) as sc10, \
                             nc.psum_tensor(f"psB_sc11_{b}", [128, 512], F32) as sc11, \
                             nc.psum_tensor(f"psB_av00_{b}", [128, 512], F32) as av00, \
                             nc.psum_tensor(f"psB_av01_{b}", [128, 512], F32) as av01, \
                             nc.psum_tensor(f"psB_av10_{b}", [128, 512], F32) as av10, \
                             nc.psum_tensor(f"psB_av11_{b}", [128, 512], F32) as av11:
                            sc = ((sc00, sc01), (sc10, sc11))
                            av = ((av00, av01), (av10, av11))
                            expcnt = [0, 0]
                            exp_after = {}
                            avevcnt = [P.cnt["dve"], P.cnt["dve"]]
                            tricnt = [P.cnt["dve"]]
                            slot = 0
                            for hpg in range(2):
                                hps = [4 * hpg + i_ for i_ in range(4)]
                                for s_ in range(2):
                                    g2 = slot % 2
                                    units = [(True, 0)] + [(False, k) for k in range(8 * (s_ + 1))]
                                    nu = len(units)
                                    ui = 0
                                    while ui < nu:
                                        pu = [(ui + j, units[ui + j][0], units[ui + j][1],
                                               (ui + j) % 2) for j in range(min(2, nu - ui))]
                                        pecnt = {}
                                        dcnt = {}
                                        for (uj, isdiag, k, se) in pu:
                                            @blk.tensor
                                            def _(t, s_=s_, isdiag=isdiag, k=k, uj=uj, se=se,
                                                  hps=hps):
                                                P.wv(t, "act", expcnt[se])
                                                if uj == 0:
                                                    P.wv(t, "dve", tricnt[0])
                                                for hi, hp in enumerate(hps):
                                                    lt = (kob[:, hp, s_ * 128:(s_ + 1) * 128]
                                                          if isdiag
                                                          else kb[:, hp, k * 128:(k + 1) * 128])
                                                    t.matmul(sc[se][0][:, hi * 128:(hi + 1) * 128],
                                                             lt[0:64, :],
                                                             qb[0:64, hp, s_ * 128:(s_ + 1) * 128],
                                                             start=True, stop=True,
                                                             tile_position=(0, 0))
                                                    i = t.matmul(
                                                        sc[se][1][:, hi * 128:(hi + 1) * 128],
                                                        lt[64:128, :],
                                                        qb[64:128, hp, s_ * 128:(s_ + 1) * 128],
                                                        start=True, stop=True,
                                                        tile_position=(64, 0))
                                                P.inc(i, "pe")
                                            pecnt[uj] = P.cnt["pe"]
                                            if isdiag:
                                                @blk.vector
                                                def _(v, se=se):
                                                    P.w(v, "pe")
                                                    trib = bass.AP(tri, 0,
                                                                   [list(tri.ap().ap[0]),
                                                                    [0, 4], [1, 128]])
                                                    for h01 in range(2):
                                                        v.tensor_add(
                                                            sc[se][h01][:].rearrange(
                                                                "p (hp i) -> p hp i", hp=4),
                                                            sc[se][h01][:].rearrange(
                                                                "p (hp i) -> p hp i", hp=4),
                                                            trib)
                                                dcnt[uj] = P.cnt["dve"]
                                                tricnt[0] = P.cnt["dve"]
                                        @blk.scalar
                                        def _(scl_, pu=pu, s_=s_, b=b, pecnt=pecnt, dcnt=dcnt):
                                            for (uj, isdiag, k, se) in pu:
                                                P.wv(scl_, "pe", pecnt[uj])
                                                if isdiag:
                                                    P.wv(scl_, "dve", dcnt[uj])
                                                    scalev = ro_c[:, 2 * b + s_: 2 * b + s_ + 1]
                                                    bia = btabd[:, s_:s_ + 1]
                                                else:
                                                    scalev = r_cb[:, k:k + 1]
                                                    bia = btab[:, s_ * 16 + k: s_ * 16 + k + 1]
                                                for h01 in range(2):
                                                    scl_.activation(psb[se][h01][:],
                                                                    sc[se][h01][:],
                                                                    AF.Exp, bias=bia,
                                                                    scale=scalev)
                                                expcnt[se] = P.cnt["act"]
                                                exp_after[uj] = P.cnt["act"]
                                        for (uj, isdiag, k, se) in pu:
                                            @blk.tensor
                                            def _(t, s_=s_, isdiag=isdiag, k=k, uj=uj, se=se,
                                                  hps=hps, g2=g2, ecnt=exp_after[uj],
                                                  last_u=(uj == nu - 1)):
                                                P.wv(t, "act", ecnt)
                                                if uj == 0:
                                                    P.wv(t, "dve", avevcnt[g2])
                                                for h01 in range(2):
                                                    for hi, hp in enumerate(hps):
                                                        h = 2 * hp + h01
                                                        vt = (vob[:, s_, h * 65:(h + 1) * 65]
                                                              if isdiag
                                                              else vb[:, k, h * 65:(h + 1) * 65])
                                                        i = t.matmul(
                                                            av[g2][h01][0:65,
                                                                        hi * 128:(hi + 1) * 128],
                                                            vt,
                                                            psb[se][h01][:,
                                                                         hi * 128:(hi + 1) * 128],
                                                            start=(uj == 0 and hi == 0),
                                                            stop=last_u)
                                                P.inc(i, "pe")
                                        ui += len(pu)
                                    @blk.vector
                                    def _(v, s_=s_, hpg=hpg, g2=g2):
                                        P.w(v, "pe")
                                        for h01 in range(2):
                                            dst = onraw[h01 * 64:(h01 + 1) * 64,
                                                        4 * hpg:4 * hpg + 4,
                                                        s_ * 128:(s_ + 1) * 128]
                                            v.tensor_copy(dst, av[g2][h01][0:64, :].rearrange(
                                                "p (c n) -> p c n", c=4))
                                            rdst = bass.AP(recs[h01],
                                                           4 * hpg * 256 + s_ * 128,
                                                           [list(recs[h01].ap().ap[0]),
                                                            [256, 4], [1, 128]])
                                            v.reciprocal(rdst, av[g2][h01][64:65, :].rearrange(
                                                "p (c n) -> p c n", c=4))
                                    avevcnt[g2] = P.cnt["dve"]
                                    slot += 1
                        @blk.gpsimd
                        def _(g):
                            P.w(g, "dve", "gp")
                            P.inc(g.dma_start(out=d_recb.ap()[0:1, :], in_=rec0[:]), "gp", 16)
                            P.inc(g.dma_start(out=d_recb.ap()[1:2, :], in_=rec1[:]), "gp", 16)
                            g.wait_ge(P.sem["gp"], P.cnt["gp"])
                            for a in range(2):
                                rsrc = bass.AP(d_recb, a * 2048, [[0, 64], [1, 2048]])
                                rdst = scr8[a * 64:(a + 1) * 64, :, :].rearrange(
                                    "p c n -> p (c n)")
                                P.inc(g.dma_start(out=rdst, in_=rsrc), "gp", 16)
                        @blk.vector
                        def _(v):
                            P.w(v, "gp", "dve")
                            P.inc(v.tensor_mul(onn[:], onraw[:], scr8[:]), "dve")
                        if debug and b == 1:
                            @blk.sync
                            def _(s):
                                P.w(s, "dve")
                                P.inc(s.dma_start(out=d_ton,
                                                  in_=onn[:].rearrange("p c n -> p (c n)")),
                                      "st", 16)
                                P.inc(s.dma_start(out=d_tor,
                                                  in_=kob[:].rearrange("p c n -> p (c n)")),
                                      "st", 16)
                                P.inc(s.dma_start(out=d_tyt,
                                                  in_=vob[:, 0, 0:256]), "st", 16)
                                P.inc(s.dma_start(out=d_tpy,
                                                  in_=vob[:, 0, 256:512]), "st", 16)
                                P.inc(s.dma_start(out=d_tfa,
                                                  in_=scr8[:].rearrange("p c n -> p (c n)")),
                                      "st", 16)
                                P.inc(s.dma_start(out=d_trc[0:1, 0:2048], in_=bass.AP(ro_c, 0, [[16, 128], [1, 16]]).rearrange("p c -> (p c)")[None, :] if False else rec0[:]), "st", 16)
                                P.inc(s.dma_start(out=d_trc[1:2, :], in_=rec1[:]), "st", 16)
                                P.inc(s.dma_start(out=d_twt[:, 0:16], in_=ro_c[:]), "st", 16)
                                P.inc(s.dma_start(out=d_twt[:, 16:32], in_=c_cb[:]), "st", 16)
                                P.inc(s.dma_start(out=d_twt[:, 32:48], in_=co_c[:]), "st", 16)
                                P.inc(s.dma_start(out=d_twt[:, 48:64], in_=r_cb[:]), "st", 16)
                                s.wait_ge(P.sem["st"], P.cnt["st"])
                        @blk.sync
                        def _(s, b=b):
                            P.w(s, "dve", "act", "pool", "st")
                            P.inc(s.dma_start(out=scr8[:],
                                              in_=xo_r[:, :, b * 256:(b + 1) * 256]), "ld", 16)
                        with nc.psum_tensor(f"psB_y0_{b}", [128, 256], F32) as py0, \
                             nc.psum_tensor(f"psB_y1_{b}", [128, 256], F32) as py1:
                            pys = (py0, py1)
                            wtuse2 = wtuse
                            for fc in range(8):
                                py = pys[fc % 2]
                                @blk.sync
                                def _(s, fc=fc):
                                    P.wv(s, "pe", wtuse2[fc % 2])
                                    P.w(s, "dve") if fc == 0 else None
                                    P.inc(s.dma_start(out=wt[:, fc % 2],
                                                      in_=woutbf_r[:, :, fc * 128:(fc + 1) * 128]),
                                          "ld", 16)
                                @blk.tensor
                                def _(t, fc=fc, py=py):
                                    P.w(t, "ld", "dve")
                                    if fc == 0:
                                        P.w(t, "act")
                                    for ec in range(NCH):
                                        i = t.matmul(py[:], wt[:, fc % 2, ec, :], onn[:, ec, :],
                                                     start=(ec == 0), stop=(ec == NCH - 1))
                                    P.inc(i, "pe")
                                @blk.vector
                                def _(v, fc=fc, py=py):
                                    P.w(v, "pe", "ld", "st")
                                    v.tensor_add(yt8[:, fc % 2, :], py[:], scr8[:, fc, :])
                                wtuse2[fc % 2] = P.cnt["pe"]
                                if fc % 2 == 1:
                                    @blk.sync
                                    def _(s, fc=fc, b=b):
                                        P.w(s, "dve")
                                        P.inc(s.dma_start(
                                            out=d_ypark.ap().rearrange("(c p) n -> p c n", p=128)[
                                                :, fc - 1:fc + 1, b * 256:(b + 1) * 256],
                                            in_=yt8[:]), "st", 16)
                    @blk.scalar
                    def _(sc):
                        P.w(sc, "gp", "dve")
                    @blk.vector
                    def _(v):
                        P.w(v, "gp", "st")
                    if debug and b == 0:
                        @blk.sync
                        def _(s):
                            P.inc(s.dma_start(out=d_tkb, in_=kb[:].rearrange("p c n -> p (c n)")),
                                  "st", 16)
                            P.inc(s.dma_start(out=d_tnb, in_=negb[:]), "st", 16)
                            s.wait_ge(P.sem["st"], P.cnt["st"])
        ypark_r = d_ypark.ap().rearrange("(c p) n -> p c n", p=128)

        with ExitStack() as cctx:
            csb = lambda n, sh, dt: cctx.enter_context(nc.sbuf_tensor(n, sh, dt))
            z = csb("z", [128, NCH, OWN], BF16)
            cuo = csb("cuo", [128, NCH, 256], F32)
            wt_c = csb("wt_c", [128, 2, NCH, 128], BF16)
            ysl = csb("ysl", [128, NCH, 256], F32)
            ybf_c = csb("ybf_c", [128, NCH, 256], BF16)
            tmp_c = csb("tmp_c", [128, 256], F32)

            with nc.psum_tensor("psC_su", [1, 2048], F32) as psu, \
                 nc.psum_tensor("psC_sq", [1, 2048], F32) as psq:
                for sl in range(8):
                    @blk.sync
                    def _(s, sl=sl):
                        P.w(s, "dve", "pe")
                        P.inc(s.dma_start(out=ysl[:, :, :],
                                          in_=ypark_r[:, :, sl * 256:(sl + 1) * 256]), "ld", 16)
                    @blk.vector
                    def _(v, sl=sl):
                        P.w(v, "ld", "pe")
                        v.tensor_copy(ybf_c[:], ysl[:, :, :])
                    @blk.tensor
                    def _(t, sl=sl):
                        P.w(t, "dve")
                        for c in range(NCH):
                            i = t.matmul(psu[:, sl * 256:(sl + 1) * 256], ones_bf[:],
                                         ybf_c[:, c, :], start=(c == 0), stop=(c == NCH - 1))
                        P.inc(i, "pe")
                    @blk.vector
                    def _(v, sl=sl):
                        P.w(v, "pe")
                        v.tensor_mul(ybf_c[:], ysl[:, :, :], ysl[:, :, :])
                    @blk.tensor
                    def _(t, sl=sl):
                        P.w(t, "dve")
                        for c in range(NCH):
                            i = t.matmul(psq[:, sl * 256:(sl + 1) * 256], ones_bf[:],
                                         ybf_c[:, c, :], start=(c == 0), stop=(c == NCH - 1))
                        P.inc(i, "pe")
                @blk.scalar
                def _(sc):
                    P.w(sc, "pe")
                    sc.mul(rw1[:], psu[:], 1.0 / D)
                    sc.mul(rw2[:], psq[:], 1.0 / D)
            @blk.vector
            def _(v):
                P.w(v, "act")
                for sl in range(8):
                    v.tensor_mul(tmp_c[0:1, 0:256], rw1[:, sl * 256:(sl + 1) * 256],
                                 rw1[:, sl * 256:(sl + 1) * 256])
                    v.tensor_sub(rw2[:, sl * 256:(sl + 1) * 256],
                                 rw2[:, sl * 256:(sl + 1) * 256], tmp_c[0:1, 0:256])
            @blk.scalar
            def _(sc):
                P.w(sc, "dve")
                sc.activation(rw2[:], rw2[:], AF.Ln, bias=epsc[:])
                sc.activation(rw2[:], rw2[:], AF.Exp, scale=-0.5)
            @blk.gpsimd
            def _(g):
                P.w(g, "act", "gp")
                P.inc(g.dma_start(out=d_rowb.ap(), in_=rw1[:]), "gp", 16)
                g.wait_ge(P.sem["gp"], P.cnt["gp"])
                P.inc(g.dma_start(out=d_recb.ap()[0:1, :], in_=rw2[:]), "gp", 16)
            with nc.sbuf_tensor("mu2bc", [128, OWN], F32) as mu2bc, \
                 nc.sbuf_tensor("r2bc", [128, OWN], F32) as r2bc:
                @blk.gpsimd
                def _(g):
                    P.w(g, "gp")
                    P.inc(g.dma_start(out=mu2bc[:],
                                      in_=d_rowb.ap().partition_broadcast(128)[:, 0, :]),
                          "gp", 16)
                    P.inc(g.dma_start(out=r2bc[:],
                                      in_=bass.AP(d_recb, 0, [[0, 128], [1, OWN]])), "gp", 16)
                for sl in range(8):
                    @blk.sync
                    def _(s, sl=sl):
                        P.w(s, "dve", "pe")
                        P.inc(s.dma_start(out=ysl[:, :, :],
                                          in_=ypark_r[:, :, sl * 256:(sl + 1) * 256]), "ld", 16)
                    @blk.vector
                    def _(v, sl=sl):
                        P.w(v, "gp", "ld")
                        mub = bass.AP(mu2bc, sl * 256, [list(mu2bc.ap().ap[0]), [0, NCH], [1, 256]])
                        rbb = bass.AP(r2bc, sl * 256, [list(r2bc.ap().ap[0]), [0, NCH], [1, 256]])
                        v.tensor_sub(ysl[:, :, :], ysl[:, :, :], mub)
                        v.tensor_mul(z[:, :, sl * 256:(sl + 1) * 256], ysl[:, :, :], rbb)
            rctx = ExitStack()
            wres = {w: rctx.enter_context(nc.sbuf_tensor("w_" + w, [128, NCH, D], BF16))
                    for w in ("wf", "wi", "wc", "uf", "ui", "uc")}
            cst = rctx.enter_context(nc.sbuf_tensor("cst", [128, NCH, 256], F32))
            cbf = rctx.enter_context(nc.sbuf_tensor("cbf", [128, NCH, 256], BF16))
            gates = {g: rctx.enter_context(nc.sbuf_tensor("g_" + g, [128, NCH, 256], F32))
                     for g in ("f", "i", "c")}

            @blk.sync
            def _(s):
                P.w(s, "dve")
                for wi_, w8 in enumerate(("wf", "wi", "wc", "uf", "ui", "uc")):
                    P.inc(s.dma_start(out=wres[w8][:],
                                      in_=wffnbf_r[:, :, wi_ * 1024:(wi_ + 1) * 1024]),
                          "ld", 16)
            for c in range(NCH):
                @blk.vector
                def _(v, c=c):
                    P.w(v, "gp")
                    src = bass.AP(bv["incv"], c, [list(bv["incv"].ap().ap[0]), [0, 256]])
                    P.inc(v.tensor_copy(cst[:, c, :], src), "dve")
            @blk.vector
            def _(v):
                P.w(v, "dve", "pool")
                P.inc(v.tensor_copy(cbf[:], cst[:]), "dve")
            gnames = (("f", "wf", "uf", "bfv", AF.Sigmoid),
                      ("i", "wi", "ui", "biv", AF.Sigmoid),
                      ("c", "wc", "uc", "bcv", AF.Tanh))
            with nc.psum_tensor("psC_g0", [128, 256], F32) as pg0, \
                 nc.psum_tensor("psC_g1", [128, 256], F32) as pg1, \
                 nc.psum_tensor("psC_g2", [128, 256], F32) as pg2, \
                 nc.psum_tensor("psC_g3", [128, 256], F32) as pg3:
                pgs = (pg0, pg1, pg2, pg3)
                pguse = [0, 0, 0, 0]
                gi = 0
                for t_ in range(B):
                    for (gn, wn, un, bn, af) in gnames:
                        for mc in range(NCH):
                            bi = gi % 4
                            gi += 1
                            pg = pgs[bi]
                            @blk.tensor
                            def _(t, t_=t_, wn=wn, un=un, mc=mc, pg=pg, bi=bi):
                                P.wv(t, "act", pguse[bi])
                                if mc == 0 and gn == "f":
                                    P.w(t, "dve", "pool", "ld")
                                for ec in range(NCH):
                                    t.matmul(pg[:], wres[wn][:, ec, mc * 128:(mc + 1) * 128],
                                             z[:, ec, t_ * 256:(t_ + 1) * 256],
                                             start=(ec == 0), stop=False)
                                for ec in range(NCH):
                                    i = t.matmul(pg[:], wres[un][:, ec, mc * 128:(mc + 1) * 128],
                                                 cbf[:, ec, :],
                                                 start=False, stop=(ec == NCH - 1))
                                P.inc(i, "pe")
                            @blk.scalar
                            def _(sc, gn=gn, bn=bn, af=af, mc=mc, pg=pg, bi=bi):
                                P.w(sc, "pe")
                                sc.activation(gates[gn][:, mc, :], pg[:], af,
                                              bias=bv[bn][:, mc:mc + 1])
                                pguse[bi] = P.cnt["act"]
                    for mc in range(NCH):
                        dec = blk.vector if mc % 2 == 0 else blk.gpsimd
                        nm = "dve" if mc % 2 == 0 else "pool"
                        @dec
                        def _(e, mc=mc, nm=nm):
                            P.w(e, "act")
                            scratch = tmp_c[:, :] if nm == "dve" else ysl[:, 0, :]
                            e.tensor_mul(cst[:, mc, :], cst[:, mc, :], gates["f"][:, mc, :])
                            e.tensor_mul(scratch, gates["i"][:, mc, :], gates["c"][:, mc, :])
                            e.tensor_add(cst[:, mc, :], cst[:, mc, :], scratch)
                    @blk.vector
                    def _(v, t_=t_):
                        P.w(v, "dve", "pool")
                        v.tensor_copy(cbf[:], cst[:])
            wcuse = [0, 0]
            for mc in range(NCH):
                @blk.sync
                def _(s, mc=mc):
                    P.wv(s, "pe", wcuse[mc % 2])
                    if mc == 0:
                        P.w(s, "dve")
                    P.inc(s.dma_start(out=wt_c[:, mc % 2],
                                      in_=wffnbf_r[:, :, 7 * 1024 + mc * 128:
                                                   7 * 1024 + (mc + 1) * 128]), "ld", 16)
                with nc.psum_tensor(f"psC_uo_{mc}", [128, 256], F32) as pu:
                    @blk.tensor
                    def _(t, mc=mc, pu=pu):
                        P.w(t, "ld", "act", "dve", "pool")
                        for ec in range(NCH):
                            i = t.matmul(pu[:], wt_c[:, mc % 2, ec, :], cbf[:, ec, :],
                                         start=(ec == 0), stop=(ec == NCH - 1))
                        P.inc(i, "pe")
                    wcuse[mc % 2] = P.cnt["pe"]
                    @blk.vector
                    def _(v, mc=mc, pu=pu):
                        P.w(v, "pe")
                        v.tensor_copy(cuo[:, mc, :], pu[:])
            rctx.close()
            with ExitStack() as octx:
                wo_res = octx.enter_context(nc.sbuf_tensor("wo_res", [128, NCH, D], BF16))
                ystage = octx.enter_context(nc.sbuf_tensor("ystage", [128, NCH, 256], F32))
                yo_r = d_yo.rearrange("(c p) n -> p c n", p=128)
                @blk.sync
                def _(s):
                    P.w(s, "pe")
                    P.inc(s.dma_start(out=wo_res[:],
                                      in_=wffnbf_r[:, :, 6 * 1024:7 * 1024]), "ld", 16)
                for bt in range(B):
                    @blk.sync
                    def _(s, bt=bt):
                        P.w(s, "dve", "st")
                        P.inc(s.dma_start(out=ysl[:, :, :],
                                          in_=ypark_r[:, :, bt * 256:(bt + 1) * 256]),
                              "ld", 16)
                    for mc in range(NCH):
                        with nc.psum_tensor(f"psC_o_{bt}_{mc}", [128, 256], F32) as po:
                            @blk.tensor
                            def _(t, mc=mc, bt=bt, po=po):
                                P.w(t, "ld")
                                P.w(t, "dve")
                                for ec in range(NCH):
                                    i = t.matmul(po[:], wo_res[:, ec, mc * 128:(mc + 1) * 128],
                                                 z[:, ec, bt * 256:(bt + 1) * 256],
                                                 start=(ec == 0), stop=(ec == NCH - 1))
                                P.inc(i, "pe")
                            @blk.vector
                            def _(v, mc=mc, po=po):
                                P.w(v, "pe", "act")
                                P.inc(v.tensor_add(tmp_c[:], po[:], cuo[:, mc, :]), "dve")
                            @blk.scalar
                            def _(sc, mc=mc):
                                P.w(sc, "dve")
                                P.inc(sc.activation(tmp_c[:], tmp_c[:], AF.Tanh,
                                                    bias=bv["bov"][:, mc:mc + 1]), "act")
                            @blk.vector
                            def _(v, mc=mc):
                                P.w(v, "act", "ld")
                                P.inc(v.tensor_add(ystage[:, mc, :], tmp_c[:], ysl[:, mc, :]),
                                      "dve")
                    @blk.sync
                    def _(s, bt=bt):
                        P.w(s, "dve")
                        P.inc(s.dma_start(out=yo_r[:, :, bt * 256:(bt + 1) * 256],
                                          in_=ystage[:, :, :]), "st", 16)
        @blk.sync
        def _(s):
            P.wall(s)
    return nc


_NC_CACHE = {}


def kernel(L0_attn_query, L0_attn_key, L0_attn_value, L0_attn_in_proj_weight,
           L0_attn_out_proj_weight, L0_ffn_wf, L0_ffn_uf, L0_ffn_bf, L0_ffn_wi,
           L0_ffn_ui, L0_ffn_bi, L0_ffn_wc, L0_ffn_uc, L0_ffn_bc, L0_ffn_wo,
           L0_ffn_uo, L0_ffn_bo, L0_ffn_in_c):
    x = np.asarray(L0_attn_query, np.float32)
    xf = np.ascontiguousarray(x.transpose(2, 0, 1).reshape(D, B * T))
    wqkv = np.ascontiguousarray(np.asarray(L0_attn_in_proj_weight, np.float32).T)
    wout = np.ascontiguousarray(np.asarray(L0_attn_out_proj_weight, np.float32).T)
    wmats = {k: np.ascontiguousarray(np.asarray(v, np.float32).T) for k, v in
             dict(wf=L0_ffn_wf, wi=L0_ffn_wi, wc=L0_ffn_wc, uf=L0_ffn_uf,
                  ui=L0_ffn_ui, uc=L0_ffn_uc, wo=L0_ffn_wo, uo=L0_ffn_uo).items()}
    bvecs = {k: np.ascontiguousarray(np.asarray(v, np.float32).reshape(NCH, 128).T)
             for k, v in dict(bfv=L0_ffn_bf, biv=L0_ffn_bi, bcv=L0_ffn_bc,
                              bov=L0_ffn_bo, incv=L0_ffn_in_c).items()}
    jj = np.arange(128)
    trimask = np.where(jj[:, None] > jj[None, :], np.float32(NEG), np.float32(0.0))
    trimask = np.ascontiguousarray(trimask.astype(np.float32))

    in_maps = []
    for core in range(8):
        blks = (core, 8 + core)
        xo = np.empty((D, OWN), np.float32)
        for b in range(B):
            for s_, blkid in enumerate(blks):
                xo[:, b * 256 + s_ * 128: b * 256 + (s_ + 1) * 128] = \
                    x[b, blkid * 128:(blkid + 1) * 128, :].T
        deny = np.zeros((128, 32), np.float32)
        for s_, blkid in enumerate(blks):
            for k in range(16):
                if k >= blkid:
                    deny[:, s_ * 16 + k] = NEG
        m = dict(xf=xf, xo=np.ascontiguousarray(xo), wqkv=wqkv, wout=wout,
                 deny=deny, trimask=trimask, iden=np.eye(128, dtype=np.float32))
        m.update(wmats)
        m.update(bvecs)
        in_maps.append(m)

    dbg = bool(_NC_CACHE.get("debug"))
    key = ("nc", dbg)
    if key not in _NC_CACHE:
        _NC_CACHE[key] = build_nc(debug=dbg)
    nc = _NC_CACHE[key]
    res = run_bass_kernel_spmd(nc, in_maps, core_ids=list(range(8)))

    if dbg:
        _NC_CACHE["results"] = res.results
    out = np.empty((B, T, D), np.float32)
    for core in range(8):
        yo = res.results[core]["yo"]
        for b in range(B):
            for s_, blkid in enumerate((core, 8 + core)):
                out[b, blkid * 128:(blkid + 1) * 128, :] = \
                    yo[:, b * 256 + s_ * 128: b * 256 + (s_ + 1) * 128].T
    return out


# version 09
# speedup vs baseline: 1.1636x; 1.0532x over previous
import numpy as np
from contextlib import ExitStack
import concourse.bass as bass
import concourse.mybir as mybir
from concourse.bass_utils import run_bass_kernel_spmd

F32 = mybir.dt.float32
BF16 = mybir.dt.bfloat16
AF = mybir.ActivationFunctionType
ALU = mybir.AluOpType
AX = mybir.AxisListType

D, NCH, B, T, H = 1024, 8, 8, 2048, 16
OWN = 2048
NEG = -1.0e30
EPS = 1e-5


class P_:
    def __init__(self, nc, ctx):
        self.sem = {}
        self.cnt = {}
        for n in ("ld", "gp", "pe", "act", "dve", "pool", "st"):
            self.sem[n] = ctx.enter_context(nc.semaphore(n))
            self.cnt[n] = 0
        self.tagged = set()
        self._keep = []

    def inc(self, instr, name, amt=None):
        if id(instr) in self.tagged:
            return
        amt = amt if amt is not None else 1
        instr.then_inc(self.sem[name], amt)
        self.cnt[name] += amt

    def wv(self, eng, name, val):
        if val > 0:
            eng.wait_ge(self.sem[name], val)

    def w(self, eng, *names):
        for n in names:
            if self.cnt[n] > 0:
                eng.wait_ge(self.sem[n], self.cnt[n])

    def wall(self, eng):
        self.w(eng, "ld", "gp", "pe", "act", "dve", "pool", "st")


_PASSTHRU = ("wait_ge", "sem_inc", "nop", "dma_start", "dma_start_transpose",
             "indirect_dma_start", "trigger_dma", "collective_compute")


class SeqProxy:

    def __init__(self, eng, P, nm):
        self._eng, self._P, self._nm = eng, P, nm

    def __getattr__(self, name):
        f = getattr(self._eng, name)
        if name.startswith("_") or name in _PASSTHRU or not callable(f):
            return f

        def wrap(*args, **kw):
            P, nm = self._P, self._nm
            if P.cnt[nm] > 0:
                self._eng.wait_ge(P.sem[nm], P.cnt[nm])
            instr = f(*args, **kw)
            instr.then_inc(P.sem[nm], 1)
            P.cnt[nm] += 1
            P.tagged.add(id(instr))
            P._keep.append(instr)
            return instr
        return wrap


class BlkWrap:
    def __init__(self, blk, P):
        self._blk, self._P = blk, P

    @property
    def sync(self):
        return self._blk.sync

    @property
    def tensor(self):
        return self._blk.tensor

    @property
    def vector(self):
        return lambda fn: self._blk.vector(lambda e: fn(SeqProxy(e, self._P, "dve")))

    @property
    def scalar(self):
        return lambda fn: self._blk.scalar(lambda e: fn(SeqProxy(e, self._P, "act")))

    @property
    def gpsimd(self):
        return lambda fn: self._blk.gpsimd(lambda e: fn(SeqProxy(e, self._P, "pool")))


def build_nc(debug=False):
    nc = bass.Bass("TRN2", target_bir_lowering=False)

    d_xf = nc.dram_tensor("xf", [D, B * T], F32, kind="ExternalInput").ap()
    d_xo = nc.dram_tensor("xo", [D, OWN], F32, kind="ExternalInput").ap()
    d_wqkv = nc.dram_tensor("wqkv", [D, 3 * D], F32, kind="ExternalInput").ap()
    d_wout = nc.dram_tensor("wout", [D, D], F32, kind="ExternalInput").ap()
    d_w = {w: nc.dram_tensor(w, [D, D], F32, kind="ExternalInput").ap()
           for w in ("wf", "wi", "wc", "uf", "ui", "uc", "wo", "uo")}
    d_bv = {v: nc.dram_tensor(v, [128, NCH], F32, kind="ExternalInput").ap()
            for v in ("bfv", "biv", "bcv", "bov", "incv")}
    d_deny = nc.dram_tensor("deny", [128, 32], F32, kind="ExternalInput").ap()
    d_tri = nc.dram_tensor("trimask", [128, 128], F32, kind="ExternalInput").ap()
    d_iden = nc.dram_tensor("iden", [128, 128], F32, kind="ExternalInput").ap()
    d_yo = nc.dram_tensor("yo", [D, OWN], F32, kind="ExternalOutput").ap()

    dkind = "ExternalOutput" if debug else "Internal"
    d_wqkv_bf = nc.dram_tensor("wqkv_bf", [D, 3 * D], BF16)
    d_wout_bf = nc.dram_tensor("wout_bf", [D, D], BF16)
    d_wffn_bf = nc.dram_tensor("wffn_bf", [D, 8 * D], BF16)
    d_qpark = nc.dram_tensor("q_park", [D, OWN], BF16, kind=dkind)
    d_ypark = nc.dram_tensor("y_park", [D, OWN], F32, kind=dkind)
    if debug:
        d_txh = nc.dram_tensor("t_xhat", [128, NCH * OWN], BF16, kind="ExternalOutput").ap()
        d_tkb = nc.dram_tensor("t_kb", [128, NCH * T], BF16, kind="ExternalOutput").ap()
        d_ton = nc.dram_tensor("t_onn", [128, NCH * 256], BF16, kind="ExternalOutput").ap()
        d_tor = nc.dram_tensor("t_onraw", [128, NCH * 256], BF16, kind="ExternalOutput").ap()
        d_trc = nc.dram_tensor("t_rec", [2, 2048], F32, kind="ExternalOutput").ap()
        d_tfa = nc.dram_tensor("t_fac", [128, NCH * 256], F32, kind="ExternalOutput").ap()
        d_tyt = nc.dram_tensor("t_yt", [128, 256], BF16, kind="ExternalOutput").ap()
        d_txr = nc.dram_tensor("t_xr", [128, 256], F32, kind="ExternalOutput").ap()
        d_tpy = nc.dram_tensor("t_py", [128, 256], BF16, kind="ExternalOutput").ap()
        d_twt = nc.dram_tensor("t_wt", [128, 128], F32, kind="ExternalOutput").ap()
        d_tnb = nc.dram_tensor("t_nb", [128, 1], F32, kind="ExternalOutput").ap()
        d_tz = nc.dram_tensor("t_z", [128, NCH * OWN], BF16, kind="ExternalOutput").ap()
    d_rowb = nc.dram_tensor("row_bounce", [1, T], F32)
    d_recb = nc.dram_tensor("rec_bounce", [2, 2048], F32)

    xf_r = d_xf.rearrange("(c p) n -> p c n", p=128)
    xo_r = d_xo.rearrange("(c p) n -> p c n", p=128)
    wqkv_r = d_wqkv.rearrange("(c p) f -> p c f", p=128)
    wout_r = d_wout.rearrange("(c p) f -> p c f", p=128)
    w_r = {k: v.rearrange("(c p) f -> p c f", p=128) for k, v in d_w.items()}
    qpark_r = d_qpark.ap().rearrange("(c p) n -> p c n", p=128)
    wqkvbf_r = d_wqkv_bf.ap().rearrange("(c p) f -> p c f", p=128)
    woutbf_r = d_wout_bf.ap().rearrange("(c p) f -> p c f", p=128)
    wffnbf_r = d_wffn_bf.ap().rearrange("(c p) f -> p c f", p=128)
    FFN_ORDER = ("wf", "wi", "wc", "uf", "ui", "uc", "wo", "uo")

    with ExitStack() as ctx:
        sb = lambda n, sh, dt: ctx.enter_context(nc.sbuf_tensor(n, sh, dt))
        tri = sb("tri", [128, 128], F32)
        iden = sb("iden_s", [128, 128], F32)
        deny = sb("deny_s", [128, 32], F32)
        ones_bf = sb("ones_bf", [128, 1], BF16)
        ones_f1 = sb("ones_f1", [1, 128], F32)
        idy = sb("idy", [1, 1], F32)
        bv = {v: sb(v + "_s", [128, NCH], F32) for v in d_bv}
        ro_c = sb("ro_c", [128, 16], F32)
        lo_c = sb("lo_c", [128, 16], F32)
        co_c = sb("co_c", [128, 16], F32)
        qmax2 = sb("qmax2", [1, 1], F32)
        epsc = sb("epsc", [1, 1], F32)
        rw1 = sb("rw1", [1, T], F32)
        rw2 = sb("rw2", [1, T], F32)

        P = P_(nc, ctx)
        blk = BlkWrap(ctx.enter_context(nc.Block()), P)

        @blk.sync
        def _(s):
            P.inc(s.dma_start(out=tri[:], in_=d_tri), "ld", 16)
            P.inc(s.dma_start(out=iden[:], in_=d_iden), "ld", 16)
            P.inc(s.dma_start(out=deny[:], in_=d_deny), "ld", 16)
            for v in bv:
                P.inc(s.dma_start(out=bv[v][:], in_=d_bv[v]), "ld", 16)

        @blk.vector
        def _(v):
            v.memset(ones_bf[:], 1.0)
            v.memset(ones_f1[:], 1.0)
            v.memset(idy[:], 1.0)
            v.memset(epsc[:], EPS)
            P.inc(v.memset(qmax2[:], 0.0), "dve")

        with ExitStack() as abctx:
            xa_bf = abctx.enter_context(nc.sbuf_tensor("xa_bf", [128, NCH, OWN], BF16))
            mu_own_bf = abctx.enter_context(nc.sbuf_tensor("mu_own_bf", [1, OWN], BF16))

            with ExitStack() as actx:
                asb = lambda n, sh, dt: actx.enter_context(nc.sbuf_tensor(n, sh, dt))
                x2sl = asb("x2sl_a", [128, NCH, 512], BF16)
                wq = asb("wq_a", [128, NCH, 1024], BF16)
                nwq = asb("nwq_a", [1, 1024], BF16)
                qev = asb("qev_a", [128, 512], BF16)
                q2 = asb("q2_a", [128, 512], BF16)
                rbc = asb("rbc_a", [128, 512], F32)
                qscr = asb("qscr_a", [1, 4], F32)

                wstage = asb("wstage_a", [128, NCH, 1024], BF16)

                @blk.gpsimd
                def _(g):
                    P.inc(g.dma_start(out=xa_bf[:], in_=xo_r), "gp", 16)
                    g.wait_ge(P.sem["gp"], P.cnt["gp"])
                    P.inc(g.dma_start(out=wq[:], in_=wqkv_r[:, :, 0:1024]), "gp", 16)
                for seg in range(4):
                    srcv = (wqkv_r[:, :, seg * 1024:(seg + 1) * 1024] if seg < 3
                            else wout_r[:, :, 0:1024])
                    dstv = (d_wqkv_bf.ap().rearrange("(c p) f -> p c f", p=128)[
                                :, :, seg * 1024:(seg + 1) * 1024] if seg < 3
                            else woutbf_r[:, :, 0:1024])
                    @blk.gpsimd
                    def _(g, srcv=srcv, seg=seg):
                        P.w(g, "st")
                        P.inc(g.dma_start(out=wstage[:], in_=srcv), "gp", 16)
                    @blk.sync
                    def _(s, dstv=dstv):
                        P.w(s, "gp")
                        P.inc(s.dma_start(out=dstv, in_=wstage[:]), "st", 16)

                with nc.psum_tensor("psA_su", [1, 2048], F32) as psu, \
                     nc.psum_tensor("psA_sq", [1, 2048], F32) as psq:
                    for sl in range(4):
                        @blk.vector
                        def _(v, sl=sl):
                            P.w(v, "gp", "pe")
                            v.tensor_mul(x2sl[:], xa_bf[:, :, sl * 512:(sl + 1) * 512],
                                         xa_bf[:, :, sl * 512:(sl + 1) * 512])
                        @blk.tensor
                        def _(t, sl=sl):
                            P.w(t, "dve", "gp")
                            for c in range(NCH):
                                t.matmul(psu[:, sl * 512:(sl + 1) * 512], ones_bf[:],
                                         xa_bf[:, c, sl * 512:(sl + 1) * 512],
                                         start=(c == 0), stop=(c == NCH - 1))
                                i = t.matmul(psq[:, sl * 512:(sl + 1) * 512], ones_bf[:],
                                             x2sl[:, c, :], start=(c == 0), stop=(c == NCH - 1))
                            P.inc(i, "pe")
                    @blk.scalar
                    def _(sc):
                        P.w(sc, "pe")
                        sc.mul(rw1[:], psu[:], 1.0 / D)
                        sc.mul(rw2[:], psq[:], 1.0 / D)
                        sc.copy(mu_own_bf[:], rw1[:])
                @blk.vector
                def _(v):
                    P.w(v, "act")
                    for sl in range(4):
                        v.tensor_mul(rbc[0:1, :], rw1[:, sl * 512:(sl + 1) * 512],
                                     rw1[:, sl * 512:(sl + 1) * 512])
                        v.tensor_sub(rw2[:, sl * 512:(sl + 1) * 512],
                                     rw2[:, sl * 512:(sl + 1) * 512], rbc[0:1, 0:512])
                @blk.scalar
                def _(sc):
                    P.w(sc, "dve")
                    sc.activation(rw2[:], rw2[:], AF.Ln, bias=epsc[:])
                    sc.activation(rw2[:], rw2[:], AF.Exp, scale=-0.5)
                with nc.psum_tensor("psA_tr", [128, 16], F32) as ptr:
                    @blk.tensor
                    def _(t):
                        P.w(t, "act")
                        for jt in range(16):
                            i = t.transpose(ptr[:, jt:jt + 1], rw2[:, jt * 128:(jt + 1) * 128],
                                            idy[:])
                        P.inc(i, "pe")
                    @blk.vector
                    def _(v):
                        P.w(v, "pe")
                        v.tensor_copy(ro_c[:], ptr[:])
                        v.reciprocal(co_c[:], ptr[:])
                @blk.scalar
                def _(sc):
                    P.w(sc, "dve")
                    sc.activation(lo_c[:], ro_c[:], AF.Ln)
                    sc.mul(rw1[:], rw2[:], 0.125)
                @blk.gpsimd
                def _(g):
                    P.w(g, "act")
                    P.inc(g.dma_start(out=d_rowb.ap(), in_=rw1[:]), "gp", 16)
                with nc.psum_tensor("psA_ws", [1, 128], F32) as pws:
                    for fc in range(8):
                        @blk.tensor
                        def _(t, fc=fc):
                            P.w(t, "gp")
                            if fc:
                                P.w(t, "act")
                            for ec in range(NCH):
                                i = t.matmul(pws[:], ones_bf[:],
                                             wq[:, ec, fc * 128:(fc + 1) * 128],
                                             start=(ec == 0), stop=(ec == NCH - 1))
                            P.inc(i, "pe")
                        @blk.scalar
                        def _(sc, fc=fc):
                            P.w(sc, "pe")
                            sc.mul(nwq[:, fc * 128:(fc + 1) * 128], pws[:], -1.0)
                with nc.psum_tensor("psA_q", [128, 512], F32) as pq, \
                     nc.psum_tensor("psA_n0", [1, 512], F32) as pn0, \
                     nc.psum_tensor("psA_n1", [1, 512], F32) as pn1:
                    for sl in range(4):
                        @blk.gpsimd
                        def _(g, sl=sl):
                            P.w(g, "gp", "dve")
                            P.inc(g.dma_start(
                                out=rbc[:],
                                in_=d_rowb.ap().partition_broadcast(128)[:, 0, sl * 512:(sl + 1) * 512]),
                                "gp", 16)
                        for fc in range(8):
                            @blk.tensor
                            def _(t, fc=fc, sl=sl):
                                P.w(t, "gp", "act", "dve")
                                for c in range(NCH):
                                    t.matmul(pq[:], wq[:, c, fc * 128:(fc + 1) * 128],
                                             xa_bf[:, c, sl * 512:(sl + 1) * 512],
                                             start=(c == 0), stop=False)
                                i = t.matmul(pq[:], nwq[:, fc * 128:(fc + 1) * 128],
                                             mu_own_bf[:, sl * 512:(sl + 1) * 512],
                                             start=False, stop=True)
                                P.inc(i, "pe")
                            @blk.vector
                            def _(v, fc=fc, sl=sl):
                                P.w(v, "pe", "st")
                                v.tensor_mul(qev[:], pq[:], rbc[:])
                                v.tensor_mul(q2[:], qev[:], qev[:])
                            @blk.sync
                            def _(s, fc=fc, sl=sl):
                                P.w(s, "dve")
                                P.inc(s.dma_start(
                                    out=d_qpark.ap()[fc * 128:(fc + 1) * 128,
                                                     sl * 512:(sl + 1) * 512],
                                    in_=qev[:]), "st", 16)
                            @blk.tensor
                            def _(t, fc=fc):
                                P.w(t, "dve")
                                t.matmul(pn0[:], ones_bf[0:64, :], q2[0:64, :],
                                         start=True, stop=True)
                                P.inc(t.matmul(pn1[:], ones_bf[64:128, :], q2[64:128, :],
                                               start=True, stop=True), "pe")
                            @blk.vector
                            def _(v):
                                P.w(v, "pe", "gp")
                                v.tensor_reduce(qscr[:, 0:1], pn0[:], axis=AX.X, op=ALU.max)
                                v.tensor_reduce(qscr[:, 1:2], pn1[:], axis=AX.X, op=ALU.max)
                                v.tensor_max(qscr[:, 0:1], qscr[:, 0:1], qscr[:, 1:2])
                                v.tensor_max(qmax2[:], qmax2[:], qscr[:, 0:1])
            for dec in (blk.sync, blk.gpsimd, blk.tensor, blk.scalar, blk.vector):
                @dec
                def _(e):
                    P.wall(e)
            if debug:
                @blk.sync
                def _(s):
                    P.inc(s.dma_start(out=d_txh, in_=xa_bf[:].rearrange("p c n -> p (c n)")),
                          "st", 16)
                    s.wait_ge(P.sem["st"], P.cnt["st"])

            with ExitStack() as bctx:
                bsb = lambda n, sh, dt: bctx.enter_context(nc.sbuf_tensor(n, sh, dt))
                xb_bf = bsb("xb_bf", [128, NCH, T], BF16)
                kb = bsb("kb", [128, NCH, T], BF16)
                vb = bsb("vb", [128, 16, 1040], BF16)
                kob = bsb("kob", [128, NCH, 256], BF16)
                vob = bsb("vob", [128, 2, 1040], BF16)
                nwv = bsb("nwv", [1, 1024], BF16)
                nwk = bsb("nwk", [1, 128], BF16)
                mu_b_bf = bsb("mu_b_bf", [1, T], BF16)
                wt = bsb("wt", [128, 2, NCH, 128], BF16)
                qb = bsb("qb", [128, NCH, 256], BF16)
                scr8 = bsb("scr8", [128, NCH, 256], F32)
                yt8 = bsb("yt8", [128, 2, 256], F32)
                r_cb = bsb("r_cb", [128, 16], F32)
                l_cb = bsb("l_cb", [128, 16], F32)
                c_cb = bsb("c_cb", [128, 16], F32)
                btab = bsb("btab", [128, 32], F32)
                btabd = bsb("btabd", [128, 2], F32)
                negb = bsb("negb", [128, 1], F32)
                bscr = bsb("bscr", [1, 8], F32)


                wstage_b = bsb("wstage_b", [128, NCH, 512], BF16)
                wtuse = [0, 0]
                for seg in range(16):
                    w8 = FFN_ORDER[seg // 2]
                    half = seg % 2
                    @blk.gpsimd
                    def _(g, w8=w8, half=half):
                        P.w(g, "st")
                        P.inc(g.dma_start(out=wstage_b[:],
                                          in_=w_r[w8][:, :, half * 512:(half + 1) * 512]),
                              "gp", 16)
                    @blk.sync
                    def _(s, seg=seg, w8=w8, half=half):
                        P.w(s, "gp")
                        P.inc(s.dma_start(
                            out=wffnbf_r[:, :, seg * 512:(seg + 1) * 512],
                            in_=wstage_b[:]), "st", 16)
                for b in range(B):
                    bscope = ExitStack()
                    x2b = bscope.enter_context(nc.sbuf_tensor(f"x2b_{b}", [128, NCH, 128], BF16))
                    wvres = bscope.enter_context(
                        nc.sbuf_tensor(f"wvres_{b}", [128, NCH, 512], BF16))
                    @blk.gpsimd
                    def _(g, b=b):
                        P.w(g, "pe", "dve")
                        P.inc(g.dma_start(out=xb_bf[:],
                                          in_=xf_r[:, :, b * T:(b + 1) * T]), "gp", 16)
                    with nc.psum_tensor(f"psB_su{b}", [1, 2048], F32) as psu, \
                         nc.psum_tensor(f"psB_sq{b}", [1, 2048], F32) as psq:
                        for sl in range(16):
                            @blk.vector
                            def _(v, sl=sl):
                                P.w(v, "gp", "pe")
                                v.tensor_mul(x2b[:], xb_bf[:, :, sl * 128:(sl + 1) * 128],
                                             xb_bf[:, :, sl * 128:(sl + 1) * 128])
                            @blk.tensor
                            def _(t, sl=sl):
                                P.w(t, "dve", "gp")
                                if sl == 0:
                                    P.w(t, "act")
                                for c in range(NCH):
                                    t.matmul(psu[:, sl * 128:(sl + 1) * 128], ones_bf[:],
                                             xb_bf[:, c, sl * 128:(sl + 1) * 128],
                                             start=(c == 0), stop=(c == NCH - 1))
                                    i = t.matmul(psq[:, sl * 128:(sl + 1) * 128], ones_bf[:],
                                                 x2b[:, c, :], start=(c == 0), stop=(c == NCH - 1))
                                P.inc(i, "pe")
                        @blk.scalar
                        def _(sc):
                            P.w(sc, "pe")
                            sc.mul(rw1[:], psu[:], 1.0 / D)
                            sc.mul(rw2[:], psq[:], 1.0 / D)
                            sc.copy(mu_b_bf[:], rw1[:])
                    @blk.vector
                    def _(v):
                        P.w(v, "act")

                        s8f = scr8[:].rearrange("p c n -> p (c n)")
                        for sl in range(4):
                            v.tensor_mul(s8f[0:1, 0:512],
                                         rw1[:, sl * 512:(sl + 1) * 512],
                                         rw1[:, sl * 512:(sl + 1) * 512])
                            v.tensor_sub(rw2[:, sl * 512:(sl + 1) * 512],
                                         rw2[:, sl * 512:(sl + 1) * 512], s8f[0:1, 0:512])
                    @blk.scalar
                    def _(sc):
                        P.w(sc, "dve")
                        sc.activation(rw1[:], rw2[:], AF.Ln, bias=epsc[:])
                        sc.activation(rw2[:], rw1[:], AF.Exp, scale=-0.5)
                    with nc.psum_tensor(f"psB_tr{b}", [128, 16], F32) as ptrb:
                        @blk.tensor
                        def _(t):
                            P.w(t, "act")
                            for jt in range(16):
                                i = t.transpose(ptrb[:, jt:jt + 1],
                                                rw2[:, jt * 128:(jt + 1) * 128], idy[:])
                            P.inc(i, "pe")
                        @blk.vector
                        def _(v):
                            P.w(v, "pe")
                            v.tensor_copy(r_cb[:], ptrb[:])
                            v.reciprocal(c_cb[:], ptrb[:])
                    @blk.scalar
                    def _(sc):
                        P.w(sc, "dve")
                        sc.activation(l_cb[:], r_cb[:], AF.Ln)
                    @blk.vector
                    def _(v):
                        P.w(v, "act")
                        v.tensor_reduce(bscr[:, 1:2], rw2[:], axis=AX.X, op=ALU.max)
                        v.tensor_reduce(bscr[:, 2:3], rw1[:], axis=AX.X, op=ALU.min)
                        v.tensor_scalar_mul(bscr[:, 2:3], bscr[:, 2:3], -0.5)
                    for fi in range(8):
                        @blk.sync
                        def _(s, fi=fi):
                            P.wv(s, "pe", wtuse[fi % 2])
                            P.inc(s.dma_start(out=wt[:, fi % 2],
                                              in_=wqkvbf_r[:, :, 1024 + fi * 128:1024 + (fi + 1) * 128]),
                                  "ld", 16)
                        with nc.psum_tensor(f"psB_wk{b}_{fi}", [1, 128], F32) as pws2:
                            @blk.tensor
                            def _(t, fi=fi, pws2=pws2):
                                P.w(t, "ld", "act")
                                for ec in range(NCH):
                                    i = t.matmul(pws2[:], ones_bf[:], wt[:, fi % 2, ec, :],
                                                 start=(ec == 0), stop=(ec == NCH - 1))
                                P.inc(i, "pe")
                            @blk.scalar
                            def _(sc, pws2=pws2):
                                P.w(sc, "pe")
                                sc.mul(nwk[:], pws2[:], -1.0)
                        with nc.psum_tensor(f"psB_k{b}_{fi}", [128, 512], F32) as pk:
                            for tt in range(4):
                                @blk.tensor
                                def _(t, fi=fi, tt=tt, pk=pk, b=b):
                                    P.w(t, "act", "dve")
                                    for c in range(NCH):
                                        t.matmul(pk[:], wt[:, fi % 2, c, :],
                                                 xb_bf[:, c, tt * 512:(tt + 1) * 512],
                                                 start=(c == 0), stop=False)
                                    i = t.matmul(pk[:], nwk[:],
                                                 mu_b_bf[:, tt * 512:(tt + 1) * 512],
                                                 start=False, stop=True)
                                    P.inc(i, "pe")
                                @blk.vector
                                def _(v, fi=fi, tt=tt, pk=pk):
                                    P.w(v, "pe")
                                    v.tensor_copy(kb[:, fi, tt * 512:(tt + 1) * 512], pk[:])
                            with nc.psum_tensor(f"psB_ko{b}_{fi}", [128, 256], F32) as pko:
                                @blk.tensor
                                def _(t, fi=fi, b=b, pko=pko):
                                    P.w(t, "dve")
                                    for c in range(NCH):
                                        t.matmul(pko[:], wt[:, fi % 2, c, :],
                                                 xa_bf[:, c, b * 256:(b + 1) * 256],
                                                 start=(c == 0), stop=False)
                                    i = t.matmul(pko[:], nwk[:],
                                                 mu_own_bf[:, b * 256:(b + 1) * 256],
                                                 start=False, stop=True)
                                    P.inc(i, "pe")
                                @blk.vector
                                def _(v, fi=fi, pko=pko):
                                    P.w(v, "pe")
                                    v.tensor_copy(kob[:, fi, :], pko[:])
                            wtuse[fi % 2] = P.cnt["pe"]
                    with nc.psum_tensor(f"psB_v0_{b}", [128, 512], F32) as pv0, \
                         nc.psum_tensor(f"psB_v1_{b}", [128, 512], F32) as pv1, \
                         nc.psum_tensor(f"psB_wsv{b}", [1, 512], F32) as pwsv:
                        pvs = (pv0, pv1)
                        for ft in range(2):
                            @blk.sync
                            def _(s, ft=ft):
                                P.w(s, "pe")
                                P.inc(s.dma_start(out=wvres[:],
                                                  in_=wqkvbf_r[:, :, 2048 + ft * 512:
                                                               2048 + (ft + 1) * 512]), "ld", 16)
                            @blk.tensor
                            def _(t, ft=ft):
                                P.w(t, "ld", "act")
                                for ec in range(NCH):
                                    i = t.matmul(pwsv[:], ones_bf[:], wvres[:, ec, :],
                                                 start=(ec == 0), stop=(ec == NCH - 1))
                                P.inc(i, "pe")
                            @blk.scalar
                            def _(sc, ft=ft):
                                P.w(sc, "pe")
                                sc.mul(nwv[:, 0:512], pwsv[:], -1.0)
                            for jt in range(16):
                                pv = pvs[jt % 2]
                                @blk.tensor
                                def _(t, jt=jt, ft=ft, pv=pv, b=b):
                                    P.w(t, "dve", "act")
                                    for ec in range(NCH):
                                        t.matmul(pv[:],
                                                 xb_bf[:, ec, jt * 128:(jt + 1) * 128],
                                                 wvres[:, ec, :],
                                                 start=(ec == 0), stop=False)
                                    i = t.matmul(pv[:],
                                                 mu_b_bf[:, jt * 128:(jt + 1) * 128],
                                                 nwv[:, 0:512],
                                                 start=False, stop=True)
                                    P.inc(i, "pe")
                                @blk.vector
                                def _(v, jt=jt, ft=ft, pv=pv):
                                    P.w(v, "pe")
                                    dst = vb[:].rearrange("p jt (h e) -> p jt h e", h=16)[
                                        :, jt, ft * 8:(ft + 1) * 8, 0:64]
                                    v.tensor_copy(dst, pv[:].rearrange("p (h e) -> p h e", h=8))
                            for sj in range(2):
                                pv = pvs[sj]
                                @blk.tensor
                                def _(t, sj=sj, ft=ft, b=b, pv=pv):
                                    P.w(t, "dve")
                                    for ec in range(NCH):
                                        t.matmul(pv[:],
                                                 xa_bf[:, ec, b * 256 + sj * 128:
                                                        b * 256 + (sj + 1) * 128],
                                                 wvres[:, ec, :],
                                                 start=(ec == 0), stop=False)
                                    i = t.matmul(pv[:],
                                                 mu_own_bf[:, b * 256 + sj * 128:
                                                           b * 256 + (sj + 1) * 128],
                                                 nwv[:, 0:512],
                                                 start=False, stop=True)
                                    P.inc(i, "pe")
                                @blk.vector
                                def _(v, sj=sj, ft=ft, pv=pv):
                                    P.w(v, "pe")
                                    dst = vob[:].rearrange("p jt (h e) -> p jt h e", h=16)[
                                        :, sj, ft * 8:(ft + 1) * 8, 0:64]
                                    v.tensor_copy(dst, pv[:].rearrange("p (h e) -> p h e", h=8))
                        for jt in range(16):
                            @blk.vector
                            def _(v, jt=jt):
                                dst = vb[:].rearrange("p jt (h e) -> p jt h e", h=16)[:, jt, :, 64:65]
                                src2 = bass.AP(c_cb, jt, [list(c_cb.ap().ap[0]), [0, 16], [1, 1]])
                                v.tensor_copy(dst, src2)
                        for sj in range(2):
                            @blk.vector
                            def _(v, sj=sj, b=b):
                                dst = vob[:].rearrange("p jt (h e) -> p jt h e", h=16)[:, sj, :, 64:65]
                                src2 = bass.AP(co_c, b * 2 + sj, [list(co_c.ap().ap[0]), [0, 16], [1, 1]])
                                v.tensor_copy(dst, src2)

                    @blk.vector
                    def _(v):
                        P.w(v, "act", "pe")
                        P.inc(v.tensor_reduce(
                            negb[:].rearrange("p (a c) -> p a c", a=1), kb[:],
                            axis=AX.XY, op=ALU.max, apply_absolute_value=True), "dve")
                    with nc.psum_tensor(f"psB_km{b}", [1, 128], F32) as pkm, \
                         nc.psum_tensor(f"psB_nb{b}", [128, 1], F32) as pnb:
                        @blk.tensor
                        def _(t, pkm=pkm):
                            P.w(t, "dve")
                            P.inc(t.transpose(pkm[:], negb[:], iden[:]), "pe")
                        @blk.vector
                        def _(v, pkm=pkm):
                            P.w(v, "pe")
                            v.tensor_reduce(bscr[:, 0:1], pkm[:], axis=AX.X, op=ALU.max)
                            P.inc(v.tensor_mul(bscr[:, 0:1], bscr[:, 0:1], bscr[:, 1:2]), "dve")
                        @blk.scalar
                        def _(sc):
                            P.w(sc, "dve")
                            sc.activation(bscr[:, 3:4], qmax2[:], AF.Ln)
                            P.inc(sc.activation(bscr[:, 3:4], bscr[:, 3:4], AF.Exp, scale=0.5),
                                  "act")
                        @blk.vector
                        def _(v):
                            P.w(v, "act")
                            v.tensor_mul(bscr[:, 0:1], bscr[:, 0:1], bscr[:, 3:4])
                            v.tensor_scalar_mul(bscr[:, 0:1], bscr[:, 0:1], 8.0)
                            v.tensor_add(bscr[:, 0:1], bscr[:, 0:1], bscr[:, 2:3])
                            P.inc(v.tensor_scalar_mul(bscr[:, 0:1], bscr[:, 0:1], -1.0), "dve")
                        @blk.tensor
                        def _(t, pnb=pnb):
                            P.w(t, "dve")
                            P.inc(t.matmul(pnb[:], ones_f1[:], bscr[:, 0:1],
                                           start=True, stop=True), "pe")
                        @blk.vector
                        def _(v, pnb=pnb):
                            P.w(v, "pe")
                            P.inc(v.tensor_copy(negb[:], pnb[:]), "dve")
                    @blk.vector
                    def _(v, b=b):
                        P.w(v, "dve")
                        lview = bass.AP(l_cb, 0, [list(l_cb.ap().ap[0]), [0, 2], [1, 16]])
                        v.tensor_add(btab[:].rearrange("p (s k) -> p s k", s=2), lview,
                                     deny[:].rearrange("p (s k) -> p s k", s=2))
                        v.tensor_scalar_add(btab[:], btab[:], negb[:])
                        v.tensor_copy(btabd[:], lo_c[:, 2 * b:2 * b + 2])
                        P.inc(v.tensor_scalar_add(btabd[:], btabd[:], negb[:]), "dve")
                    @blk.sync
                    def _(s, b=b):
                        P.w(s, "st", "pe")
                        P.inc(s.dma_start(out=qb[:],
                                          in_=qpark_r[:, :, b * 256:(b + 1) * 256]), "ld", 16)
                    @blk.tensor
                    def _(t):
                        P.w(t, "ld", "dve", "act", "pool")
                    @blk.scalar
                    def _(sc):
                        P.w(sc, "dve", "pool")
                    bscope.close()

                    with ExitStack() as sctx:
                        ssb = lambda n, sh, dt: sctx.enter_context(nc.sbuf_tensor(n, sh, dt))
                        psb = [[ssb(f"psb{se}{h}_{b}", [128, 512], BF16) for h in range(2)]
                               for se in range(2)]
                        onraw = ssb(f"onraw_{b}", [128, NCH, 256], F32)
                        onn = ssb(f"onn_{b}", [128, NCH, 256], BF16)
                        rec0, rec1 = rw1, rw2
                        recs = (rec0, rec1)
                        with nc.psum_tensor(f"psB_sc00_{b}", [128, 512], F32) as sc00, \
                             nc.psum_tensor(f"psB_sc01_{b}", [128, 512], F32) as sc01, \
                             nc.psum_tensor(f"psB_sc10_{b}", [128, 512], F32) as sc10, \
                             nc.psum_tensor(f"psB_sc11_{b}", [128, 512], F32) as sc11, \
                             nc.psum_tensor(f"psB_av00_{b}", [128, 512], F32) as av00, \
                             nc.psum_tensor(f"psB_av01_{b}", [128, 512], F32) as av01, \
                             nc.psum_tensor(f"psB_av10_{b}", [128, 512], F32) as av10, \
                             nc.psum_tensor(f"psB_av11_{b}", [128, 512], F32) as av11:
                            sc = ((sc00, sc01), (sc10, sc11))
                            av = ((av00, av01), (av10, av11))
                            expcnt = [0, 0]
                            avevcnt = [(P.cnt["dve"], P.cnt["pool"]),
                                       (P.cnt["dve"], P.cnt["pool"])]
                            tricnt = [P.cnt["dve"]]
                            pending = [None]
                            slot = 0
                            for hpg in range(2):
                                hps = [4 * hpg + i_ for i_ in range(4)]
                                for s_ in range(2):
                                    g2 = slot % 2
                                    units = [(True, 0)] + [(False, k) for k in range(8 * (s_ + 1))]
                                    nu = len(units)
                                    pe_base = P.cnt["pe"]
                                    act_base = P.cnt["act"]
                                    dve_tri = P.cnt["dve"] + 2
                                    @blk.scalar
                                    def _(scl_, units=units, s_=s_, b=b, pe_base=pe_base,
                                          dve_tri=dve_tri):
                                        for u, (isdiag, k) in enumerate(units):
                                            p2, r2 = divmod(u, 2)
                                            P.wv(scl_, "pe", pe_base + 4 * p2 + 1 + r2)
                                            if isdiag:
                                                P.wv(scl_, "dve", dve_tri)
                                                scalev = ro_c[:, 2 * b + s_: 2 * b + s_ + 1]
                                                bia = btabd[:, s_:s_ + 1]
                                            else:
                                                scalev = r_cb[:, k:k + 1]
                                                bia = btab[:, s_ * 16 + k: s_ * 16 + k + 1]
                                            se = u % 2
                                            for h01 in range(2):
                                                scl_.activation(psb[se][h01][:],
                                                                sc[se][h01][:],
                                                                AF.Exp, bias=bia,
                                                                scale=scalev)
                                    assert P.cnt["act"] == act_base + 2 * nu
                                    ui = 0
                                    first_pair = True
                                    while ui < nu:
                                        pu = [(ui + j, units[ui + j][0], units[ui + j][1],
                                               (ui + j) % 2) for j in range(min(2, nu - ui))]
                                        for (uj, isdiag, k, se) in pu:
                                            @blk.tensor
                                            def _(t, s_=s_, isdiag=isdiag, k=k, uj=uj, se=se,
                                                  hps=hps):
                                                P.wv(t, "act", expcnt[se])
                                                if uj == 0:
                                                    P.wv(t, "dve", tricnt[0])
                                                for hi, hp in enumerate(hps):
                                                    lt = (kob[:, hp, s_ * 128:(s_ + 1) * 128]
                                                          if isdiag
                                                          else kb[:, hp, k * 128:(k + 1) * 128])
                                                    t.matmul(sc[se][0][:, hi * 128:(hi + 1) * 128],
                                                             lt[0:64, :],
                                                             qb[0:64, hp, s_ * 128:(s_ + 1) * 128],
                                                             start=True, stop=True,
                                                             tile_position=(0, 0))
                                                    i = t.matmul(
                                                        sc[se][1][:, hi * 128:(hi + 1) * 128],
                                                        lt[64:128, :],
                                                        qb[64:128, hp, s_ * 128:(s_ + 1) * 128],
                                                        start=True, stop=True,
                                                        tile_position=(64, 0))
                                                P.inc(i, "pe")
                                            assert P.cnt["pe"] == pe_base + 4 * (uj // 2) + 1 + uj % 2
                                            if isdiag:
                                                @blk.vector
                                                def _(v, se=se):
                                                    P.w(v, "pe")
                                                    trib = bass.AP(tri, 0,
                                                                   [list(tri.ap().ap[0]),
                                                                    [0, 4], [1, 128]])
                                                    for h01 in range(2):
                                                        v.tensor_add(
                                                            sc[se][h01][:].rearrange(
                                                                "p (hp i) -> p hp i", hp=4),
                                                            sc[se][h01][:].rearrange(
                                                                "p (hp i) -> p hp i", hp=4),
                                                            trib)
                                                tricnt[0] = P.cnt["dve"]
                                                assert tricnt[0] == dve_tri
                                        for (uj, isdiag, k, se) in pu:
                                            @blk.tensor
                                            def _(t, s_=s_, isdiag=isdiag, k=k, uj=uj, se=se,
                                                  hps=hps, g2=g2,
                                                  ecnt=act_base + 2 * (uj + 1),
                                                  last_u=(uj == nu - 1)):
                                                P.wv(t, "act", ecnt)
                                                if uj == 0:
                                                    P.wv(t, "dve", avevcnt[g2][0])
                                                    P.wv(t, "pool", avevcnt[g2][1])
                                                for h01 in range(2):
                                                    for hi, hp in enumerate(hps):
                                                        h = 2 * hp + h01
                                                        vt = (vob[:, s_, h * 65:(h + 1) * 65]
                                                              if isdiag
                                                              else vb[:, k, h * 65:(h + 1) * 65])
                                                        i = t.matmul(
                                                            av[g2][h01][0:65,
                                                                        hi * 128:(hi + 1) * 128],
                                                            vt,
                                                            psb[se][h01][:,
                                                                         hi * 128:(hi + 1) * 128],
                                                            start=(uj == 0 and hi == 0),
                                                            stop=last_u)
                                                P.inc(i, "pe")
                                        ui += len(pu)
                                        if first_pair:
                                            first_pair = False
                                            if pending[0] is not None:
                                                pending[0]()
                                                pending[0] = None
                                    last_even = nu - 1 if (nu - 1) % 2 == 0 else nu - 2
                                    last_odd = nu - 1 if (nu - 1) % 2 == 1 else nu - 2
                                    expcnt[0] = act_base + 2 * (last_even + 1)
                                    expcnt[1] = act_base + 2 * (last_odd + 1)
                                    def _evac(s_=s_, hpg=hpg, g2=g2):
                                        @blk.vector
                                        def _(v):
                                            P.w(v, "pe")
                                            for h01 in range(2):
                                                dst = onraw[h01 * 64:(h01 + 1) * 64,
                                                            4 * hpg:4 * hpg + 4,
                                                            s_ * 128:(s_ + 1) * 128]
                                                v.tensor_copy(dst,
                                                              av[g2][h01][0:64, :].rearrange(
                                                                  "p (c n) -> p c n", c=4))
                                        @blk.vector
                                        def _(v):
                                            for h01 in range(2):
                                                rdst = bass.AP(recs[h01],
                                                               4 * hpg * 256 + s_ * 128,
                                                               [list(recs[h01].ap().ap[0]),
                                                                [256, 4], [1, 128]])
                                                v.tensor_copy(rdst, av[g2][h01][64:65, :]
                                                              .rearrange("p (c n) -> p c n",
                                                                         c=4))
                                        avevcnt[g2] = (P.cnt["dve"], P.cnt["pool"])
                                    pending[0] = _evac
                                    slot += 1
                            if pending[0] is not None:
                                pending[0]()
                                pending[0] = None
                        @blk.gpsimd
                        def _(g):
                            P.w(g, "dve", "gp")
                            P.inc(g.dma_start(out=d_recb.ap()[0:1, :], in_=rec0[:]), "gp", 16)
                            P.inc(g.dma_start(out=d_recb.ap()[1:2, :], in_=rec1[:]), "gp", 16)
                            g.wait_ge(P.sem["gp"], P.cnt["gp"])
                            for a in range(2):
                                rsrc = bass.AP(d_recb, a * 2048, [[0, 64], [1, 2048]])
                                rdst = scr8[a * 64:(a + 1) * 64, :, :].rearrange(
                                    "p c n -> p (c n)")
                                P.inc(g.dma_start(out=rdst, in_=rsrc), "gp", 16)
                        @blk.vector
                        def _(v):
                            P.w(v, "gp", "dve")
                            s8f = scr8[:].rearrange("p c n -> p (c n)")
                            v.reciprocal(s8f, s8f)
                            P.inc(v.tensor_mul(onn[:], onraw[:], scr8[:]), "dve")
                        if debug and b == 1:
                            @blk.sync
                            def _(s):
                                P.w(s, "dve")
                                P.inc(s.dma_start(out=d_ton,
                                                  in_=onn[:].rearrange("p c n -> p (c n)")),
                                      "st", 16)
                                P.inc(s.dma_start(out=d_tor,
                                                  in_=kob[:].rearrange("p c n -> p (c n)")),
                                      "st", 16)
                                P.inc(s.dma_start(out=d_tyt,
                                                  in_=vob[:, 0, 0:256]), "st", 16)
                                P.inc(s.dma_start(out=d_tpy,
                                                  in_=vob[:, 0, 256:512]), "st", 16)
                                P.inc(s.dma_start(out=d_tfa,
                                                  in_=scr8[:].rearrange("p c n -> p (c n)")),
                                      "st", 16)
                                P.inc(s.dma_start(out=d_trc[0:1, 0:2048], in_=bass.AP(ro_c, 0, [[16, 128], [1, 16]]).rearrange("p c -> (p c)")[None, :] if False else rec0[:]), "st", 16)
                                P.inc(s.dma_start(out=d_trc[1:2, :], in_=rec1[:]), "st", 16)
                                P.inc(s.dma_start(out=d_twt[:, 0:16], in_=ro_c[:]), "st", 16)
                                P.inc(s.dma_start(out=d_twt[:, 16:32], in_=c_cb[:]), "st", 16)
                                P.inc(s.dma_start(out=d_twt[:, 32:48], in_=co_c[:]), "st", 16)
                                P.inc(s.dma_start(out=d_twt[:, 48:64], in_=r_cb[:]), "st", 16)
                                s.wait_ge(P.sem["st"], P.cnt["st"])
                        @blk.sync
                        def _(s, b=b):
                            P.w(s, "dve", "act", "pool", "st")
                            P.inc(s.dma_start(out=scr8[:],
                                              in_=xo_r[:, :, b * 256:(b + 1) * 256]), "ld", 16)
                        with nc.psum_tensor(f"psB_y0_{b}", [128, 256], F32) as py0, \
                             nc.psum_tensor(f"psB_y1_{b}", [128, 256], F32) as py1:
                            pys = (py0, py1)
                            wtuse2 = wtuse
                            for fc in range(8):
                                py = pys[fc % 2]
                                @blk.sync
                                def _(s, fc=fc):
                                    P.wv(s, "pe", wtuse2[fc % 2])
                                    P.w(s, "dve") if fc == 0 else None
                                    P.inc(s.dma_start(out=wt[:, fc % 2],
                                                      in_=woutbf_r[:, :, fc * 128:(fc + 1) * 128]),
                                          "ld", 16)
                                @blk.tensor
                                def _(t, fc=fc, py=py):
                                    P.w(t, "ld", "dve")
                                    if fc == 0:
                                        P.w(t, "act")
                                    for ec in range(NCH):
                                        i = t.matmul(py[:], wt[:, fc % 2, ec, :], onn[:, ec, :],
                                                     start=(ec == 0), stop=(ec == NCH - 1))
                                    P.inc(i, "pe")
                                @blk.vector
                                def _(v, fc=fc, py=py):
                                    P.w(v, "pe", "ld", "st")
                                    v.tensor_add(yt8[:, fc % 2, :], py[:], scr8[:, fc, :])
                                wtuse2[fc % 2] = P.cnt["pe"]
                                if fc % 2 == 1:
                                    @blk.sync
                                    def _(s, fc=fc, b=b):
                                        P.w(s, "dve")
                                        P.inc(s.dma_start(
                                            out=d_ypark.ap().rearrange("(c p) n -> p c n", p=128)[
                                                :, fc - 1:fc + 1, b * 256:(b + 1) * 256],
                                            in_=yt8[:]), "st", 16)
                    @blk.scalar
                    def _(sc):
                        P.w(sc, "gp", "dve")
                    @blk.vector
                    def _(v):
                        P.w(v, "gp", "st")
                    if debug and b == 0:
                        @blk.sync
                        def _(s):
                            P.inc(s.dma_start(out=d_tkb, in_=kb[:].rearrange("p c n -> p (c n)")),
                                  "st", 16)
                            P.inc(s.dma_start(out=d_tnb, in_=negb[:]), "st", 16)
                            s.wait_ge(P.sem["st"], P.cnt["st"])
        ypark_r = d_ypark.ap().rearrange("(c p) n -> p c n", p=128)

        with ExitStack() as cctx:
            csb = lambda n, sh, dt: cctx.enter_context(nc.sbuf_tensor(n, sh, dt))
            z = csb("z", [128, NCH, OWN], BF16)
            cuo = csb("cuo", [128, NCH, 256], F32)
            wt_c = csb("wt_c", [128, 2, NCH, 128], BF16)
            ysl = csb("ysl", [128, NCH, 256], F32)
            ybf_c = csb("ybf_c", [128, NCH, 256], BF16)
            tmp_c = csb("tmp_c", [128, 256], F32)

            with nc.psum_tensor("psC_su", [1, 2048], F32) as psu, \
                 nc.psum_tensor("psC_sq", [1, 2048], F32) as psq:
                for sl in range(8):
                    @blk.sync
                    def _(s, sl=sl):
                        P.w(s, "dve", "pe")
                        P.inc(s.dma_start(out=ysl[:, :, :],
                                          in_=ypark_r[:, :, sl * 256:(sl + 1) * 256]), "ld", 16)
                    @blk.vector
                    def _(v, sl=sl):
                        P.w(v, "ld", "pe")
                        v.tensor_copy(ybf_c[:], ysl[:, :, :])
                    @blk.tensor
                    def _(t, sl=sl):
                        P.w(t, "dve")
                        for c in range(NCH):
                            i = t.matmul(psu[:, sl * 256:(sl + 1) * 256], ones_bf[:],
                                         ybf_c[:, c, :], start=(c == 0), stop=(c == NCH - 1))
                        P.inc(i, "pe")
                    @blk.vector
                    def _(v, sl=sl):
                        P.w(v, "pe")
                        v.tensor_mul(ybf_c[:], ysl[:, :, :], ysl[:, :, :])
                    @blk.tensor
                    def _(t, sl=sl):
                        P.w(t, "dve")
                        for c in range(NCH):
                            i = t.matmul(psq[:, sl * 256:(sl + 1) * 256], ones_bf[:],
                                         ybf_c[:, c, :], start=(c == 0), stop=(c == NCH - 1))
                        P.inc(i, "pe")
                @blk.scalar
                def _(sc):
                    P.w(sc, "pe")
                    sc.mul(rw1[:], psu[:], 1.0 / D)
                    sc.mul(rw2[:], psq[:], 1.0 / D)
            @blk.vector
            def _(v):
                P.w(v, "act")
                for sl in range(8):
                    v.tensor_mul(tmp_c[0:1, 0:256], rw1[:, sl * 256:(sl + 1) * 256],
                                 rw1[:, sl * 256:(sl + 1) * 256])
                    v.tensor_sub(rw2[:, sl * 256:(sl + 1) * 256],
                                 rw2[:, sl * 256:(sl + 1) * 256], tmp_c[0:1, 0:256])
            @blk.scalar
            def _(sc):
                P.w(sc, "dve")
                sc.activation(rw2[:], rw2[:], AF.Ln, bias=epsc[:])
                sc.activation(rw2[:], rw2[:], AF.Exp, scale=-0.5)
            @blk.gpsimd
            def _(g):
                P.w(g, "act", "gp")
                P.inc(g.dma_start(out=d_rowb.ap(), in_=rw1[:]), "gp", 16)
                g.wait_ge(P.sem["gp"], P.cnt["gp"])
                P.inc(g.dma_start(out=d_recb.ap()[0:1, :], in_=rw2[:]), "gp", 16)
            with nc.sbuf_tensor("mu2bc", [128, OWN], F32) as mu2bc, \
                 nc.sbuf_tensor("r2bc", [128, OWN], F32) as r2bc:
                @blk.gpsimd
                def _(g):
                    P.w(g, "gp")
                    P.inc(g.dma_start(out=mu2bc[:],
                                      in_=d_rowb.ap().partition_broadcast(128)[:, 0, :]),
                          "gp", 16)
                    P.inc(g.dma_start(out=r2bc[:],
                                      in_=bass.AP(d_recb, 0, [[0, 128], [1, OWN]])), "gp", 16)
                for sl in range(8):
                    @blk.sync
                    def _(s, sl=sl):
                        P.w(s, "dve", "pe")
                        P.inc(s.dma_start(out=ysl[:, :, :],
                                          in_=ypark_r[:, :, sl * 256:(sl + 1) * 256]), "ld", 16)
                    @blk.vector
                    def _(v, sl=sl):
                        P.w(v, "gp", "ld")
                        mub = bass.AP(mu2bc, sl * 256, [list(mu2bc.ap().ap[0]), [0, NCH], [1, 256]])
                        rbb = bass.AP(r2bc, sl * 256, [list(r2bc.ap().ap[0]), [0, NCH], [1, 256]])
                        v.tensor_sub(ysl[:, :, :], ysl[:, :, :], mub)
                        v.tensor_mul(z[:, :, sl * 256:(sl + 1) * 256], ysl[:, :, :], rbb)
            rctx = ExitStack()
            wres = {w: rctx.enter_context(nc.sbuf_tensor("w_" + w, [128, NCH, D], BF16))
                    for w in ("wf", "wi", "wc", "uf", "ui", "uc")}
            cst = rctx.enter_context(nc.sbuf_tensor("cst", [128, NCH, 256], F32))
            cbf = rctx.enter_context(nc.sbuf_tensor("cbf", [128, NCH, 256], BF16))
            gates = {g: rctx.enter_context(nc.sbuf_tensor("g_" + g, [128, NCH, 256], F32))
                     for g in ("f", "i", "c")}

            @blk.sync
            def _(s):
                P.w(s, "dve")
                for wi_, w8 in enumerate(("wf", "wi", "wc", "uf", "ui", "uc")):
                    P.inc(s.dma_start(out=wres[w8][:],
                                      in_=wffnbf_r[:, :, wi_ * 1024:(wi_ + 1) * 1024]),
                          "ld", 16)
            for c in range(NCH):
                @blk.vector
                def _(v, c=c):
                    P.w(v, "gp")
                    src = bass.AP(bv["incv"], c, [list(bv["incv"].ap().ap[0]), [0, 256]])
                    P.inc(v.tensor_copy(cst[:, c, :], src), "dve")
            @blk.vector
            def _(v):
                P.w(v, "dve", "pool")
                P.inc(v.tensor_copy(cbf[:], cst[:]), "dve")
            gnames = (("f", "wf", "uf", "bfv", AF.Sigmoid),
                      ("i", "wi", "ui", "biv", AF.Sigmoid),
                      ("c", "wc", "uc", "bcv", AF.Tanh))
            with nc.psum_tensor("psC_g0", [128, 256], F32) as pg0, \
                 nc.psum_tensor("psC_g1", [128, 256], F32) as pg1, \
                 nc.psum_tensor("psC_g2", [128, 256], F32) as pg2, \
                 nc.psum_tensor("psC_g3", [128, 256], F32) as pg3:
                pgs = (pg0, pg1, pg2, pg3)
                pguse = [0, 0, 0, 0]
                gi = 0
                for t_ in range(B):
                    for (gn, wn, un, bn, af) in gnames:
                        for mc in range(NCH):
                            bi = gi % 4
                            gi += 1
                            pg = pgs[bi]
                            @blk.tensor
                            def _(t, t_=t_, wn=wn, un=un, mc=mc, pg=pg, bi=bi):
                                P.wv(t, "act", pguse[bi])
                                if mc == 0 and gn == "f":
                                    P.w(t, "dve", "pool", "ld")
                                for ec in range(NCH):
                                    t.matmul(pg[:], wres[wn][:, ec, mc * 128:(mc + 1) * 128],
                                             z[:, ec, t_ * 256:(t_ + 1) * 256],
                                             start=(ec == 0), stop=False)
                                for ec in range(NCH):
                                    i = t.matmul(pg[:], wres[un][:, ec, mc * 128:(mc + 1) * 128],
                                                 cbf[:, ec, :],
                                                 start=False, stop=(ec == NCH - 1))
                                P.inc(i, "pe")
                            @blk.scalar
                            def _(sc, gn=gn, bn=bn, af=af, mc=mc, pg=pg, bi=bi):
                                P.w(sc, "pe")
                                sc.activation(gates[gn][:, mc, :], pg[:], af,
                                              bias=bv[bn][:, mc:mc + 1])
                                pguse[bi] = P.cnt["act"]
                    for mc in range(NCH):
                        dec = blk.vector if mc % 2 == 0 else blk.gpsimd
                        nm = "dve" if mc % 2 == 0 else "pool"
                        @dec
                        def _(e, mc=mc, nm=nm):
                            P.w(e, "act")
                            scratch = tmp_c[:, :] if nm == "dve" else ysl[:, 0, :]
                            e.tensor_mul(cst[:, mc, :], cst[:, mc, :], gates["f"][:, mc, :])
                            e.tensor_mul(scratch, gates["i"][:, mc, :], gates["c"][:, mc, :])
                            e.tensor_add(cst[:, mc, :], cst[:, mc, :], scratch)
                    @blk.vector
                    def _(v, t_=t_):
                        P.w(v, "dve", "pool")
                        v.tensor_copy(cbf[:], cst[:])
            wcuse = [0, 0]
            for mc in range(NCH):
                @blk.sync
                def _(s, mc=mc):
                    P.wv(s, "pe", wcuse[mc % 2])
                    if mc == 0:
                        P.w(s, "dve")
                    P.inc(s.dma_start(out=wt_c[:, mc % 2],
                                      in_=wffnbf_r[:, :, 7 * 1024 + mc * 128:
                                                   7 * 1024 + (mc + 1) * 128]), "ld", 16)
                with nc.psum_tensor(f"psC_uo_{mc}", [128, 256], F32) as pu:
                    @blk.tensor
                    def _(t, mc=mc, pu=pu):
                        P.w(t, "ld", "act", "dve", "pool")
                        for ec in range(NCH):
                            i = t.matmul(pu[:], wt_c[:, mc % 2, ec, :], cbf[:, ec, :],
                                         start=(ec == 0), stop=(ec == NCH - 1))
                        P.inc(i, "pe")
                    wcuse[mc % 2] = P.cnt["pe"]
                    @blk.vector
                    def _(v, mc=mc, pu=pu):
                        P.w(v, "pe")
                        v.tensor_copy(cuo[:, mc, :], pu[:])
            rctx.close()
            with ExitStack() as octx:
                wo_res = octx.enter_context(nc.sbuf_tensor("wo_res", [128, NCH, D], BF16))
                ystage = octx.enter_context(nc.sbuf_tensor("ystage", [128, NCH, 256], F32))
                yo_r = d_yo.rearrange("(c p) n -> p c n", p=128)
                @blk.sync
                def _(s):
                    P.w(s, "pe")
                    P.inc(s.dma_start(out=wo_res[:],
                                      in_=wffnbf_r[:, :, 6 * 1024:7 * 1024]), "ld", 16)
                for bt in range(B):
                    @blk.sync
                    def _(s, bt=bt):
                        P.w(s, "dve", "st")
                        P.inc(s.dma_start(out=ysl[:, :, :],
                                          in_=ypark_r[:, :, bt * 256:(bt + 1) * 256]),
                              "ld", 16)
                    for mc in range(NCH):
                        with nc.psum_tensor(f"psC_o_{bt}_{mc}", [128, 256], F32) as po:
                            @blk.tensor
                            def _(t, mc=mc, bt=bt, po=po):
                                P.w(t, "ld")
                                P.w(t, "dve")
                                for ec in range(NCH):
                                    i = t.matmul(po[:], wo_res[:, ec, mc * 128:(mc + 1) * 128],
                                                 z[:, ec, bt * 256:(bt + 1) * 256],
                                                 start=(ec == 0), stop=(ec == NCH - 1))
                                P.inc(i, "pe")
                            @blk.vector
                            def _(v, mc=mc, po=po):
                                P.w(v, "pe", "act")
                                P.inc(v.tensor_add(tmp_c[:], po[:], cuo[:, mc, :]), "dve")
                            @blk.scalar
                            def _(sc, mc=mc):
                                P.w(sc, "dve")
                                P.inc(sc.activation(tmp_c[:], tmp_c[:], AF.Tanh,
                                                    bias=bv["bov"][:, mc:mc + 1]), "act")
                            @blk.vector
                            def _(v, mc=mc):
                                P.w(v, "act", "ld")
                                P.inc(v.tensor_add(ystage[:, mc, :], tmp_c[:], ysl[:, mc, :]),
                                      "dve")
                    @blk.sync
                    def _(s, bt=bt):
                        P.w(s, "dve")
                        P.inc(s.dma_start(out=yo_r[:, :, bt * 256:(bt + 1) * 256],
                                          in_=ystage[:, :, :]), "st", 16)
        @blk.sync
        def _(s):
            P.wall(s)
    return nc


_NC_CACHE = {}


def kernel(L0_attn_query, L0_attn_key, L0_attn_value, L0_attn_in_proj_weight,
           L0_attn_out_proj_weight, L0_ffn_wf, L0_ffn_uf, L0_ffn_bf, L0_ffn_wi,
           L0_ffn_ui, L0_ffn_bi, L0_ffn_wc, L0_ffn_uc, L0_ffn_bc, L0_ffn_wo,
           L0_ffn_uo, L0_ffn_bo, L0_ffn_in_c):
    x = np.asarray(L0_attn_query, np.float32)
    xf = np.ascontiguousarray(x.transpose(2, 0, 1).reshape(D, B * T))
    wqkv = np.ascontiguousarray(np.asarray(L0_attn_in_proj_weight, np.float32).T)
    wout = np.ascontiguousarray(np.asarray(L0_attn_out_proj_weight, np.float32).T)
    wmats = {k: np.ascontiguousarray(np.asarray(v, np.float32).T) for k, v in
             dict(wf=L0_ffn_wf, wi=L0_ffn_wi, wc=L0_ffn_wc, uf=L0_ffn_uf,
                  ui=L0_ffn_ui, uc=L0_ffn_uc, wo=L0_ffn_wo, uo=L0_ffn_uo).items()}
    bvecs = {k: np.ascontiguousarray(np.asarray(v, np.float32).reshape(NCH, 128).T)
             for k, v in dict(bfv=L0_ffn_bf, biv=L0_ffn_bi, bcv=L0_ffn_bc,
                              bov=L0_ffn_bo, incv=L0_ffn_in_c).items()}
    jj = np.arange(128)
    trimask = np.where(jj[:, None] > jj[None, :], np.float32(NEG), np.float32(0.0))
    trimask = np.ascontiguousarray(trimask.astype(np.float32))

    in_maps = []
    for core in range(8):
        blks = (core, 8 + core)
        xo = np.empty((D, OWN), np.float32)
        for b in range(B):
            for s_, blkid in enumerate(blks):
                xo[:, b * 256 + s_ * 128: b * 256 + (s_ + 1) * 128] = \
                    x[b, blkid * 128:(blkid + 1) * 128, :].T
        deny = np.zeros((128, 32), np.float32)
        for s_, blkid in enumerate(blks):
            for k in range(16):
                if k >= blkid:
                    deny[:, s_ * 16 + k] = NEG
        m = dict(xf=xf, xo=np.ascontiguousarray(xo), wqkv=wqkv, wout=wout,
                 deny=deny, trimask=trimask, iden=np.eye(128, dtype=np.float32))
        m.update(wmats)
        m.update(bvecs)
        in_maps.append(m)

    dbg = bool(_NC_CACHE.get("debug"))
    key = ("nc", dbg)
    if key not in _NC_CACHE:
        _NC_CACHE[key] = build_nc(debug=dbg)
    nc = _NC_CACHE[key]
    res = run_bass_kernel_spmd(nc, in_maps, core_ids=list(range(8)))

    if dbg:
        _NC_CACHE["results"] = res.results
    out = np.empty((B, T, D), np.float32)
    for core in range(8):
        yo = res.results[core]["yo"]
        for b in range(B):
            for s_, blkid in enumerate((core, 8 + core)):
                out[b, blkid * 128:(blkid + 1) * 128, :] = \
                    yo[:, b * 256 + s_ * 128: b * 256 + (s_ + 1) * 128].T
    return out


# version 10
# speedup vs baseline: 1.1745x; 1.0094x over previous
import numpy as np
from contextlib import ExitStack
import concourse.bass as bass
import concourse.mybir as mybir
from concourse.bass_utils import run_bass_kernel_spmd

F32 = mybir.dt.float32
BF16 = mybir.dt.bfloat16
AF = mybir.ActivationFunctionType
ALU = mybir.AluOpType
AX = mybir.AxisListType

D, NCH, B, T, H = 1024, 8, 8, 2048, 16
OWN = 2048
NEG = -1.0e30
EPS = 1e-5


class P_:
    def __init__(self, nc, ctx):
        self.sem = {}
        self.cnt = {}
        for n in ("ld", "gp", "pe", "act", "dve", "pool", "st"):
            self.sem[n] = ctx.enter_context(nc.semaphore(n))
            self.cnt[n] = 0
        self.tagged = set()
        self._keep = []

    def inc(self, instr, name, amt=None):
        if id(instr) in self.tagged:
            return
        amt = amt if amt is not None else 1
        instr.then_inc(self.sem[name], amt)
        self.cnt[name] += amt

    def wv(self, eng, name, val):
        if val > 0:
            eng.wait_ge(self.sem[name], val)

    def w(self, eng, *names):
        for n in names:
            if self.cnt[n] > 0:
                eng.wait_ge(self.sem[n], self.cnt[n])

    def wall(self, eng):
        self.w(eng, "ld", "gp", "pe", "act", "dve", "pool", "st")


_PASSTHRU = ("wait_ge", "sem_inc", "nop", "dma_start", "dma_start_transpose",
             "indirect_dma_start", "trigger_dma", "collective_compute")


class SeqProxy:

    def __init__(self, eng, P, nm):
        self._eng, self._P, self._nm = eng, P, nm

    def __getattr__(self, name):
        f = getattr(self._eng, name)
        if name.startswith("_") or name in _PASSTHRU or not callable(f):
            return f

        def wrap(*args, **kw):
            P, nm = self._P, self._nm
            if P.cnt[nm] > 0:
                self._eng.wait_ge(P.sem[nm], P.cnt[nm])
            instr = f(*args, **kw)
            instr.then_inc(P.sem[nm], 1)
            P.cnt[nm] += 1
            P.tagged.add(id(instr))
            P._keep.append(instr)
            return instr
        return wrap


class BlkWrap:
    def __init__(self, blk, P):
        self._blk, self._P = blk, P

    @property
    def sync(self):
        return self._blk.sync

    @property
    def tensor(self):
        return self._blk.tensor

    @property
    def vector(self):
        return lambda fn: self._blk.vector(lambda e: fn(SeqProxy(e, self._P, "dve")))

    @property
    def scalar(self):
        return lambda fn: self._blk.scalar(lambda e: fn(SeqProxy(e, self._P, "act")))

    @property
    def gpsimd(self):
        return lambda fn: self._blk.gpsimd(lambda e: fn(SeqProxy(e, self._P, "pool")))


def build_nc(debug=False):
    nc = bass.Bass("TRN2", target_bir_lowering=False)

    d_xf = nc.dram_tensor("xf", [D, B * T], F32, kind="ExternalInput").ap()
    d_xo = nc.dram_tensor("xo", [D, OWN], F32, kind="ExternalInput").ap()
    d_wqkv = nc.dram_tensor("wqkv", [D, 3 * D], F32, kind="ExternalInput").ap()
    d_wout = nc.dram_tensor("wout", [D, D], F32, kind="ExternalInput").ap()
    d_w = {w: nc.dram_tensor(w, [D, D], F32, kind="ExternalInput").ap()
           for w in ("wf", "wi", "wc", "uf", "ui", "uc", "wo", "uo")}
    d_bv = {v: nc.dram_tensor(v, [128, NCH], F32, kind="ExternalInput").ap()
            for v in ("bfv", "biv", "bcv", "bov", "incv")}
    d_deny = nc.dram_tensor("deny", [128, 32], F32, kind="ExternalInput").ap()
    d_tri = nc.dram_tensor("trimask", [128, 128], F32, kind="ExternalInput").ap()
    d_iden = nc.dram_tensor("iden", [128, 128], F32, kind="ExternalInput").ap()
    d_yo = nc.dram_tensor("yo", [D, OWN], F32, kind="ExternalOutput").ap()

    dkind = "ExternalOutput" if debug else "Internal"
    d_wqkv_bf = nc.dram_tensor("wqkv_bf", [D, 3 * D], BF16)
    d_wout_bf = nc.dram_tensor("wout_bf", [D, D], BF16)
    d_wffn_bf = nc.dram_tensor("wffn_bf", [D, 8 * D], BF16)
    d_qpark = nc.dram_tensor("q_park", [D, OWN], BF16, kind=dkind)
    d_ypark = nc.dram_tensor("y_park", [D, OWN], F32, kind=dkind)
    if debug:
        d_txh = nc.dram_tensor("t_xhat", [128, NCH * OWN], BF16, kind="ExternalOutput").ap()
        d_tkb = nc.dram_tensor("t_kb", [128, NCH * T], BF16, kind="ExternalOutput").ap()
        d_ton = nc.dram_tensor("t_onn", [128, NCH * 256], BF16, kind="ExternalOutput").ap()
        d_tor = nc.dram_tensor("t_onraw", [128, NCH * 256], BF16, kind="ExternalOutput").ap()
        d_trc = nc.dram_tensor("t_rec", [2, 2048], F32, kind="ExternalOutput").ap()
        d_tfa = nc.dram_tensor("t_fac", [128, NCH * 256], F32, kind="ExternalOutput").ap()
        d_tyt = nc.dram_tensor("t_yt", [128, 256], BF16, kind="ExternalOutput").ap()
        d_txr = nc.dram_tensor("t_xr", [128, 256], F32, kind="ExternalOutput").ap()
        d_tpy = nc.dram_tensor("t_py", [128, 256], BF16, kind="ExternalOutput").ap()
        d_twt = nc.dram_tensor("t_wt", [128, 128], F32, kind="ExternalOutput").ap()
        d_tnb = nc.dram_tensor("t_nb", [128, 1], F32, kind="ExternalOutput").ap()
        d_tz = nc.dram_tensor("t_z", [128, NCH * OWN], BF16, kind="ExternalOutput").ap()
    d_rowb = nc.dram_tensor("row_bounce", [1, T], F32)
    d_recb = nc.dram_tensor("rec_bounce", [2, 2048], F32)

    xf_r = d_xf.rearrange("(c p) n -> p c n", p=128)
    xo_r = d_xo.rearrange("(c p) n -> p c n", p=128)
    wqkv_r = d_wqkv.rearrange("(c p) f -> p c f", p=128)
    wout_r = d_wout.rearrange("(c p) f -> p c f", p=128)
    w_r = {k: v.rearrange("(c p) f -> p c f", p=128) for k, v in d_w.items()}
    qpark_r = d_qpark.ap().rearrange("(c p) n -> p c n", p=128)
    wqkvbf_r = d_wqkv_bf.ap().rearrange("(c p) f -> p c f", p=128)
    woutbf_r = d_wout_bf.ap().rearrange("(c p) f -> p c f", p=128)
    wffnbf_r = d_wffn_bf.ap().rearrange("(c p) f -> p c f", p=128)
    FFN_ORDER = ("wf", "wi", "wc", "uf", "ui", "uc", "wo", "uo")

    with ExitStack() as ctx:
        sb = lambda n, sh, dt: ctx.enter_context(nc.sbuf_tensor(n, sh, dt))
        tri = sb("tri", [128, 128], F32)
        iden = sb("iden_s", [128, 128], F32)
        deny = sb("deny_s", [128, 32], F32)
        ones_bf = sb("ones_bf", [128, 1], BF16)
        ones_f1 = sb("ones_f1", [1, 128], F32)
        idy = sb("idy", [1, 1], F32)
        bv = {v: sb(v + "_s", [128, NCH], F32) for v in d_bv}
        ro_c = sb("ro_c", [128, 16], F32)
        lo_c = sb("lo_c", [128, 16], F32)
        co_c = sb("co_c", [128, 16], F32)
        qmax2 = sb("qmax2", [1, 1], F32)
        epsc = sb("epsc", [1, 1], F32)
        rw1 = sb("rw1", [1, T], F32)
        rw2 = sb("rw2", [1, T], F32)

        P = P_(nc, ctx)
        blk = BlkWrap(ctx.enter_context(nc.Block()), P)

        @blk.sync
        def _(s):
            P.inc(s.dma_start(out=tri[:], in_=d_tri), "ld", 16)
            P.inc(s.dma_start(out=iden[:], in_=d_iden), "ld", 16)
            P.inc(s.dma_start(out=deny[:], in_=d_deny), "ld", 16)
            for v in bv:
                P.inc(s.dma_start(out=bv[v][:], in_=d_bv[v]), "ld", 16)

        @blk.vector
        def _(v):
            v.memset(ones_bf[:], 1.0)
            v.memset(ones_f1[:], 1.0)
            v.memset(idy[:], 1.0)
            v.memset(epsc[:], EPS)
            P.inc(v.memset(qmax2[:], 0.0), "dve")

        with ExitStack() as abctx:
            xa_bf = abctx.enter_context(nc.sbuf_tensor("xa_bf", [128, NCH, OWN], BF16))
            mu_own_bf = abctx.enter_context(nc.sbuf_tensor("mu_own_bf", [1, OWN], BF16))

            with ExitStack() as actx:
                asb = lambda n, sh, dt: actx.enter_context(nc.sbuf_tensor(n, sh, dt))
                x2sl = asb("x2sl_a", [128, NCH, 512], BF16)
                wq = asb("wq_a", [128, NCH, 1024], BF16)
                nwq = asb("nwq_a", [1, 1024], BF16)
                qev = asb("qev_a", [128, 512], BF16)
                q2 = asb("q2_a", [128, 512], BF16)
                rbc = asb("rbc_a", [128, 512], F32)
                qscr = asb("qscr_a", [1, 4], F32)

                wstage = asb("wstage_a", [128, NCH, 1024], BF16)

                @blk.gpsimd
                def _(g):
                    P.inc(g.dma_start(out=xa_bf[:], in_=xo_r), "gp", 16)
                    g.wait_ge(P.sem["gp"], P.cnt["gp"])
                    P.inc(g.dma_start(out=wq[:], in_=wqkv_r[:, :, 0:1024]), "gp", 16)
                for seg in range(4):
                    srcv = (wqkv_r[:, :, seg * 1024:(seg + 1) * 1024] if seg < 3
                            else wout_r[:, :, 0:1024])
                    dstv = (d_wqkv_bf.ap().rearrange("(c p) f -> p c f", p=128)[
                                :, :, seg * 1024:(seg + 1) * 1024] if seg < 3
                            else woutbf_r[:, :, 0:1024])
                    @blk.gpsimd
                    def _(g, srcv=srcv, seg=seg):
                        P.w(g, "st")
                        P.inc(g.dma_start(out=wstage[:], in_=srcv), "gp", 16)
                    @blk.sync
                    def _(s, dstv=dstv):
                        P.w(s, "gp")
                        P.inc(s.dma_start(out=dstv, in_=wstage[:]), "st", 16)

                with nc.psum_tensor("psA_su", [1, 2048], F32) as psu, \
                     nc.psum_tensor("psA_sq", [1, 2048], F32) as psq:
                    for sl in range(4):
                        @blk.vector
                        def _(v, sl=sl):
                            P.w(v, "gp", "pe")
                            v.tensor_mul(x2sl[:], xa_bf[:, :, sl * 512:(sl + 1) * 512],
                                         xa_bf[:, :, sl * 512:(sl + 1) * 512])
                        @blk.tensor
                        def _(t, sl=sl):
                            P.w(t, "dve", "gp")
                            for c in range(NCH):
                                t.matmul(psu[:, sl * 512:(sl + 1) * 512], ones_bf[:],
                                         xa_bf[:, c, sl * 512:(sl + 1) * 512],
                                         start=(c == 0), stop=(c == NCH - 1))
                                i = t.matmul(psq[:, sl * 512:(sl + 1) * 512], ones_bf[:],
                                             x2sl[:, c, :], start=(c == 0), stop=(c == NCH - 1))
                            P.inc(i, "pe")
                    @blk.scalar
                    def _(sc):
                        P.w(sc, "pe")
                        sc.mul(rw1[:], psu[:], 1.0 / D)
                        sc.mul(rw2[:], psq[:], 1.0 / D)
                        sc.copy(mu_own_bf[:], rw1[:])
                @blk.vector
                def _(v):
                    P.w(v, "act")
                    for sl in range(4):
                        v.tensor_mul(rbc[0:1, :], rw1[:, sl * 512:(sl + 1) * 512],
                                     rw1[:, sl * 512:(sl + 1) * 512])
                        v.tensor_sub(rw2[:, sl * 512:(sl + 1) * 512],
                                     rw2[:, sl * 512:(sl + 1) * 512], rbc[0:1, 0:512])
                @blk.scalar
                def _(sc):
                    P.w(sc, "dve")
                    sc.activation(rw2[:], rw2[:], AF.Ln, bias=epsc[:])
                    sc.activation(rw2[:], rw2[:], AF.Exp, scale=-0.5)
                with nc.psum_tensor("psA_tr", [128, 16], F32) as ptr:
                    @blk.tensor
                    def _(t):
                        P.w(t, "act")
                        for jt in range(16):
                            i = t.transpose(ptr[:, jt:jt + 1], rw2[:, jt * 128:(jt + 1) * 128],
                                            idy[:])
                        P.inc(i, "pe")
                    @blk.vector
                    def _(v):
                        P.w(v, "pe")
                        v.tensor_copy(ro_c[:], ptr[:])
                        v.reciprocal(co_c[:], ptr[:])
                @blk.scalar
                def _(sc):
                    P.w(sc, "dve")
                    sc.activation(lo_c[:], ro_c[:], AF.Ln)
                    sc.mul(rw1[:], rw2[:], 0.125)
                @blk.gpsimd
                def _(g):
                    P.w(g, "act")
                    P.inc(g.dma_start(out=d_rowb.ap(), in_=rw1[:]), "gp", 16)
                with nc.psum_tensor("psA_ws", [1, 128], F32) as pws:
                    for fc in range(8):
                        @blk.tensor
                        def _(t, fc=fc):
                            P.w(t, "gp")
                            if fc:
                                P.w(t, "act")
                            for ec in range(NCH):
                                i = t.matmul(pws[:], ones_bf[:],
                                             wq[:, ec, fc * 128:(fc + 1) * 128],
                                             start=(ec == 0), stop=(ec == NCH - 1))
                            P.inc(i, "pe")
                        @blk.scalar
                        def _(sc, fc=fc):
                            P.w(sc, "pe")
                            sc.mul(nwq[:, fc * 128:(fc + 1) * 128], pws[:], -1.0)
                with nc.psum_tensor("psA_q", [128, 512], F32) as pq, \
                     nc.psum_tensor("psA_n0", [1, 512], F32) as pn0, \
                     nc.psum_tensor("psA_n1", [1, 512], F32) as pn1:
                    for sl in range(4):
                        @blk.gpsimd
                        def _(g, sl=sl):
                            P.w(g, "gp", "dve")
                            P.inc(g.dma_start(
                                out=rbc[:],
                                in_=d_rowb.ap().partition_broadcast(128)[:, 0, sl * 512:(sl + 1) * 512]),
                                "gp", 16)
                        for fc in range(8):
                            @blk.tensor
                            def _(t, fc=fc, sl=sl):
                                P.w(t, "gp", "act", "dve")
                                for c in range(NCH):
                                    t.matmul(pq[:], wq[:, c, fc * 128:(fc + 1) * 128],
                                             xa_bf[:, c, sl * 512:(sl + 1) * 512],
                                             start=(c == 0), stop=False)
                                i = t.matmul(pq[:], nwq[:, fc * 128:(fc + 1) * 128],
                                             mu_own_bf[:, sl * 512:(sl + 1) * 512],
                                             start=False, stop=True)
                                P.inc(i, "pe")
                            @blk.vector
                            def _(v, fc=fc, sl=sl):
                                P.w(v, "pe", "st")
                                v.tensor_mul(qev[:], pq[:], rbc[:])
                                v.tensor_mul(q2[:], qev[:], qev[:])
                            @blk.sync
                            def _(s, fc=fc, sl=sl):
                                P.w(s, "dve")
                                P.inc(s.dma_start(
                                    out=d_qpark.ap()[fc * 128:(fc + 1) * 128,
                                                     sl * 512:(sl + 1) * 512],
                                    in_=qev[:]), "st", 16)
                            @blk.tensor
                            def _(t, fc=fc):
                                P.w(t, "dve")
                                t.matmul(pn0[:], ones_bf[0:64, :], q2[0:64, :],
                                         start=True, stop=True)
                                P.inc(t.matmul(pn1[:], ones_bf[64:128, :], q2[64:128, :],
                                               start=True, stop=True), "pe")
                            @blk.vector
                            def _(v):
                                P.w(v, "pe", "gp")
                                v.tensor_reduce(qscr[:, 0:1], pn0[:], axis=AX.X, op=ALU.max)
                                v.tensor_reduce(qscr[:, 1:2], pn1[:], axis=AX.X, op=ALU.max)
                                v.tensor_max(qscr[:, 0:1], qscr[:, 0:1], qscr[:, 1:2])
                                v.tensor_max(qmax2[:], qmax2[:], qscr[:, 0:1])
            for dec in (blk.sync, blk.gpsimd, blk.tensor, blk.scalar, blk.vector):
                @dec
                def _(e):
                    P.wall(e)
            @blk.scalar
            def _(sc):
                P.w(sc, "dve")
                sc.activation(qmax2[:], qmax2[:], AF.Ln)
                P.inc(sc.activation(qmax2[:], qmax2[:], AF.Exp, scale=0.5), "act")
            if debug:
                @blk.sync
                def _(s):
                    P.inc(s.dma_start(out=d_txh, in_=xa_bf[:].rearrange("p c n -> p (c n)")),
                          "st", 16)
                    s.wait_ge(P.sem["st"], P.cnt["st"])

            with ExitStack() as bctx:
                bsb = lambda n, sh, dt: bctx.enter_context(nc.sbuf_tensor(n, sh, dt))
                xb_bf = bsb("xb_bf", [128, NCH, T], BF16)
                kb = bsb("kb", [128, NCH, T], BF16)
                vb = bsb("vb", [128, 16, 1040], BF16)
                kob = bsb("kob", [128, NCH, 256], BF16)
                vob = bsb("vob", [128, 2, 1040], BF16)
                nwv = bsb("nwv", [1, 1024], BF16)
                nwk = bsb("nwk", [1, 128], BF16)
                mu_b_bf = bsb("mu_b_bf", [1, T], BF16)
                wt = bsb("wt", [128, 2, NCH, 128], BF16)
                qb = bsb("qb", [128, NCH, 256], BF16)
                scr8 = bsb("scr8", [128, NCH, 256], F32)
                yt8 = bsb("yt8", [128, 2, 256], F32)
                r_cb = bsb("r_cb", [128, 16], F32)
                l_cb = bsb("l_cb", [128, 16], F32)
                c_cb = bsb("c_cb", [128, 16], F32)
                btab = bsb("btab", [128, 32], F32)
                btabd = bsb("btabd", [128, 2], F32)
                negb = bsb("negb", [128, 1], F32)
                bscr = bsb("bscr", [1, 8], F32)


                wstage_b = bsb("wstage_b", [128, NCH, 512], BF16)
                wtuse = [0, 0]
                for b in range(B):
                    bscope = ExitStack()
                    x2b = bscope.enter_context(nc.sbuf_tensor(f"x2b_{b}", [128, NCH, 128], BF16))
                    maxkf = bscope.enter_context(
                        nc.sbuf_tensor(f"maxkf_{b}", [128, 8], F32))
                    wvres = bscope.enter_context(
                        nc.sbuf_tensor(f"wvres_{b}", [128, NCH, 512], BF16))
                    if b == 0:
                        @blk.gpsimd
                        def _(g):
                            P.w(g, "pe", "dve")
                            P.inc(g.dma_start(out=xb_bf[:],
                                              in_=xf_r[:, :, 0:T]), "gp", 16)
                    with nc.psum_tensor(f"psB_su{b}", [1, 2048], F32) as psu, \
                         nc.psum_tensor(f"psB_sq{b}", [1, 2048], F32) as psq:
                        for sl in range(16):
                            @blk.vector
                            def _(v, sl=sl):
                                P.w(v, "gp", "pe")
                                v.tensor_mul(x2b[:], xb_bf[:, :, sl * 128:(sl + 1) * 128],
                                             xb_bf[:, :, sl * 128:(sl + 1) * 128])
                            @blk.tensor
                            def _(t, sl=sl):
                                P.w(t, "dve", "gp")
                                if sl == 0:
                                    P.w(t, "act")
                                for c in range(NCH):
                                    t.matmul(psu[:, sl * 128:(sl + 1) * 128], ones_bf[:],
                                             xb_bf[:, c, sl * 128:(sl + 1) * 128],
                                             start=(c == 0), stop=(c == NCH - 1))
                                    i = t.matmul(psq[:, sl * 128:(sl + 1) * 128], ones_bf[:],
                                                 x2b[:, c, :], start=(c == 0), stop=(c == NCH - 1))
                                P.inc(i, "pe")
                        @blk.scalar
                        def _(sc):
                            P.w(sc, "pe")
                            sc.mul(rw1[:], psu[:], 1.0 / D)
                            sc.mul(rw2[:], psq[:], 1.0 / D)
                            sc.copy(mu_b_bf[:], rw1[:])
                    @blk.vector
                    def _(v):
                        P.w(v, "act")

                        s8f = scr8[:].rearrange("p c n -> p (c n)")
                        for sl in range(4):
                            v.tensor_mul(s8f[0:1, 0:512],
                                         rw1[:, sl * 512:(sl + 1) * 512],
                                         rw1[:, sl * 512:(sl + 1) * 512])
                            v.tensor_sub(rw2[:, sl * 512:(sl + 1) * 512],
                                         rw2[:, sl * 512:(sl + 1) * 512], s8f[0:1, 0:512])
                    @blk.scalar
                    def _(sc):
                        P.w(sc, "dve")
                        sc.activation(rw1[:], rw2[:], AF.Ln, bias=epsc[:])
                        sc.activation(rw2[:], rw1[:], AF.Exp, scale=-0.5)
                    with nc.psum_tensor(f"psB_tr{b}", [128, 16], F32) as ptrb:
                        @blk.tensor
                        def _(t):
                            P.w(t, "act")
                            for jt in range(16):
                                i = t.transpose(ptrb[:, jt:jt + 1],
                                                rw2[:, jt * 128:(jt + 1) * 128], idy[:])
                            P.inc(i, "pe")
                        @blk.vector
                        def _(v):
                            P.w(v, "pe")
                            v.tensor_copy(r_cb[:], ptrb[:])
                            v.reciprocal(c_cb[:], ptrb[:])
                    @blk.scalar
                    def _(sc):
                        P.w(sc, "dve")
                        sc.activation(l_cb[:], r_cb[:], AF.Ln)
                    @blk.vector
                    def _(v):
                        P.w(v, "act")
                        v.tensor_reduce(bscr[:, 1:2], rw2[:], axis=AX.X, op=ALU.max)
                        v.tensor_reduce(bscr[:, 2:3], rw1[:], axis=AX.X, op=ALU.min)
                        v.tensor_scalar_mul(bscr[:, 2:3], bscr[:, 2:3], -0.5)
                    for fi in range(8):
                        @blk.sync
                        def _(s, fi=fi):
                            P.wv(s, "pe", wtuse[fi % 2])
                            P.inc(s.dma_start(out=wt[:, fi % 2],
                                              in_=wqkvbf_r[:, :, 1024 + fi * 128:1024 + (fi + 1) * 128]),
                                  "ld", 16)
                        with nc.psum_tensor(f"psB_wk{b}_{fi}", [1, 128], F32) as pws2:
                            @blk.tensor
                            def _(t, fi=fi, pws2=pws2):
                                P.w(t, "ld", "act")
                                for ec in range(NCH):
                                    i = t.matmul(pws2[:], ones_bf[:], wt[:, fi % 2, ec, :],
                                                 start=(ec == 0), stop=(ec == NCH - 1))
                                P.inc(i, "pe")
                            @blk.scalar
                            def _(sc, pws2=pws2):
                                P.w(sc, "pe")
                                sc.mul(nwk[:], pws2[:], -1.0)
                        with nc.psum_tensor(f"psB_k{b}_{fi}", [128, 512], F32) as pk:
                            for tt in range(4):
                                @blk.tensor
                                def _(t, fi=fi, tt=tt, pk=pk, b=b):
                                    P.w(t, "act", "dve")
                                    for c in range(NCH):
                                        t.matmul(pk[:], wt[:, fi % 2, c, :],
                                                 xb_bf[:, c, tt * 512:(tt + 1) * 512],
                                                 start=(c == 0), stop=False)
                                    i = t.matmul(pk[:], nwk[:],
                                                 mu_b_bf[:, tt * 512:(tt + 1) * 512],
                                                 start=False, stop=True)
                                    P.inc(i, "pe")
                                @blk.vector
                                def _(v, fi=fi, tt=tt, pk=pk):
                                    P.w(v, "pe")
                                    v.tensor_copy(kb[:, fi, tt * 512:(tt + 1) * 512], pk[:])
                                    if tt == 3:
                                        v.tensor_reduce(maxkf[:, fi:fi + 1], kb[:, fi, :],
                                                        axis=AX.X, op=ALU.max,
                                                        apply_absolute_value=True)
                            with nc.psum_tensor(f"psB_ko{b}_{fi}", [128, 256], F32) as pko:
                                @blk.tensor
                                def _(t, fi=fi, b=b, pko=pko):
                                    P.w(t, "dve")
                                    for c in range(NCH):
                                        t.matmul(pko[:], wt[:, fi % 2, c, :],
                                                 xa_bf[:, c, b * 256:(b + 1) * 256],
                                                 start=(c == 0), stop=False)
                                    i = t.matmul(pko[:], nwk[:],
                                                 mu_own_bf[:, b * 256:(b + 1) * 256],
                                                 start=False, stop=True)
                                    P.inc(i, "pe")
                                @blk.vector
                                def _(v, fi=fi, pko=pko):
                                    P.w(v, "pe")
                                    v.tensor_copy(kob[:, fi, :], pko[:])
                            wtuse[fi % 2] = P.cnt["pe"]
                    with nc.psum_tensor(f"psB_v0_{b}", [128, 512], F32) as pv0, \
                         nc.psum_tensor(f"psB_v1_{b}", [128, 512], F32) as pv1, \
                         nc.psum_tensor(f"psB_wsv{b}", [1, 512], F32) as pwsv:
                        pvs = (pv0, pv1)
                        for ft in range(2):
                            @blk.sync
                            def _(s, ft=ft):
                                P.w(s, "pe")
                                P.inc(s.dma_start(out=wvres[:],
                                                  in_=wqkvbf_r[:, :, 2048 + ft * 512:
                                                               2048 + (ft + 1) * 512]), "ld", 16)
                            @blk.tensor
                            def _(t, ft=ft):
                                P.w(t, "ld", "act")
                                for ec in range(NCH):
                                    i = t.matmul(pwsv[:], ones_bf[:], wvres[:, ec, :],
                                                 start=(ec == 0), stop=(ec == NCH - 1))
                                P.inc(i, "pe")
                            @blk.scalar
                            def _(sc, ft=ft):
                                P.w(sc, "pe")
                                sc.mul(nwv[:, 0:512], pwsv[:], -1.0)
                            for jt in range(16):
                                pv = pvs[jt % 2]
                                @blk.tensor
                                def _(t, jt=jt, ft=ft, pv=pv, b=b):
                                    P.w(t, "dve", "act")
                                    for ec in range(NCH):
                                        t.matmul(pv[:],
                                                 xb_bf[:, ec, jt * 128:(jt + 1) * 128],
                                                 wvres[:, ec, :],
                                                 start=(ec == 0), stop=False)
                                    i = t.matmul(pv[:],
                                                 mu_b_bf[:, jt * 128:(jt + 1) * 128],
                                                 nwv[:, 0:512],
                                                 start=False, stop=True)
                                    P.inc(i, "pe")
                                @blk.vector
                                def _(v, jt=jt, ft=ft, pv=pv):
                                    P.w(v, "pe")
                                    dst = vb[:].rearrange("p jt (h e) -> p jt h e", h=16)[
                                        :, jt, ft * 8:(ft + 1) * 8, 0:64]
                                    v.tensor_copy(dst, pv[:].rearrange("p (h e) -> p h e", h=8))
                            for sj in range(2):
                                pv = pvs[sj]
                                @blk.tensor
                                def _(t, sj=sj, ft=ft, b=b, pv=pv):
                                    P.w(t, "dve")
                                    for ec in range(NCH):
                                        t.matmul(pv[:],
                                                 xa_bf[:, ec, b * 256 + sj * 128:
                                                        b * 256 + (sj + 1) * 128],
                                                 wvres[:, ec, :],
                                                 start=(ec == 0), stop=False)
                                    i = t.matmul(pv[:],
                                                 mu_own_bf[:, b * 256 + sj * 128:
                                                           b * 256 + (sj + 1) * 128],
                                                 nwv[:, 0:512],
                                                 start=False, stop=True)
                                    P.inc(i, "pe")
                                @blk.vector
                                def _(v, sj=sj, ft=ft, pv=pv):
                                    P.w(v, "pe")
                                    dst = vob[:].rearrange("p jt (h e) -> p jt h e", h=16)[
                                        :, sj, ft * 8:(ft + 1) * 8, 0:64]
                                    v.tensor_copy(dst, pv[:].rearrange("p (h e) -> p h e", h=8))
                        for jt in range(16):
                            @blk.vector
                            def _(v, jt=jt):
                                dst = vb[:].rearrange("p jt (h e) -> p jt h e", h=16)[:, jt, :, 64:65]
                                src2 = bass.AP(c_cb, jt, [list(c_cb.ap().ap[0]), [0, 16], [1, 1]])
                                v.tensor_copy(dst, src2)
                        for sj in range(2):
                            @blk.vector
                            def _(v, sj=sj, b=b):
                                dst = vob[:].rearrange("p jt (h e) -> p jt h e", h=16)[:, sj, :, 64:65]
                                src2 = bass.AP(co_c, b * 2 + sj, [list(co_c.ap().ap[0]), [0, 16], [1, 1]])
                                v.tensor_copy(dst, src2)

                    @blk.vector
                    def _(v):
                        P.w(v, "act", "pe")
                        P.inc(v.tensor_reduce(
                            negb[:], maxkf[:],
                            axis=AX.X, op=ALU.max, apply_absolute_value=True), "dve")
                    with nc.psum_tensor(f"psB_km{b}", [1, 128], F32) as pkm, \
                         nc.psum_tensor(f"psB_nb{b}", [128, 1], F32) as pnb:
                        @blk.tensor
                        def _(t, pkm=pkm):
                            P.w(t, "dve")
                            P.inc(t.transpose(pkm[:], negb[:], iden[:]), "pe")
                        @blk.vector
                        def _(v, pkm=pkm):
                            P.w(v, "pe")
                            v.tensor_reduce(bscr[:, 0:1], pkm[:], axis=AX.X, op=ALU.max)
                            P.inc(v.tensor_mul(bscr[:, 0:1], bscr[:, 0:1], bscr[:, 1:2]), "dve")
                        @blk.vector
                        def _(v):
                            v.tensor_mul(bscr[:, 0:1], bscr[:, 0:1], qmax2[:])
                            v.tensor_scalar_mul(bscr[:, 0:1], bscr[:, 0:1], 8.0)
                            v.tensor_add(bscr[:, 0:1], bscr[:, 0:1], bscr[:, 2:3])
                            P.inc(v.tensor_scalar_mul(bscr[:, 0:1], bscr[:, 0:1], -1.0), "dve")
                        @blk.tensor
                        def _(t, pnb=pnb):
                            P.w(t, "dve")
                            P.inc(t.matmul(pnb[:], ones_f1[:], bscr[:, 0:1],
                                           start=True, stop=True), "pe")
                        @blk.vector
                        def _(v, pnb=pnb):
                            P.w(v, "pe")
                            P.inc(v.tensor_copy(negb[:], pnb[:]), "dve")
                    @blk.vector
                    def _(v, b=b):
                        P.w(v, "dve")
                        lview = bass.AP(l_cb, 0, [list(l_cb.ap().ap[0]), [0, 2], [1, 16]])
                        v.tensor_add(btab[:].rearrange("p (s k) -> p s k", s=2), lview,
                                     deny[:].rearrange("p (s k) -> p s k", s=2))
                        v.tensor_scalar_add(btab[:], btab[:], negb[:])
                        v.tensor_copy(btabd[:], lo_c[:, 2 * b:2 * b + 2])
                        P.inc(v.tensor_scalar_add(btabd[:], btabd[:], negb[:]), "dve")
                    @blk.sync
                    def _(s, b=b):
                        P.w(s, "st", "pe")
                        P.inc(s.dma_start(out=qb[:],
                                          in_=qpark_r[:, :, b * 256:(b + 1) * 256]), "ld", 16)
                    @blk.tensor
                    def _(t):
                        P.w(t, "ld", "dve", "act", "pool")
                    @blk.scalar
                    def _(sc):
                        P.w(sc, "dve", "pool")
                    bscope.close()
                    if b + 1 < B:
                        @blk.gpsimd
                        def _(g, b=b):
                            P.w(g, "pe", "dve")
                            P.inc(g.dma_start(out=xb_bf[:],
                                              in_=xf_r[:, :, (b + 1) * T:(b + 2) * T]),
                                  "gp", 16)

                    with ExitStack() as sctx:
                        ssb = lambda n, sh, dt: sctx.enter_context(nc.sbuf_tensor(n, sh, dt))
                        psb = [[ssb(f"psb{se}{h}_{b}", [128, 512], BF16) for h in range(2)]
                               for se in range(2)]
                        onraw = ssb(f"onraw_{b}", [128, NCH, 256], F32)
                        onn = ssb(f"onn_{b}", [128, NCH, 256], BF16)
                        rec0, rec1 = rw1, rw2
                        recs = (rec0, rec1)
                        with nc.psum_tensor(f"psB_sc00_{b}", [128, 512], F32) as sc00, \
                             nc.psum_tensor(f"psB_sc01_{b}", [128, 512], F32) as sc01, \
                             nc.psum_tensor(f"psB_sc10_{b}", [128, 512], F32) as sc10, \
                             nc.psum_tensor(f"psB_sc11_{b}", [128, 512], F32) as sc11, \
                             nc.psum_tensor(f"psB_av00_{b}", [128, 512], F32) as av00, \
                             nc.psum_tensor(f"psB_av01_{b}", [128, 512], F32) as av01, \
                             nc.psum_tensor(f"psB_av10_{b}", [128, 512], F32) as av10, \
                             nc.psum_tensor(f"psB_av11_{b}", [128, 512], F32) as av11:
                            sc = ((sc00, sc01), (sc10, sc11))
                            av = ((av00, av01), (av10, av11))
                            expcnt = [0, 0]
                            avevcnt = [(P.cnt["dve"], P.cnt["pool"]),
                                       (P.cnt["dve"], P.cnt["pool"])]
                            tricnt = [P.cnt["dve"]]
                            pending = [None]
                            slot = 0
                            for hpg in range(2):
                                hps = [4 * hpg + i_ for i_ in range(4)]
                                for s_ in range(2):
                                    g2 = slot % 2
                                    units = [(True, 0)] + [(False, k) for k in range(8 * (s_ + 1))]
                                    nu = len(units)
                                    pe_base = P.cnt["pe"]
                                    act_base = P.cnt["act"]
                                    dve_tri = P.cnt["dve"] + 2
                                    @blk.scalar
                                    def _(scl_, units=units, s_=s_, b=b, pe_base=pe_base,
                                          dve_tri=dve_tri):
                                        for u, (isdiag, k) in enumerate(units):
                                            p2, r2 = divmod(u, 2)
                                            P.wv(scl_, "pe", pe_base + 4 * p2 + 1 + r2)
                                            if isdiag:
                                                P.wv(scl_, "dve", dve_tri)
                                                scalev = ro_c[:, 2 * b + s_: 2 * b + s_ + 1]
                                                bia = btabd[:, s_:s_ + 1]
                                            else:
                                                scalev = r_cb[:, k:k + 1]
                                                bia = btab[:, s_ * 16 + k: s_ * 16 + k + 1]
                                            se = u % 2
                                            for h01 in range(2):
                                                scl_.activation(psb[se][h01][:],
                                                                sc[se][h01][:],
                                                                AF.Exp, bias=bia,
                                                                scale=scalev)
                                    assert P.cnt["act"] == act_base + 2 * nu
                                    ui = 0
                                    first_pair = True
                                    while ui < nu:
                                        pu = [(ui + j, units[ui + j][0], units[ui + j][1],
                                               (ui + j) % 2) for j in range(min(2, nu - ui))]
                                        for (uj, isdiag, k, se) in pu:
                                            @blk.tensor
                                            def _(t, s_=s_, isdiag=isdiag, k=k, uj=uj, se=se,
                                                  hps=hps):
                                                P.wv(t, "act", expcnt[se])
                                                if uj == 0:
                                                    P.wv(t, "dve", tricnt[0])
                                                for hi, hp in enumerate(hps):
                                                    lt = (kob[:, hp, s_ * 128:(s_ + 1) * 128]
                                                          if isdiag
                                                          else kb[:, hp, k * 128:(k + 1) * 128])
                                                    t.matmul(sc[se][0][:, hi * 128:(hi + 1) * 128],
                                                             lt[0:64, :],
                                                             qb[0:64, hp, s_ * 128:(s_ + 1) * 128],
                                                             start=True, stop=True,
                                                             tile_position=(0, 0))
                                                    i = t.matmul(
                                                        sc[se][1][:, hi * 128:(hi + 1) * 128],
                                                        lt[64:128, :],
                                                        qb[64:128, hp, s_ * 128:(s_ + 1) * 128],
                                                        start=True, stop=True,
                                                        tile_position=(64, 0))
                                                P.inc(i, "pe")
                                            assert P.cnt["pe"] == pe_base + 4 * (uj // 2) + 1 + uj % 2
                                            if isdiag:
                                                @blk.vector
                                                def _(v, se=se):
                                                    P.w(v, "pe")
                                                    trib = bass.AP(tri, 0,
                                                                   [list(tri.ap().ap[0]),
                                                                    [0, 4], [1, 128]])
                                                    for h01 in range(2):
                                                        v.tensor_add(
                                                            sc[se][h01][:].rearrange(
                                                                "p (hp i) -> p hp i", hp=4),
                                                            sc[se][h01][:].rearrange(
                                                                "p (hp i) -> p hp i", hp=4),
                                                            trib)
                                                tricnt[0] = P.cnt["dve"]
                                                assert tricnt[0] == dve_tri
                                        for (uj, isdiag, k, se) in pu:
                                            @blk.tensor
                                            def _(t, s_=s_, isdiag=isdiag, k=k, uj=uj, se=se,
                                                  hps=hps, g2=g2,
                                                  ecnt=act_base + 2 * (uj + 1),
                                                  last_u=(uj == nu - 1)):
                                                P.wv(t, "act", ecnt)
                                                if uj == 0:
                                                    P.wv(t, "dve", avevcnt[g2][0])
                                                    P.wv(t, "pool", avevcnt[g2][1])
                                                for h01 in range(2):
                                                    for hi, hp in enumerate(hps):
                                                        h = 2 * hp + h01
                                                        vt = (vob[:, s_, h * 65:(h + 1) * 65]
                                                              if isdiag
                                                              else vb[:, k, h * 65:(h + 1) * 65])
                                                        i = t.matmul(
                                                            av[g2][h01][0:65,
                                                                        hi * 128:(hi + 1) * 128],
                                                            vt,
                                                            psb[se][h01][:,
                                                                         hi * 128:(hi + 1) * 128],
                                                            start=(uj == 0 and hi == 0),
                                                            stop=last_u)
                                                P.inc(i, "pe")
                                        ui += len(pu)
                                        if first_pair:
                                            first_pair = False
                                            if pending[0] is not None:
                                                pending[0]()
                                                pending[0] = None
                                    last_even = nu - 1 if (nu - 1) % 2 == 0 else nu - 2
                                    last_odd = nu - 1 if (nu - 1) % 2 == 1 else nu - 2
                                    expcnt[0] = act_base + 2 * (last_even + 1)
                                    expcnt[1] = act_base + 2 * (last_odd + 1)
                                    def _evac(s_=s_, hpg=hpg, g2=g2):
                                        @blk.vector
                                        def _(v):
                                            P.w(v, "pe")
                                            for h01 in range(2):
                                                dst = onraw[h01 * 64:(h01 + 1) * 64,
                                                            4 * hpg:4 * hpg + 4,
                                                            s_ * 128:(s_ + 1) * 128]
                                                v.tensor_copy(dst,
                                                              av[g2][h01][0:64, :].rearrange(
                                                                  "p (c n) -> p c n", c=4))
                                        @blk.vector
                                        def _(v):
                                            for h01 in range(2):
                                                rdst = bass.AP(recs[h01],
                                                               4 * hpg * 256 + s_ * 128,
                                                               [list(recs[h01].ap().ap[0]),
                                                                [256, 4], [1, 128]])
                                                v.tensor_copy(rdst, av[g2][h01][64:65, :]
                                                              .rearrange("p (c n) -> p c n",
                                                                         c=4))
                                        avevcnt[g2] = (P.cnt["dve"], P.cnt["pool"])
                                        @blk.gpsimd
                                        def _(g):
                                            P.w(g, "dve")
                                            for h01 in range(2):
                                                wdst = d_recb.ap()[h01:h01 + 1, :].rearrange(
                                                    "a (c u n) -> a c u n", c=8, u=2)[
                                                    :, 4 * hpg:4 * hpg + 4, s_, :]
                                                wsrc = bass.AP(recs[h01],
                                                               4 * hpg * 256 + s_ * 128,
                                                               [list(recs[h01].ap().ap[0]),
                                                                [256, 4], [1, 128]])
                                                P.inc(g.dma_start(out=wdst, in_=wsrc),
                                                      "gp", 16)
                                            g.wait_ge(P.sem["gp"], P.cnt["gp"])
                                            for h01 in range(2):
                                                bsrc = bass.AP(
                                                    d_recb,
                                                    h01 * 2048 + 4 * hpg * 256 + s_ * 128,
                                                    [[0, 64], [256, 4], [1, 128]])
                                                bdst = scr8[h01 * 64:(h01 + 1) * 64,
                                                            4 * hpg:4 * hpg + 4,
                                                            s_ * 128:(s_ + 1) * 128]
                                                P.inc(g.dma_start(out=bdst, in_=bsrc),
                                                      "gp", 16)
                                        @blk.vector
                                        def _(v):
                                            P.w(v, "gp")
                                            for h01 in range(2):
                                                sv = scr8[h01 * 64:(h01 + 1) * 64,
                                                          4 * hpg:4 * hpg + 4,
                                                          s_ * 128:(s_ + 1) * 128]
                                                v.reciprocal(sv, sv)
                                    pending[0] = _evac
                                    slot += 1
                            if pending[0] is not None:
                                pending[0]()
                                pending[0] = None
                        @blk.vector
                        def _(v):
                            P.w(v, "gp", "dve")
                            P.inc(v.tensor_mul(onn[:], onraw[:], scr8[:]), "dve")
                        if debug and b == 1:
                            @blk.sync
                            def _(s):
                                P.w(s, "dve")
                                P.inc(s.dma_start(out=d_ton,
                                                  in_=onn[:].rearrange("p c n -> p (c n)")),
                                      "st", 16)
                                P.inc(s.dma_start(out=d_tor,
                                                  in_=kob[:].rearrange("p c n -> p (c n)")),
                                      "st", 16)
                                P.inc(s.dma_start(out=d_tyt,
                                                  in_=vob[:, 0, 0:256]), "st", 16)
                                P.inc(s.dma_start(out=d_tpy,
                                                  in_=vob[:, 0, 256:512]), "st", 16)
                                P.inc(s.dma_start(out=d_tfa,
                                                  in_=scr8[:].rearrange("p c n -> p (c n)")),
                                      "st", 16)
                                P.inc(s.dma_start(out=d_trc[0:1, 0:2048], in_=bass.AP(ro_c, 0, [[16, 128], [1, 16]]).rearrange("p c -> (p c)")[None, :] if False else rec0[:]), "st", 16)
                                P.inc(s.dma_start(out=d_trc[1:2, :], in_=rec1[:]), "st", 16)
                                P.inc(s.dma_start(out=d_twt[:, 0:16], in_=ro_c[:]), "st", 16)
                                P.inc(s.dma_start(out=d_twt[:, 16:32], in_=c_cb[:]), "st", 16)
                                P.inc(s.dma_start(out=d_twt[:, 32:48], in_=co_c[:]), "st", 16)
                                P.inc(s.dma_start(out=d_twt[:, 48:64], in_=r_cb[:]), "st", 16)
                                s.wait_ge(P.sem["st"], P.cnt["st"])
                        @blk.sync
                        def _(s, b=b):
                            P.w(s, "dve", "act", "pool", "st", "gp")
                            P.inc(s.dma_start(out=scr8[:],
                                              in_=xo_r[:, :, b * 256:(b + 1) * 256]), "ld", 16)
                        with nc.psum_tensor(f"psB_y0_{b}", [128, 256], F32) as py0, \
                             nc.psum_tensor(f"psB_y1_{b}", [128, 256], F32) as py1:
                            pys = (py0, py1)
                            wtuse2 = wtuse
                            for fc in range(8):
                                py = pys[fc % 2]
                                @blk.sync
                                def _(s, fc=fc):
                                    P.wv(s, "pe", wtuse2[fc % 2])
                                    P.w(s, "dve") if fc == 0 else None
                                    P.inc(s.dma_start(out=wt[:, fc % 2],
                                                      in_=woutbf_r[:, :, fc * 128:(fc + 1) * 128]),
                                          "ld", 16)
                                @blk.tensor
                                def _(t, fc=fc, py=py):
                                    P.w(t, "ld", "dve")
                                    if fc == 0:
                                        P.w(t, "act")
                                    for ec in range(NCH):
                                        i = t.matmul(py[:], wt[:, fc % 2, ec, :], onn[:, ec, :],
                                                     start=(ec == 0), stop=(ec == NCH - 1))
                                    P.inc(i, "pe")
                                @blk.vector
                                def _(v, fc=fc, py=py):
                                    P.w(v, "pe", "ld", "st")
                                    v.tensor_add(yt8[:, fc % 2, :], py[:], scr8[:, fc, :])
                                wtuse2[fc % 2] = P.cnt["pe"]
                                if fc % 2 == 1:
                                    @blk.sync
                                    def _(s, fc=fc, b=b):
                                        P.w(s, "dve")
                                        P.inc(s.dma_start(
                                            out=d_ypark.ap().rearrange("(c p) n -> p c n", p=128)[
                                                :, fc - 1:fc + 1, b * 256:(b + 1) * 256],
                                            in_=yt8[:]), "st", 16)
                    @blk.scalar
                    def _(sc):
                        P.w(sc, "gp", "dve")
                    @blk.vector
                    def _(v):
                        P.w(v, "gp", "st")
                    for seg in (2 * b, 2 * b + 1):
                        w8 = FFN_ORDER[seg // 2]
                        half = seg % 2
                        @blk.gpsimd
                        def _(g, w8=w8, half=half):
                            P.w(g, "st")
                            P.inc(g.dma_start(out=wstage_b[:],
                                              in_=w_r[w8][:, :, half * 512:(half + 1) * 512]),
                                  "gp", 16)
                        @blk.sync
                        def _(s, seg=seg):
                            P.w(s, "gp")
                            P.inc(s.dma_start(
                                out=wffnbf_r[:, :, seg * 512:(seg + 1) * 512],
                                in_=wstage_b[:]), "st", 16)
                    if debug and b == 0:
                        @blk.sync
                        def _(s):
                            P.inc(s.dma_start(out=d_tkb, in_=kb[:].rearrange("p c n -> p (c n)")),
                                  "st", 16)
                            P.inc(s.dma_start(out=d_tnb, in_=negb[:]), "st", 16)
                            s.wait_ge(P.sem["st"], P.cnt["st"])
        ypark_r = d_ypark.ap().rearrange("(c p) n -> p c n", p=128)

        with ExitStack() as cctx:
            csb = lambda n, sh, dt: cctx.enter_context(nc.sbuf_tensor(n, sh, dt))
            z = csb("z", [128, NCH, OWN], BF16)
            cuo = csb("cuo", [128, NCH, 256], F32)
            wt_c = csb("wt_c", [128, 2, NCH, 128], BF16)
            ysl = csb("ysl", [128, NCH, 256], F32)
            ybf_c = csb("ybf_c", [128, NCH, 256], BF16)
            tmp_c = csb("tmp_c", [128, 256], F32)

            with nc.psum_tensor("psC_su", [1, 2048], F32) as psu, \
                 nc.psum_tensor("psC_sq", [1, 2048], F32) as psq:
                for sl in range(8):
                    @blk.sync
                    def _(s, sl=sl):
                        P.w(s, "dve", "pe")
                        P.inc(s.dma_start(out=ysl[:, :, :],
                                          in_=ypark_r[:, :, sl * 256:(sl + 1) * 256]), "ld", 16)
                    @blk.vector
                    def _(v, sl=sl):
                        P.w(v, "ld", "pe")
                        v.tensor_copy(ybf_c[:], ysl[:, :, :])
                    @blk.tensor
                    def _(t, sl=sl):
                        P.w(t, "dve")
                        for c in range(NCH):
                            i = t.matmul(psu[:, sl * 256:(sl + 1) * 256], ones_bf[:],
                                         ybf_c[:, c, :], start=(c == 0), stop=(c == NCH - 1))
                        P.inc(i, "pe")
                    @blk.vector
                    def _(v, sl=sl):
                        P.w(v, "pe")
                        v.tensor_mul(ybf_c[:], ysl[:, :, :], ysl[:, :, :])
                    @blk.tensor
                    def _(t, sl=sl):
                        P.w(t, "dve")
                        for c in range(NCH):
                            i = t.matmul(psq[:, sl * 256:(sl + 1) * 256], ones_bf[:],
                                         ybf_c[:, c, :], start=(c == 0), stop=(c == NCH - 1))
                        P.inc(i, "pe")
                @blk.scalar
                def _(sc):
                    P.w(sc, "pe")
                    sc.mul(rw1[:], psu[:], 1.0 / D)
                    sc.mul(rw2[:], psq[:], 1.0 / D)
            @blk.vector
            def _(v):
                P.w(v, "act")
                for sl in range(8):
                    v.tensor_mul(tmp_c[0:1, 0:256], rw1[:, sl * 256:(sl + 1) * 256],
                                 rw1[:, sl * 256:(sl + 1) * 256])
                    v.tensor_sub(rw2[:, sl * 256:(sl + 1) * 256],
                                 rw2[:, sl * 256:(sl + 1) * 256], tmp_c[0:1, 0:256])
            @blk.scalar
            def _(sc):
                P.w(sc, "dve")
                sc.activation(rw2[:], rw2[:], AF.Ln, bias=epsc[:])
                sc.activation(rw2[:], rw2[:], AF.Exp, scale=-0.5)
            @blk.gpsimd
            def _(g):
                P.w(g, "act", "gp")
                P.inc(g.dma_start(out=d_rowb.ap(), in_=rw1[:]), "gp", 16)
                g.wait_ge(P.sem["gp"], P.cnt["gp"])
                P.inc(g.dma_start(out=d_recb.ap()[0:1, :], in_=rw2[:]), "gp", 16)
            with nc.sbuf_tensor("mu2bc", [128, OWN], F32) as mu2bc, \
                 nc.sbuf_tensor("r2bc", [128, OWN], F32) as r2bc:
                @blk.gpsimd
                def _(g):
                    P.w(g, "gp")
                    P.inc(g.dma_start(out=mu2bc[:],
                                      in_=d_rowb.ap().partition_broadcast(128)[:, 0, :]),
                          "gp", 16)
                    P.inc(g.dma_start(out=r2bc[:],
                                      in_=bass.AP(d_recb, 0, [[0, 128], [1, OWN]])), "gp", 16)
                for sl in range(8):
                    @blk.sync
                    def _(s, sl=sl):
                        P.w(s, "dve", "pe")
                        P.inc(s.dma_start(out=ysl[:, :, :],
                                          in_=ypark_r[:, :, sl * 256:(sl + 1) * 256]), "ld", 16)
                    @blk.vector
                    def _(v, sl=sl):
                        P.w(v, "gp", "ld")
                        mub = bass.AP(mu2bc, sl * 256, [list(mu2bc.ap().ap[0]), [0, NCH], [1, 256]])
                        rbb = bass.AP(r2bc, sl * 256, [list(r2bc.ap().ap[0]), [0, NCH], [1, 256]])
                        v.tensor_sub(ysl[:, :, :], ysl[:, :, :], mub)
                        v.tensor_mul(z[:, :, sl * 256:(sl + 1) * 256], ysl[:, :, :], rbb)
            rctx = ExitStack()
            wres = {w: rctx.enter_context(nc.sbuf_tensor("w_" + w, [128, NCH, D], BF16))
                    for w in ("wf", "wi", "wc", "uf", "ui", "uc")}
            cst = rctx.enter_context(nc.sbuf_tensor("cst", [128, NCH, 256], F32))
            cbf = rctx.enter_context(nc.sbuf_tensor("cbf", [128, NCH, 256], BF16))
            gates = {g: rctx.enter_context(nc.sbuf_tensor("g_" + g, [128, NCH, 256], F32))
                     for g in ("f", "i", "c")}

            @blk.sync
            def _(s):
                P.w(s, "dve")
                for wi_, w8 in enumerate(("wf", "wi", "wc", "uf", "ui", "uc")):
                    P.inc(s.dma_start(out=wres[w8][:],
                                      in_=wffnbf_r[:, :, wi_ * 1024:(wi_ + 1) * 1024]),
                          "ld", 16)
            for c in range(NCH):
                @blk.vector
                def _(v, c=c):
                    P.w(v, "gp")
                    src = bass.AP(bv["incv"], c, [list(bv["incv"].ap().ap[0]), [0, 256]])
                    P.inc(v.tensor_copy(cst[:, c, :], src), "dve")
            @blk.vector
            def _(v):
                P.w(v, "dve", "pool")
                P.inc(v.tensor_copy(cbf[:], cst[:]), "dve")
            gnames = (("f", "wf", "uf", "bfv", AF.Sigmoid),
                      ("i", "wi", "ui", "biv", AF.Sigmoid),
                      ("c", "wc", "uc", "bcv", AF.Tanh))
            with nc.psum_tensor("psC_g0", [128, 256], F32) as pg0, \
                 nc.psum_tensor("psC_g1", [128, 256], F32) as pg1, \
                 nc.psum_tensor("psC_g2", [128, 256], F32) as pg2, \
                 nc.psum_tensor("psC_g3", [128, 256], F32) as pg3:
                pgs = (pg0, pg1, pg2, pg3)
                pguse = [0, 0, 0, 0]
                gi = 0
                for t_ in range(B):
                    for (gn, wn, un, bn, af) in gnames:
                        for mc in range(NCH):
                            bi = gi % 4
                            gi += 1
                            pg = pgs[bi]
                            @blk.tensor
                            def _(t, t_=t_, wn=wn, un=un, mc=mc, pg=pg, bi=bi):
                                P.wv(t, "act", pguse[bi])
                                if mc == 0 and gn == "f":
                                    P.w(t, "dve", "pool", "ld")
                                for ec in range(NCH):
                                    t.matmul(pg[:], wres[wn][:, ec, mc * 128:(mc + 1) * 128],
                                             z[:, ec, t_ * 256:(t_ + 1) * 256],
                                             start=(ec == 0), stop=False)
                                for ec in range(NCH):
                                    i = t.matmul(pg[:], wres[un][:, ec, mc * 128:(mc + 1) * 128],
                                                 cbf[:, ec, :],
                                                 start=False, stop=(ec == NCH - 1))
                                P.inc(i, "pe")
                            @blk.scalar
                            def _(sc, gn=gn, bn=bn, af=af, mc=mc, pg=pg, bi=bi):
                                P.w(sc, "pe")
                                sc.activation(gates[gn][:, mc, :], pg[:], af,
                                              bias=bv[bn][:, mc:mc + 1])
                                pguse[bi] = P.cnt["act"]
                    for mc in range(NCH):
                        dec = blk.vector if mc % 2 == 0 else blk.gpsimd
                        nm = "dve" if mc % 2 == 0 else "pool"
                        @dec
                        def _(e, mc=mc, nm=nm):
                            P.w(e, "act")
                            scratch = tmp_c[:, :] if nm == "dve" else ysl[:, 0, :]
                            e.tensor_mul(cst[:, mc, :], cst[:, mc, :], gates["f"][:, mc, :])
                            e.tensor_mul(scratch, gates["i"][:, mc, :], gates["c"][:, mc, :])
                            e.tensor_add(cst[:, mc, :], cst[:, mc, :], scratch)
                    @blk.vector
                    def _(v, t_=t_):
                        P.w(v, "dve", "pool")
                        v.tensor_copy(cbf[:], cst[:])
            wcuse = [0, 0]
            for mc in range(NCH):
                @blk.sync
                def _(s, mc=mc):
                    P.wv(s, "pe", wcuse[mc % 2])
                    if mc == 0:
                        P.w(s, "dve")
                    P.inc(s.dma_start(out=wt_c[:, mc % 2],
                                      in_=wffnbf_r[:, :, 7 * 1024 + mc * 128:
                                                   7 * 1024 + (mc + 1) * 128]), "ld", 16)
                with nc.psum_tensor(f"psC_uo_{mc}", [128, 256], F32) as pu:
                    @blk.tensor
                    def _(t, mc=mc, pu=pu):
                        P.w(t, "ld", "act", "dve", "pool")
                        for ec in range(NCH):
                            i = t.matmul(pu[:], wt_c[:, mc % 2, ec, :], cbf[:, ec, :],
                                         start=(ec == 0), stop=(ec == NCH - 1))
                        P.inc(i, "pe")
                    wcuse[mc % 2] = P.cnt["pe"]
                    @blk.vector
                    def _(v, mc=mc, pu=pu):
                        P.w(v, "pe")
                        v.tensor_copy(cuo[:, mc, :], pu[:])
            rctx.close()
            with ExitStack() as octx:
                wo_res = octx.enter_context(nc.sbuf_tensor("wo_res", [128, NCH, D], BF16))
                ystage = octx.enter_context(nc.sbuf_tensor("ystage", [128, NCH, 256], F32))
                yo_r = d_yo.rearrange("(c p) n -> p c n", p=128)
                @blk.sync
                def _(s):
                    P.w(s, "pe")
                    P.inc(s.dma_start(out=wo_res[:],
                                      in_=wffnbf_r[:, :, 6 * 1024:7 * 1024]), "ld", 16)
                for bt in range(B):
                    @blk.sync
                    def _(s, bt=bt):
                        P.w(s, "dve", "st")
                        P.inc(s.dma_start(out=ysl[:, :, :],
                                          in_=ypark_r[:, :, bt * 256:(bt + 1) * 256]),
                              "ld", 16)
                    for mc in range(NCH):
                        with nc.psum_tensor(f"psC_o_{bt}_{mc}", [128, 256], F32) as po:
                            @blk.tensor
                            def _(t, mc=mc, bt=bt, po=po):
                                P.w(t, "ld")
                                P.w(t, "dve")
                                for ec in range(NCH):
                                    i = t.matmul(po[:], wo_res[:, ec, mc * 128:(mc + 1) * 128],
                                                 z[:, ec, bt * 256:(bt + 1) * 256],
                                                 start=(ec == 0), stop=(ec == NCH - 1))
                                P.inc(i, "pe")
                            @blk.vector
                            def _(v, mc=mc, po=po):
                                P.w(v, "pe", "act")
                                P.inc(v.tensor_add(tmp_c[:], po[:], cuo[:, mc, :]), "dve")
                            @blk.scalar
                            def _(sc, mc=mc):
                                P.w(sc, "dve")
                                P.inc(sc.activation(tmp_c[:], tmp_c[:], AF.Tanh,
                                                    bias=bv["bov"][:, mc:mc + 1]), "act")
                            @blk.vector
                            def _(v, mc=mc):
                                P.w(v, "act", "ld")
                                P.inc(v.tensor_add(ystage[:, mc, :], tmp_c[:], ysl[:, mc, :]),
                                      "dve")
                    @blk.sync
                    def _(s, bt=bt):
                        P.w(s, "dve")
                        P.inc(s.dma_start(out=yo_r[:, :, bt * 256:(bt + 1) * 256],
                                          in_=ystage[:, :, :]), "st", 16)
        @blk.sync
        def _(s):
            P.wall(s)
    return nc


_NC_CACHE = {}


def kernel(L0_attn_query, L0_attn_key, L0_attn_value, L0_attn_in_proj_weight,
           L0_attn_out_proj_weight, L0_ffn_wf, L0_ffn_uf, L0_ffn_bf, L0_ffn_wi,
           L0_ffn_ui, L0_ffn_bi, L0_ffn_wc, L0_ffn_uc, L0_ffn_bc, L0_ffn_wo,
           L0_ffn_uo, L0_ffn_bo, L0_ffn_in_c):
    x = np.asarray(L0_attn_query, np.float32)
    xf = np.ascontiguousarray(x.transpose(2, 0, 1).reshape(D, B * T))
    wqkv = np.ascontiguousarray(np.asarray(L0_attn_in_proj_weight, np.float32).T)
    wout = np.ascontiguousarray(np.asarray(L0_attn_out_proj_weight, np.float32).T)
    wmats = {k: np.ascontiguousarray(np.asarray(v, np.float32).T) for k, v in
             dict(wf=L0_ffn_wf, wi=L0_ffn_wi, wc=L0_ffn_wc, uf=L0_ffn_uf,
                  ui=L0_ffn_ui, uc=L0_ffn_uc, wo=L0_ffn_wo, uo=L0_ffn_uo).items()}
    bvecs = {k: np.ascontiguousarray(np.asarray(v, np.float32).reshape(NCH, 128).T)
             for k, v in dict(bfv=L0_ffn_bf, biv=L0_ffn_bi, bcv=L0_ffn_bc,
                              bov=L0_ffn_bo, incv=L0_ffn_in_c).items()}
    jj = np.arange(128)
    trimask = np.where(jj[:, None] > jj[None, :], np.float32(NEG), np.float32(0.0))
    trimask = np.ascontiguousarray(trimask.astype(np.float32))

    in_maps = []
    for core in range(8):
        blks = (core, 8 + core)
        xo = np.empty((D, OWN), np.float32)
        for b in range(B):
            for s_, blkid in enumerate(blks):
                xo[:, b * 256 + s_ * 128: b * 256 + (s_ + 1) * 128] = \
                    x[b, blkid * 128:(blkid + 1) * 128, :].T
        deny = np.zeros((128, 32), np.float32)
        for s_, blkid in enumerate(blks):
            for k in range(16):
                if k >= blkid:
                    deny[:, s_ * 16 + k] = NEG
        m = dict(xf=xf, xo=np.ascontiguousarray(xo), wqkv=wqkv, wout=wout,
                 deny=deny, trimask=trimask, iden=np.eye(128, dtype=np.float32))
        m.update(wmats)
        m.update(bvecs)
        in_maps.append(m)

    dbg = bool(_NC_CACHE.get("debug"))
    key = ("nc", dbg)
    if key not in _NC_CACHE:
        _NC_CACHE[key] = build_nc(debug=dbg)
    nc = _NC_CACHE[key]
    res = run_bass_kernel_spmd(nc, in_maps, core_ids=list(range(8)))

    if dbg:
        _NC_CACHE["results"] = res.results
    out = np.empty((B, T, D), np.float32)
    for core in range(8):
        yo = res.results[core]["yo"]
        for b in range(B):
            for s_, blkid in enumerate((core, 8 + core)):
                out[b, blkid * 128:(blkid + 1) * 128, :] = \
                    yo[:, b * 256 + s_ * 128: b * 256 + (s_ + 1) * 128].T
    return out


# version 11
# speedup vs baseline: 1.2005x; 1.0221x over previous
import numpy as np
from contextlib import ExitStack
import concourse.bass as bass
import concourse.mybir as mybir
from concourse.bass_utils import run_bass_kernel_spmd

F32 = mybir.dt.float32
BF16 = mybir.dt.bfloat16
AF = mybir.ActivationFunctionType
ALU = mybir.AluOpType
AX = mybir.AxisListType

D, NCH, B, T, H = 1024, 8, 8, 2048, 16
OWN = 2048
NEG = -1.0e30
EPS = 1e-5


class P_:
    def __init__(self, nc, ctx):
        self.sem = {}
        self.cnt = {}
        for n in ("ld", "gp", "pe", "act", "dve", "pool", "st"):
            self.sem[n] = ctx.enter_context(nc.semaphore(n))
            self.cnt[n] = 0
        self.tagged = set()
        self._keep = []

    def inc(self, instr, name, amt=None):
        if id(instr) in self.tagged:
            return
        amt = amt if amt is not None else 1
        instr.then_inc(self.sem[name], amt)
        self.cnt[name] += amt

    def wv(self, eng, name, val):
        if val > 0:
            eng.wait_ge(self.sem[name], val)

    def w(self, eng, *names):
        for n in names:
            if self.cnt[n] > 0:
                eng.wait_ge(self.sem[n], self.cnt[n])

    def wall(self, eng):
        self.w(eng, "ld", "gp", "pe", "act", "dve", "pool", "st")


_PASSTHRU = ("wait_ge", "sem_inc", "nop", "dma_start", "dma_start_transpose",
             "indirect_dma_start", "trigger_dma", "collective_compute")


class SeqProxy:

    def __init__(self, eng, P, nm):
        self._eng, self._P, self._nm = eng, P, nm

    def __getattr__(self, name):
        f = getattr(self._eng, name)
        if name.startswith("_") or name in _PASSTHRU or not callable(f):
            return f

        def wrap(*args, **kw):
            P, nm = self._P, self._nm
            if P.cnt[nm] > 0:
                self._eng.wait_ge(P.sem[nm], P.cnt[nm])
            instr = f(*args, **kw)
            instr.then_inc(P.sem[nm], 1)
            P.cnt[nm] += 1
            P.tagged.add(id(instr))
            P._keep.append(instr)
            return instr
        return wrap


class BlkWrap:
    def __init__(self, blk, P):
        self._blk, self._P = blk, P

    @property
    def sync(self):
        return self._blk.sync

    @property
    def tensor(self):
        return self._blk.tensor

    @property
    def vector(self):
        return lambda fn: self._blk.vector(lambda e: fn(SeqProxy(e, self._P, "dve")))

    @property
    def scalar(self):
        return lambda fn: self._blk.scalar(lambda e: fn(SeqProxy(e, self._P, "act")))

    @property
    def gpsimd(self):
        return lambda fn: self._blk.gpsimd(lambda e: fn(SeqProxy(e, self._P, "pool")))


def build_nc(debug=False):
    nc = bass.Bass("TRN2", target_bir_lowering=False)

    d_xf = nc.dram_tensor("xf", [D, B * T], F32, kind="ExternalInput").ap()
    d_xo = nc.dram_tensor("xo", [D, OWN], F32, kind="ExternalInput").ap()
    d_wqkv = nc.dram_tensor("wqkv", [D, 3 * D], F32, kind="ExternalInput").ap()
    d_wout = nc.dram_tensor("wout", [D, D], F32, kind="ExternalInput").ap()
    d_w = {w: nc.dram_tensor(w, [D, D], F32, kind="ExternalInput").ap()
           for w in ("wf", "wi", "wc", "uf", "ui", "uc", "wo", "uo")}
    d_bv = {v: nc.dram_tensor(v, [128, NCH], F32, kind="ExternalInput").ap()
            for v in ("bfv", "biv", "bcv", "bov", "incv")}
    d_deny = nc.dram_tensor("deny", [128, 32], F32, kind="ExternalInput").ap()
    d_tri = nc.dram_tensor("trimask", [128, 128], F32, kind="ExternalInput").ap()
    d_iden = nc.dram_tensor("iden", [128, 128], F32, kind="ExternalInput").ap()
    d_yo = nc.dram_tensor("yo", [D, OWN], F32, kind="ExternalOutput").ap()

    dkind = "ExternalOutput" if debug else "Internal"
    d_wqkv_bf = nc.dram_tensor("wqkv_bf", [D, 3 * D], BF16)
    d_wout_bf = nc.dram_tensor("wout_bf", [D, D], BF16)
    d_wffn_bf = nc.dram_tensor("wffn_bf", [D, 8 * D], BF16)
    d_qpark = nc.dram_tensor("q_park", [D, OWN], BF16, kind=dkind)
    d_ypark = nc.dram_tensor("y_park", [D, OWN], F32, kind=dkind)
    if debug:
        d_txh = nc.dram_tensor("t_xhat", [128, NCH * OWN], BF16, kind="ExternalOutput").ap()
        d_tkb = nc.dram_tensor("t_kb", [128, NCH * T], BF16, kind="ExternalOutput").ap()
        d_ton = nc.dram_tensor("t_onn", [128, NCH * 256], BF16, kind="ExternalOutput").ap()
        d_tor = nc.dram_tensor("t_onraw", [128, NCH * 256], BF16, kind="ExternalOutput").ap()
        d_trc = nc.dram_tensor("t_rec", [2, 2048], F32, kind="ExternalOutput").ap()
        d_tfa = nc.dram_tensor("t_fac", [128, NCH * 256], F32, kind="ExternalOutput").ap()
        d_tyt = nc.dram_tensor("t_yt", [128, 256], BF16, kind="ExternalOutput").ap()
        d_txr = nc.dram_tensor("t_xr", [128, 256], F32, kind="ExternalOutput").ap()
        d_tpy = nc.dram_tensor("t_py", [128, 256], BF16, kind="ExternalOutput").ap()
        d_twt = nc.dram_tensor("t_wt", [128, 128], F32, kind="ExternalOutput").ap()
        d_tnb = nc.dram_tensor("t_nb", [128, 1], F32, kind="ExternalOutput").ap()
        d_tz = nc.dram_tensor("t_z", [128, NCH * OWN], BF16, kind="ExternalOutput").ap()
    d_rowb = nc.dram_tensor("row_bounce", [1, T], F32)
    d_recb = nc.dram_tensor("rec_bounce", [2, 2048], F32)

    xf_r = d_xf.rearrange("(c p) n -> p c n", p=128)
    xo_r = d_xo.rearrange("(c p) n -> p c n", p=128)
    wqkv_r = d_wqkv.rearrange("(c p) f -> p c f", p=128)
    wout_r = d_wout.rearrange("(c p) f -> p c f", p=128)
    w_r = {k: v.rearrange("(c p) f -> p c f", p=128) for k, v in d_w.items()}
    qpark_r = d_qpark.ap().rearrange("(c p) n -> p c n", p=128)
    wqkvbf_r = d_wqkv_bf.ap().rearrange("(c p) f -> p c f", p=128)
    woutbf_r = d_wout_bf.ap().rearrange("(c p) f -> p c f", p=128)
    wffnbf_r = d_wffn_bf.ap().rearrange("(c p) f -> p c f", p=128)
    FFN_ORDER = ("wf", "wi", "wc", "uf", "ui", "uc", "wo", "uo")

    with ExitStack() as ctx:
        sb = lambda n, sh, dt: ctx.enter_context(nc.sbuf_tensor(n, sh, dt))
        tri = sb("tri", [128, 128], F32)
        iden = sb("iden_s", [128, 128], F32)
        deny = sb("deny_s", [128, 32], F32)
        ones_bf = sb("ones_bf", [128, 1], BF16)
        ones_f1 = sb("ones_f1", [1, 128], F32)
        idy = sb("idy", [1, 1], F32)
        bv = {v: sb(v + "_s", [128, NCH], F32) for v in d_bv}
        ro_c = sb("ro_c", [128, 16], F32)
        lo_c = sb("lo_c", [128, 16], F32)
        co_c = sb("co_c", [128, 16], F32)
        qmax2 = sb("qmax2", [1, 1], F32)
        epsc = sb("epsc", [1, 1], F32)
        rw1 = sb("rw1", [1, T], F32)
        rw2 = sb("rw2", [1, T], F32)

        P = P_(nc, ctx)
        blk = BlkWrap(ctx.enter_context(nc.Block()), P)

        @blk.sync
        def _(s):
            P.inc(s.dma_start(out=tri[:], in_=d_tri), "ld", 16)
            P.inc(s.dma_start(out=iden[:], in_=d_iden), "ld", 16)
            P.inc(s.dma_start(out=deny[:], in_=d_deny), "ld", 16)
            for v in bv:
                P.inc(s.dma_start(out=bv[v][:], in_=d_bv[v]), "ld", 16)

        @blk.vector
        def _(v):
            v.memset(ones_bf[:], 1.0)
            v.memset(ones_f1[:], 1.0)
            v.memset(idy[:], 1.0)
            v.memset(epsc[:], EPS)
            P.inc(v.memset(qmax2[:], 0.0), "dve")

        with ExitStack() as abctx:
            xa_bf = abctx.enter_context(nc.sbuf_tensor("xa_bf", [128, NCH, OWN], BF16))
            mu_own_bf = abctx.enter_context(nc.sbuf_tensor("mu_own_bf", [1, OWN], BF16))

            with ExitStack() as actx:
                asb = lambda n, sh, dt: actx.enter_context(nc.sbuf_tensor(n, sh, dt))
                x2sl = asb("x2sl_a", [128, NCH, 512], BF16)
                wq = asb("wq_a", [128, NCH, 1024], BF16)
                nwq = asb("nwq_a", [1, 1024], BF16)
                qev = asb("qev_a", [128, 512], BF16)
                q2 = asb("q2_a", [128, 512], BF16)
                rbc = asb("rbc_a", [128, 512], F32)
                qscr = asb("qscr_a", [1, 4], F32)

                wstage = asb("wstage_a", [128, NCH, 1024], BF16)

                @blk.gpsimd
                def _(g):
                    P.inc(g.dma_start(out=xa_bf[:], in_=xo_r), "gp", 16)
                    g.wait_ge(P.sem["gp"], P.cnt["gp"])
                    P.inc(g.dma_start(out=wq[:], in_=wqkv_r[:, :, 0:1024]), "gp", 16)
                for seg in range(4):
                    srcv = (wqkv_r[:, :, seg * 1024:(seg + 1) * 1024] if seg < 3
                            else wout_r[:, :, 0:1024])
                    dstv = (d_wqkv_bf.ap().rearrange("(c p) f -> p c f", p=128)[
                                :, :, seg * 1024:(seg + 1) * 1024] if seg < 3
                            else woutbf_r[:, :, 0:1024])
                    @blk.gpsimd
                    def _(g, srcv=srcv, seg=seg):
                        P.w(g, "st")
                        P.inc(g.dma_start(out=wstage[:], in_=srcv), "gp", 16)
                    @blk.sync
                    def _(s, dstv=dstv):
                        P.w(s, "gp")
                        P.inc(s.dma_start(out=dstv, in_=wstage[:]), "st", 16)

                with nc.psum_tensor("psA_su", [1, 2048], F32) as psu, \
                     nc.psum_tensor("psA_sq", [1, 2048], F32) as psq:
                    for sl in range(4):
                        @blk.vector
                        def _(v, sl=sl):
                            P.w(v, "gp", "pe")
                            v.tensor_mul(x2sl[:], xa_bf[:, :, sl * 512:(sl + 1) * 512],
                                         xa_bf[:, :, sl * 512:(sl + 1) * 512])
                        @blk.tensor
                        def _(t, sl=sl):
                            P.w(t, "dve", "gp")
                            for c in range(NCH):
                                t.matmul(psu[:, sl * 512:(sl + 1) * 512], ones_bf[:],
                                         xa_bf[:, c, sl * 512:(sl + 1) * 512],
                                         start=(c == 0), stop=(c == NCH - 1))
                                i = t.matmul(psq[:, sl * 512:(sl + 1) * 512], ones_bf[:],
                                             x2sl[:, c, :], start=(c == 0), stop=(c == NCH - 1))
                            P.inc(i, "pe")
                    @blk.scalar
                    def _(sc):
                        P.w(sc, "pe")
                        sc.mul(rw1[:], psu[:], 1.0 / D)
                        sc.mul(rw2[:], psq[:], 1.0 / D)
                        sc.copy(mu_own_bf[:], rw1[:])
                @blk.vector
                def _(v):
                    P.w(v, "act")
                    for sl in range(4):
                        v.tensor_mul(rbc[0:1, :], rw1[:, sl * 512:(sl + 1) * 512],
                                     rw1[:, sl * 512:(sl + 1) * 512])
                        v.tensor_sub(rw2[:, sl * 512:(sl + 1) * 512],
                                     rw2[:, sl * 512:(sl + 1) * 512], rbc[0:1, 0:512])
                @blk.scalar
                def _(sc):
                    P.w(sc, "dve")
                    sc.activation(rw2[:], rw2[:], AF.Ln, bias=epsc[:])
                    sc.activation(rw2[:], rw2[:], AF.Exp, scale=-0.5)
                with nc.psum_tensor("psA_tr", [128, 16], F32) as ptr:
                    @blk.tensor
                    def _(t):
                        P.w(t, "act")
                        for jt in range(16):
                            i = t.transpose(ptr[:, jt:jt + 1], rw2[:, jt * 128:(jt + 1) * 128],
                                            idy[:])
                        P.inc(i, "pe")
                    @blk.vector
                    def _(v):
                        P.w(v, "pe")
                        v.tensor_copy(ro_c[:], ptr[:])
                        v.reciprocal(co_c[:], ptr[:])
                @blk.scalar
                def _(sc):
                    P.w(sc, "dve")
                    sc.activation(lo_c[:], ro_c[:], AF.Ln)
                    sc.mul(rw1[:], rw2[:], 0.125)
                @blk.gpsimd
                def _(g):
                    P.w(g, "act")
                    P.inc(g.dma_start(out=d_rowb.ap(), in_=rw1[:]), "gp", 16)
                with nc.psum_tensor("psA_ws", [1, 128], F32) as pws:
                    for fc in range(8):
                        @blk.tensor
                        def _(t, fc=fc):
                            P.w(t, "gp")
                            if fc:
                                P.w(t, "act")
                            for ec in range(NCH):
                                i = t.matmul(pws[:], ones_bf[:],
                                             wq[:, ec, fc * 128:(fc + 1) * 128],
                                             start=(ec == 0), stop=(ec == NCH - 1))
                            P.inc(i, "pe")
                        @blk.scalar
                        def _(sc, fc=fc):
                            P.w(sc, "pe")
                            sc.mul(nwq[:, fc * 128:(fc + 1) * 128], pws[:], -1.0)
                with nc.psum_tensor("psA_q", [128, 512], F32) as pq, \
                     nc.psum_tensor("psA_n0", [1, 512], F32) as pn0, \
                     nc.psum_tensor("psA_n1", [1, 512], F32) as pn1:
                    for sl in range(4):
                        @blk.gpsimd
                        def _(g, sl=sl):
                            P.w(g, "gp", "dve")
                            P.inc(g.dma_start(
                                out=rbc[:],
                                in_=d_rowb.ap().partition_broadcast(128)[:, 0, sl * 512:(sl + 1) * 512]),
                                "gp", 16)
                        for fc in range(8):
                            @blk.tensor
                            def _(t, fc=fc, sl=sl):
                                P.w(t, "gp", "act", "dve")
                                for c in range(NCH):
                                    t.matmul(pq[:], wq[:, c, fc * 128:(fc + 1) * 128],
                                             xa_bf[:, c, sl * 512:(sl + 1) * 512],
                                             start=(c == 0), stop=False)
                                i = t.matmul(pq[:], nwq[:, fc * 128:(fc + 1) * 128],
                                             mu_own_bf[:, sl * 512:(sl + 1) * 512],
                                             start=False, stop=True)
                                P.inc(i, "pe")
                            @blk.vector
                            def _(v, fc=fc, sl=sl):
                                P.w(v, "pe", "st")
                                v.tensor_mul(qev[:], pq[:], rbc[:])
                                v.tensor_mul(q2[:], qev[:], qev[:])
                            @blk.sync
                            def _(s, fc=fc, sl=sl):
                                P.w(s, "dve")
                                P.inc(s.dma_start(
                                    out=d_qpark.ap()[fc * 128:(fc + 1) * 128,
                                                     sl * 512:(sl + 1) * 512],
                                    in_=qev[:]), "st", 16)
                            @blk.tensor
                            def _(t, fc=fc):
                                P.w(t, "dve")
                                t.matmul(pn0[:], ones_bf[0:64, :], q2[0:64, :],
                                         start=True, stop=True)
                                P.inc(t.matmul(pn1[:], ones_bf[64:128, :], q2[64:128, :],
                                               start=True, stop=True), "pe")
                            @blk.vector
                            def _(v):
                                P.w(v, "pe", "gp")
                                v.tensor_reduce(qscr[:, 0:1], pn0[:], axis=AX.X, op=ALU.max)
                                v.tensor_reduce(qscr[:, 1:2], pn1[:], axis=AX.X, op=ALU.max)
                                v.tensor_max(qscr[:, 0:1], qscr[:, 0:1], qscr[:, 1:2])
                                v.tensor_max(qmax2[:], qmax2[:], qscr[:, 0:1])
            for dec in (blk.sync, blk.gpsimd, blk.tensor, blk.scalar, blk.vector):
                @dec
                def _(e):
                    P.wall(e)
            @blk.scalar
            def _(sc):
                P.w(sc, "dve")
                sc.activation(qmax2[:], qmax2[:], AF.Ln)
                P.inc(sc.activation(qmax2[:], qmax2[:], AF.Exp, scale=0.5), "act")
            if debug:
                @blk.sync
                def _(s):
                    P.inc(s.dma_start(out=d_txh, in_=xa_bf[:].rearrange("p c n -> p (c n)")),
                          "st", 16)
                    s.wait_ge(P.sem["st"], P.cnt["st"])

            with ExitStack() as bctx:
                bsb = lambda n, sh, dt: bctx.enter_context(nc.sbuf_tensor(n, sh, dt))
                xb_bf = bsb("xb_bf", [128, NCH, T], BF16)
                kb = bsb("kb", [128, NCH, T], BF16)
                vb = bsb("vb", [128, 16, 1040], BF16)
                kob = bsb("kob", [128, NCH, 256], BF16)
                vob = bsb("vob", [128, 2, 1040], BF16)
                nwv = bsb("nwv", [1, 1024], BF16)
                nwk = bsb("nwk", [1, 128], BF16)
                mu_b_bf = bsb("mu_b_bf", [1, T], BF16)
                wt = bsb("wt", [128, 2, NCH, 128], BF16)
                qb = bsb("qb", [128, NCH, 256], BF16)
                scr8 = bsb("scr8", [128, NCH, 256], F32)
                yt8 = bsb("yt8", [128, 2, 256], F32)
                r_cb = bsb("r_cb", [128, 16], F32)
                l_cb = bsb("l_cb", [128, 16], F32)
                c_cb = bsb("c_cb", [128, 16], F32)
                btab = bsb("btab", [128, 32], F32)
                btabd = bsb("btabd", [128, 2], F32)
                negb = bsb("negb", [128, 1], F32)
                bscr = bsb("bscr", [1, 8], F32)


                wstage_b = bsb("wstage_b", [128, NCH, 512], BF16)
                wtuse = [0, 0]
                x2b = bsb("x2b", [128, NCH, 128], BF16)
                maxkf = bsb("maxkf", [128, 8], F32)
                kpre = [False]

                def emit_B1(bb):
                    with nc.psum_tensor(f"psB_su{bb}", [1, 2048], F32) as psu, \
                         nc.psum_tensor(f"psB_sq{bb}", [1, 2048], F32) as psq:
                        for sl in range(16):
                            @blk.vector
                            def _(v, sl=sl):
                                P.w(v, "gp", "pe")
                                v.tensor_mul(x2b[:], xb_bf[:, :, sl * 128:(sl + 1) * 128],
                                             xb_bf[:, :, sl * 128:(sl + 1) * 128])
                            @blk.tensor
                            def _(t, sl=sl):
                                P.w(t, "dve", "gp")
                                if sl == 0:
                                    P.w(t, "act")
                                for c in range(NCH):
                                    t.matmul(psu[:, sl * 128:(sl + 1) * 128], ones_bf[:],
                                             xb_bf[:, c, sl * 128:(sl + 1) * 128],
                                             start=(c == 0), stop=(c == NCH - 1))
                                    i = t.matmul(psq[:, sl * 128:(sl + 1) * 128], ones_bf[:],
                                                 x2b[:, c, :], start=(c == 0), stop=(c == NCH - 1))
                                P.inc(i, "pe")
                        @blk.scalar
                        def _(sc):
                            P.w(sc, "pe", "gp")
                            sc.mul(rw1[:], psu[:], 1.0 / D)
                            sc.mul(rw2[:], psq[:], 1.0 / D)
                            sc.copy(mu_b_bf[:], rw1[:])

                def emit_B1_tail(bb):
                    with nc.psum_tensor(f"psB_var{bb}", [1, 512], F32) as pvar:
                        @blk.vector
                        def _(v):
                            P.w(v, "act")
                            for sl in range(4):
                                v.tensor_mul(pvar[:], rw1[:, sl * 512:(sl + 1) * 512],
                                             rw1[:, sl * 512:(sl + 1) * 512])
                                v.tensor_sub(rw2[:, sl * 512:(sl + 1) * 512],
                                             rw2[:, sl * 512:(sl + 1) * 512], pvar[:])
                    @blk.scalar
                    def _(sc):
                        P.w(sc, "dve")
                        sc.activation(rw1[:], rw2[:], AF.Ln, bias=epsc[:])
                        sc.activation(rw2[:], rw1[:], AF.Exp, scale=-0.5)
                    with nc.psum_tensor(f"psB_tr{bb}", [128, 16], F32) as ptrb:
                        @blk.tensor
                        def _(t):
                            P.w(t, "act")
                            for jt in range(16):
                                i = t.transpose(ptrb[:, jt:jt + 1],
                                                rw2[:, jt * 128:(jt + 1) * 128], idy[:])
                            P.inc(i, "pe")
                        @blk.vector
                        def _(v):
                            P.w(v, "pe")
                            v.tensor_copy(r_cb[:], ptrb[:])
                            v.reciprocal(c_cb[:], ptrb[:])
                    @blk.scalar
                    def _(sc):
                        P.w(sc, "dve")
                        sc.activation(l_cb[:], r_cb[:], AF.Ln)
                    @blk.vector
                    def _(v):
                        P.w(v, "act")
                        v.tensor_reduce(bscr[:, 1:2], rw2[:], axis=AX.X, op=ALU.max)
                        v.tensor_reduce(bscr[:, 2:3], rw1[:], axis=AX.X, op=ALU.min)
                        v.tensor_scalar_mul(bscr[:, 2:3], bscr[:, 2:3], -0.5)

                for b in range(B):
                    bscope = ExitStack()
                    wvres = bscope.enter_context(
                        nc.sbuf_tensor(f"wvres_{b}", [128, NCH, 512], BF16))
                    if b == 0:
                        @blk.gpsimd
                        def _(g):
                            P.w(g, "pe", "dve")
                            P.inc(g.dma_start(out=xb_bf[:],
                                              in_=xf_r[:, :, 0:T]), "gp", 16)
                        emit_B1(0)
                        emit_B1_tail(0)
                    for fi in range(8):
                        if fi >= 2 or not kpre[0]:
                            @blk.sync
                            def _(s, fi=fi):
                                P.wv(s, "pe", wtuse[fi % 2])
                                P.inc(s.dma_start(out=wt[:, fi % 2],
                                                  in_=wqkvbf_r[:, :, 1024 + fi * 128:
                                                               1024 + (fi + 1) * 128]),
                                      "ld", 16)
                        with nc.psum_tensor(f"psB_wk{b}_{fi}", [1, 128], F32) as pws2:
                            @blk.tensor
                            def _(t, fi=fi, pws2=pws2):
                                P.w(t, "ld", "act")
                                for ec in range(NCH):
                                    i = t.matmul(pws2[:], ones_bf[:], wt[:, fi % 2, ec, :],
                                                 start=(ec == 0), stop=(ec == NCH - 1))
                                P.inc(i, "pe")
                            @blk.scalar
                            def _(sc, pws2=pws2):
                                P.w(sc, "pe")
                                sc.mul(nwk[:], pws2[:], -1.0)
                        with nc.psum_tensor(f"psB_k{b}_{fi}", [128, 512], F32) as pk:
                            for tt in range(4):
                                @blk.tensor
                                def _(t, fi=fi, tt=tt, pk=pk, b=b):
                                    P.w(t, "act", "dve")
                                    for c in range(NCH):
                                        t.matmul(pk[:], wt[:, fi % 2, c, :],
                                                 xb_bf[:, c, tt * 512:(tt + 1) * 512],
                                                 start=(c == 0), stop=False)
                                    i = t.matmul(pk[:], nwk[:],
                                                 mu_b_bf[:, tt * 512:(tt + 1) * 512],
                                                 start=False, stop=True)
                                    P.inc(i, "pe")
                                @blk.vector
                                def _(v, fi=fi, tt=tt, pk=pk):
                                    P.w(v, "pe")
                                    v.tensor_copy(kb[:, fi, tt * 512:(tt + 1) * 512], pk[:])
                                    if tt == 3:
                                        v.tensor_reduce(maxkf[:, fi:fi + 1], kb[:, fi, :],
                                                        axis=AX.X, op=ALU.max,
                                                        apply_absolute_value=True)
                            with nc.psum_tensor(f"psB_ko{b}_{fi}", [128, 256], F32) as pko:
                                @blk.tensor
                                def _(t, fi=fi, b=b, pko=pko):
                                    P.w(t, "dve")
                                    for c in range(NCH):
                                        t.matmul(pko[:], wt[:, fi % 2, c, :],
                                                 xa_bf[:, c, b * 256:(b + 1) * 256],
                                                 start=(c == 0), stop=False)
                                    i = t.matmul(pko[:], nwk[:],
                                                 mu_own_bf[:, b * 256:(b + 1) * 256],
                                                 start=False, stop=True)
                                    P.inc(i, "pe")
                                @blk.vector
                                def _(v, fi=fi, pko=pko):
                                    P.w(v, "pe")
                                    v.tensor_copy(kob[:, fi, :], pko[:])
                            wtuse[fi % 2] = P.cnt["pe"]
                    with nc.psum_tensor(f"psB_v0_{b}", [128, 512], F32) as pv0, \
                         nc.psum_tensor(f"psB_v1_{b}", [128, 512], F32) as pv1, \
                         nc.psum_tensor(f"psB_wsv{b}", [1, 512], F32) as pwsv:
                        pvs = (pv0, pv1)
                        for ft in range(2):
                            @blk.sync
                            def _(s, ft=ft):
                                P.w(s, "pe")
                                P.inc(s.dma_start(out=wvres[:],
                                                  in_=wqkvbf_r[:, :, 2048 + ft * 512:
                                                               2048 + (ft + 1) * 512]), "ld", 16)
                            @blk.tensor
                            def _(t, ft=ft):
                                P.w(t, "ld", "act")
                                for ec in range(NCH):
                                    i = t.matmul(pwsv[:], ones_bf[:], wvres[:, ec, :],
                                                 start=(ec == 0), stop=(ec == NCH - 1))
                                P.inc(i, "pe")
                            @blk.scalar
                            def _(sc, ft=ft):
                                P.w(sc, "pe")
                                sc.mul(nwv[:, 0:512], pwsv[:], -1.0)
                            for jt in range(16):
                                pv = pvs[jt % 2]
                                @blk.tensor
                                def _(t, jt=jt, ft=ft, pv=pv, b=b):
                                    P.w(t, "dve", "act")
                                    for ec in range(NCH):
                                        t.matmul(pv[:],
                                                 xb_bf[:, ec, jt * 128:(jt + 1) * 128],
                                                 wvres[:, ec, :],
                                                 start=(ec == 0), stop=False)
                                    i = t.matmul(pv[:],
                                                 mu_b_bf[:, jt * 128:(jt + 1) * 128],
                                                 nwv[:, 0:512],
                                                 start=False, stop=True)
                                    P.inc(i, "pe")
                                @blk.vector
                                def _(v, jt=jt, ft=ft, pv=pv):
                                    P.w(v, "pe")
                                    dst = vb[:].rearrange("p jt (h e) -> p jt h e", h=16)[
                                        :, jt, ft * 8:(ft + 1) * 8, 0:64]
                                    v.tensor_copy(dst, pv[:].rearrange("p (h e) -> p h e", h=8))
                            for sj in range(2):
                                pv = pvs[sj]
                                @blk.tensor
                                def _(t, sj=sj, ft=ft, b=b, pv=pv):
                                    P.w(t, "dve")
                                    for ec in range(NCH):
                                        t.matmul(pv[:],
                                                 xa_bf[:, ec, b * 256 + sj * 128:
                                                        b * 256 + (sj + 1) * 128],
                                                 wvres[:, ec, :],
                                                 start=(ec == 0), stop=False)
                                    i = t.matmul(pv[:],
                                                 mu_own_bf[:, b * 256 + sj * 128:
                                                           b * 256 + (sj + 1) * 128],
                                                 nwv[:, 0:512],
                                                 start=False, stop=True)
                                    P.inc(i, "pe")
                                @blk.vector
                                def _(v, sj=sj, ft=ft, pv=pv):
                                    P.w(v, "pe")
                                    dst = vob[:].rearrange("p jt (h e) -> p jt h e", h=16)[
                                        :, sj, ft * 8:(ft + 1) * 8, 0:64]
                                    v.tensor_copy(dst, pv[:].rearrange("p (h e) -> p h e", h=8))
                        for jt in range(16):
                            @blk.vector
                            def _(v, jt=jt):
                                dst = vb[:].rearrange("p jt (h e) -> p jt h e", h=16)[:, jt, :, 64:65]
                                src2 = bass.AP(c_cb, jt, [list(c_cb.ap().ap[0]), [0, 16], [1, 1]])
                                v.tensor_copy(dst, src2)
                        for sj in range(2):
                            @blk.vector
                            def _(v, sj=sj, b=b):
                                dst = vob[:].rearrange("p jt (h e) -> p jt h e", h=16)[:, sj, :, 64:65]
                                src2 = bass.AP(co_c, b * 2 + sj, [list(co_c.ap().ap[0]), [0, 16], [1, 1]])
                                v.tensor_copy(dst, src2)

                    @blk.vector
                    def _(v):
                        P.w(v, "act", "pe")
                        P.inc(v.tensor_reduce(
                            negb[:], maxkf[:],
                            axis=AX.X, op=ALU.max, apply_absolute_value=True), "dve")
                    with nc.psum_tensor(f"psB_km{b}", [1, 128], F32) as pkm, \
                         nc.psum_tensor(f"psB_nb{b}", [128, 1], F32) as pnb:
                        @blk.tensor
                        def _(t, pkm=pkm):
                            P.w(t, "dve")
                            P.inc(t.transpose(pkm[:], negb[:], iden[:]), "pe")
                        @blk.vector
                        def _(v, pkm=pkm):
                            P.w(v, "pe")
                            v.tensor_reduce(bscr[:, 0:1], pkm[:], axis=AX.X, op=ALU.max)
                            P.inc(v.tensor_mul(bscr[:, 0:1], bscr[:, 0:1], bscr[:, 1:2]), "dve")
                        @blk.vector
                        def _(v):
                            v.tensor_mul(bscr[:, 0:1], bscr[:, 0:1], qmax2[:])
                            v.tensor_scalar_mul(bscr[:, 0:1], bscr[:, 0:1], 8.0)
                            v.tensor_add(bscr[:, 0:1], bscr[:, 0:1], bscr[:, 2:3])
                            P.inc(v.tensor_scalar_mul(bscr[:, 0:1], bscr[:, 0:1], -1.0), "dve")
                        @blk.tensor
                        def _(t, pnb=pnb):
                            P.w(t, "dve")
                            P.inc(t.matmul(pnb[:], ones_f1[:], bscr[:, 0:1],
                                           start=True, stop=True), "pe")
                        @blk.vector
                        def _(v, pnb=pnb):
                            P.w(v, "pe")
                            P.inc(v.tensor_copy(negb[:], pnb[:]), "dve")
                    @blk.vector
                    def _(v, b=b):
                        P.w(v, "dve")
                        lview = bass.AP(l_cb, 0, [list(l_cb.ap().ap[0]), [0, 2], [1, 16]])
                        v.tensor_add(btab[:].rearrange("p (s k) -> p s k", s=2), lview,
                                     deny[:].rearrange("p (s k) -> p s k", s=2))
                        v.tensor_scalar_add(btab[:], btab[:], negb[:])
                        v.tensor_copy(btabd[:], lo_c[:, 2 * b:2 * b + 2])
                        P.inc(v.tensor_scalar_add(btabd[:], btabd[:], negb[:]), "dve")
                    @blk.sync
                    def _(s, b=b):
                        P.w(s, "st", "pe")
                        P.inc(s.dma_start(out=qb[:],
                                          in_=qpark_r[:, :, b * 256:(b + 1) * 256]), "ld", 16)
                    @blk.tensor
                    def _(t):
                        P.w(t, "ld", "dve", "act", "pool")
                    @blk.scalar
                    def _(sc):
                        P.w(sc, "dve", "pool")
                    bscope.close()
                    if b + 1 < B:
                        @blk.gpsimd
                        def _(g, b=b):
                            P.w(g, "pe", "dve")
                            P.inc(g.dma_start(out=xb_bf[:],
                                              in_=xf_r[:, :, (b + 1) * T:(b + 2) * T]),
                                  "gp", 16)

                    with ExitStack() as sctx:
                        ssb = lambda n, sh, dt: sctx.enter_context(nc.sbuf_tensor(n, sh, dt))
                        psb = [[ssb(f"psb{se}{h}_{b}", [128, 512], BF16) for h in range(2)]
                               for se in range(2)]
                        onraw = ssb(f"onraw_{b}", [128, NCH, 256], BF16)
                        onn = ssb(f"onn_{b}", [128, NCH, 256], BF16)
                        rec0, rec1 = rw1, rw2
                        recs = (rec0, rec1)
                        with nc.psum_tensor(f"psB_sc00_{b}", [128, 512], F32) as sc00, \
                             nc.psum_tensor(f"psB_sc01_{b}", [128, 512], F32) as sc01, \
                             nc.psum_tensor(f"psB_sc10_{b}", [128, 512], F32) as sc10, \
                             nc.psum_tensor(f"psB_sc11_{b}", [128, 512], F32) as sc11, \
                             nc.psum_tensor(f"psB_av00_{b}", [128, 512], F32) as av00, \
                             nc.psum_tensor(f"psB_av01_{b}", [128, 512], F32) as av01, \
                             nc.psum_tensor(f"psB_av10_{b}", [128, 512], F32) as av10, \
                             nc.psum_tensor(f"psB_av11_{b}", [128, 512], F32) as av11:
                            sc = ((sc00, sc01), (sc10, sc11))
                            av = ((av00, av01), (av10, av11))
                            expcnt = [0, 0]
                            avevcnt = [(P.cnt["dve"], P.cnt["pool"]),
                                       (P.cnt["dve"], P.cnt["pool"])]
                            tricnt = [P.cnt["dve"]]
                            pending = [None]
                            slot = 0
                            for hpg in range(2):
                                hps = [4 * hpg + i_ for i_ in range(4)]
                                for s_ in range(2):
                                    g2 = slot % 2
                                    units = [(True, 0)] + [(False, k) for k in range(8 * (s_ + 1))]
                                    nu = len(units)
                                    pe_base = P.cnt["pe"]
                                    act_base = P.cnt["act"]
                                    dve_tri = P.cnt["dve"] + 2
                                    @blk.scalar
                                    def _(scl_, units=units, s_=s_, b=b, pe_base=pe_base,
                                          dve_tri=dve_tri):
                                        for u, (isdiag, k) in enumerate(units):
                                            p2, r2 = divmod(u, 2)
                                            P.wv(scl_, "pe", pe_base + 4 * p2 + 1 + r2)
                                            if isdiag:
                                                P.wv(scl_, "dve", dve_tri)
                                                scalev = ro_c[:, 2 * b + s_: 2 * b + s_ + 1]
                                                bia = btabd[:, s_:s_ + 1]
                                            else:
                                                scalev = r_cb[:, k:k + 1]
                                                bia = btab[:, s_ * 16 + k: s_ * 16 + k + 1]
                                            se = u % 2
                                            for h01 in range(2):
                                                scl_.activation(psb[se][h01][:],
                                                                sc[se][h01][:],
                                                                AF.Exp, bias=bia,
                                                                scale=scalev)
                                    assert P.cnt["act"] == act_base + 2 * nu
                                    ui = 0
                                    first_pair = True
                                    while ui < nu:
                                        pu = [(ui + j, units[ui + j][0], units[ui + j][1],
                                               (ui + j) % 2) for j in range(min(2, nu - ui))]
                                        for (uj, isdiag, k, se) in pu:
                                            @blk.tensor
                                            def _(t, s_=s_, isdiag=isdiag, k=k, uj=uj, se=se,
                                                  hps=hps):
                                                P.wv(t, "act", expcnt[se])
                                                if uj == 0:
                                                    P.wv(t, "dve", tricnt[0])
                                                for hi, hp in enumerate(hps):
                                                    lt = (kob[:, hp, s_ * 128:(s_ + 1) * 128]
                                                          if isdiag
                                                          else kb[:, hp, k * 128:(k + 1) * 128])
                                                    t.matmul(sc[se][0][:, hi * 128:(hi + 1) * 128],
                                                             lt[0:64, :],
                                                             qb[0:64, hp, s_ * 128:(s_ + 1) * 128],
                                                             start=True, stop=True,
                                                             tile_position=(0, 0))
                                                    i = t.matmul(
                                                        sc[se][1][:, hi * 128:(hi + 1) * 128],
                                                        lt[64:128, :],
                                                        qb[64:128, hp, s_ * 128:(s_ + 1) * 128],
                                                        start=True, stop=True,
                                                        tile_position=(64, 0))
                                                P.inc(i, "pe")
                                            assert P.cnt["pe"] == pe_base + 4 * (uj // 2) + 1 + uj % 2
                                            if isdiag:
                                                @blk.vector
                                                def _(v, se=se):
                                                    P.w(v, "pe")
                                                    trib = bass.AP(tri, 0,
                                                                   [list(tri.ap().ap[0]),
                                                                    [0, 4], [1, 128]])
                                                    for h01 in range(2):
                                                        v.tensor_add(
                                                            sc[se][h01][:].rearrange(
                                                                "p (hp i) -> p hp i", hp=4),
                                                            sc[se][h01][:].rearrange(
                                                                "p (hp i) -> p hp i", hp=4),
                                                            trib)
                                                tricnt[0] = P.cnt["dve"]
                                                assert tricnt[0] == dve_tri
                                        for (uj, isdiag, k, se) in pu:
                                            @blk.tensor
                                            def _(t, s_=s_, isdiag=isdiag, k=k, uj=uj, se=se,
                                                  hps=hps, g2=g2,
                                                  ecnt=act_base + 2 * (uj + 1),
                                                  last_u=(uj == nu - 1)):
                                                P.wv(t, "act", ecnt)
                                                if uj == 0:
                                                    P.wv(t, "dve", avevcnt[g2][0])
                                                    P.wv(t, "pool", avevcnt[g2][1])
                                                for h01 in range(2):
                                                    for hi, hp in enumerate(hps):
                                                        h = 2 * hp + h01
                                                        vt = (vob[:, s_, h * 65:(h + 1) * 65]
                                                              if isdiag
                                                              else vb[:, k, h * 65:(h + 1) * 65])
                                                        i = t.matmul(
                                                            av[g2][h01][0:65,
                                                                        hi * 128:(hi + 1) * 128],
                                                            vt,
                                                            psb[se][h01][:,
                                                                         hi * 128:(hi + 1) * 128],
                                                            start=(uj == 0 and hi == 0),
                                                            stop=last_u)
                                                P.inc(i, "pe")
                                        ui += len(pu)
                                        if first_pair:
                                            first_pair = False
                                            if pending[0] is not None:
                                                pending[0]()
                                                pending[0] = None
                                    last_even = nu - 1 if (nu - 1) % 2 == 0 else nu - 2
                                    last_odd = nu - 1 if (nu - 1) % 2 == 1 else nu - 2
                                    expcnt[0] = act_base + 2 * (last_even + 1)
                                    expcnt[1] = act_base + 2 * (last_odd + 1)
                                    def _evac(s_=s_, hpg=hpg, g2=g2):
                                        @blk.vector
                                        def _(v):
                                            P.w(v, "pe")
                                            for h01 in range(2):
                                                dst = onraw[h01 * 64:(h01 + 1) * 64,
                                                            4 * hpg:4 * hpg + 4,
                                                            s_ * 128:(s_ + 1) * 128]
                                                v.tensor_copy(dst,
                                                              av[g2][h01][0:64, :].rearrange(
                                                                  "p (c n) -> p c n", c=4))
                                        @blk.vector
                                        def _(v):
                                            for h01 in range(2):
                                                rdst = bass.AP(recs[h01],
                                                               4 * hpg * 256 + s_ * 128,
                                                               [list(recs[h01].ap().ap[0]),
                                                                [256, 4], [1, 128]])
                                                v.tensor_copy(rdst, av[g2][h01][64:65, :]
                                                              .rearrange("p (c n) -> p c n",
                                                                         c=4))
                                        avevcnt[g2] = (P.cnt["dve"], P.cnt["pool"])
                                        @blk.gpsimd
                                        def _(g):
                                            P.w(g, "dve")
                                            for h01 in range(2):
                                                wdst = d_recb.ap()[h01:h01 + 1, :].rearrange(
                                                    "a (c u n) -> a c u n", c=8, u=2)[
                                                    :, 4 * hpg:4 * hpg + 4, s_, :]
                                                wsrc = bass.AP(recs[h01],
                                                               4 * hpg * 256 + s_ * 128,
                                                               [list(recs[h01].ap().ap[0]),
                                                                [256, 4], [1, 128]])
                                                P.inc(g.dma_start(out=wdst, in_=wsrc),
                                                      "gp", 16)
                                            g.wait_ge(P.sem["gp"], P.cnt["gp"])
                                            for h01 in range(2):
                                                bsrc = bass.AP(
                                                    d_recb,
                                                    h01 * 2048 + 4 * hpg * 256 + s_ * 128,
                                                    [[0, 64], [256, 4], [1, 128]])
                                                bdst = scr8[h01 * 64:(h01 + 1) * 64,
                                                            4 * hpg:4 * hpg + 4,
                                                            s_ * 128:(s_ + 1) * 128]
                                                P.inc(g.dma_start(out=bdst, in_=bsrc),
                                                      "gp", 16)
                                        @blk.vector
                                        def _(v):
                                            P.w(v, "gp")
                                            for h01 in range(2):
                                                sv = scr8[h01 * 64:(h01 + 1) * 64,
                                                          4 * hpg:4 * hpg + 4,
                                                          s_ * 128:(s_ + 1) * 128]
                                                v.reciprocal(sv, sv)
                                    pending[0] = _evac
                                    slot += 1
                            if pending[0] is not None:
                                pending[0]()
                                pending[0] = None
                        actB6 = P.cnt["act"]
                        if b + 1 < B:
                            emit_B1(b + 1)
                        @blk.vector
                        def _(v):
                            P.w(v, "gp", "dve")
                            P.inc(v.tensor_mul(onn[:], onraw[:], scr8[:]), "dve")
                        mulcnt = P.cnt["dve"]
                        if b + 1 < B:
                            emit_B1_tail(b + 1)
                        if debug and b == 1:
                            @blk.sync
                            def _(s):
                                P.w(s, "dve")
                                P.inc(s.dma_start(out=d_ton,
                                                  in_=onn[:].rearrange("p c n -> p (c n)")),
                                      "st", 16)
                                P.inc(s.dma_start(out=d_tor,
                                                  in_=kob[:].rearrange("p c n -> p (c n)")),
                                      "st", 16)
                                P.inc(s.dma_start(out=d_tyt,
                                                  in_=vob[:, 0, 0:256]), "st", 16)
                                P.inc(s.dma_start(out=d_tpy,
                                                  in_=vob[:, 0, 256:512]), "st", 16)
                                P.inc(s.dma_start(out=d_tfa,
                                                  in_=scr8[:].rearrange("p c n -> p (c n)")),
                                      "st", 16)
                                P.inc(s.dma_start(out=d_trc[0:1, 0:2048], in_=bass.AP(ro_c, 0, [[16, 128], [1, 16]]).rearrange("p c -> (p c)")[None, :] if False else rec0[:]), "st", 16)
                                P.inc(s.dma_start(out=d_trc[1:2, :], in_=rec1[:]), "st", 16)
                                P.inc(s.dma_start(out=d_twt[:, 0:16], in_=ro_c[:]), "st", 16)
                                P.inc(s.dma_start(out=d_twt[:, 16:32], in_=c_cb[:]), "st", 16)
                                P.inc(s.dma_start(out=d_twt[:, 32:48], in_=co_c[:]), "st", 16)
                                P.inc(s.dma_start(out=d_twt[:, 48:64], in_=r_cb[:]), "st", 16)
                                s.wait_ge(P.sem["st"], P.cnt["st"])
                        @blk.sync
                        def _(s, b=b, mulcnt=mulcnt):
                            P.w(s, "act", "pool", "st", "gp")
                            P.wv(s, "dve", mulcnt)
                            P.inc(s.dma_start(out=scr8[:],
                                              in_=xo_r[:, :, b * 256:(b + 1) * 256]), "ld", 16)
                        with nc.psum_tensor(f"psB_y0_{b}", [128, 256], F32) as py0, \
                             nc.psum_tensor(f"psB_y1_{b}", [128, 256], F32) as py1:
                            pys = (py0, py1)
                            wtuse2 = wtuse
                            for fc in range(8):
                                py = pys[fc % 2]
                                @blk.sync
                                def _(s, fc=fc):
                                    P.wv(s, "pe", wtuse2[fc % 2])
                                    P.w(s, "dve") if fc == 0 else None
                                    P.inc(s.dma_start(out=wt[:, fc % 2],
                                                      in_=woutbf_r[:, :, fc * 128:(fc + 1) * 128]),
                                          "ld", 16)
                                @blk.tensor
                                def _(t, fc=fc, py=py, mulcnt=mulcnt, actB6=actB6):
                                    P.w(t, "ld")
                                    P.wv(t, "dve", mulcnt)
                                    if fc == 0:
                                        P.wv(t, "act", actB6)
                                    for ec in range(NCH):
                                        i = t.matmul(py[:], wt[:, fc % 2, ec, :], onn[:, ec, :],
                                                     start=(ec == 0), stop=(ec == NCH - 1))
                                    P.inc(i, "pe")
                                @blk.vector
                                def _(v, fc=fc, py=py):
                                    P.w(v, "pe", "ld", "st")
                                    v.tensor_add(yt8[:, fc % 2, :], py[:], scr8[:, fc, :])
                                wtuse2[fc % 2] = P.cnt["pe"]
                                if fc % 2 == 1:
                                    @blk.sync
                                    def _(s, fc=fc, b=b):
                                        P.w(s, "dve")
                                        P.inc(s.dma_start(
                                            out=d_ypark.ap().rearrange("(c p) n -> p c n", p=128)[
                                                :, fc - 1:fc + 1, b * 256:(b + 1) * 256],
                                            in_=yt8[:]), "st", 16)
                            if b + 1 < B:
                                for fi in range(2):
                                    @blk.sync
                                    def _(s, fi=fi):
                                        P.wv(s, "pe", wtuse2[fi])
                                        P.inc(s.dma_start(
                                            out=wt[:, fi],
                                            in_=wqkvbf_r[:, :, 1024 + fi * 128:
                                                         1024 + (fi + 1) * 128]), "ld", 16)
                                kpre[0] = True
                    @blk.scalar
                    def _(sc):
                        P.w(sc, "gp", "dve")
                    @blk.vector
                    def _(v):
                        P.w(v, "gp", "st")
                    for seg in (2 * b, 2 * b + 1):
                        w8 = FFN_ORDER[seg // 2]
                        half = seg % 2
                        @blk.gpsimd
                        def _(g, w8=w8, half=half):
                            P.w(g, "st")
                            P.inc(g.dma_start(out=wstage_b[:],
                                              in_=w_r[w8][:, :, half * 512:(half + 1) * 512]),
                                  "gp", 16)
                        @blk.sync
                        def _(s, seg=seg):
                            P.w(s, "gp")
                            P.inc(s.dma_start(
                                out=wffnbf_r[:, :, seg * 512:(seg + 1) * 512],
                                in_=wstage_b[:]), "st", 16)
                    if debug and b == 0:
                        @blk.sync
                        def _(s):
                            P.inc(s.dma_start(out=d_tkb, in_=kb[:].rearrange("p c n -> p (c n)")),
                                  "st", 16)
                            P.inc(s.dma_start(out=d_tnb, in_=negb[:]), "st", 16)
                            s.wait_ge(P.sem["st"], P.cnt["st"])
        ypark_r = d_ypark.ap().rearrange("(c p) n -> p c n", p=128)

        with ExitStack() as cctx:
            csb = lambda n, sh, dt: cctx.enter_context(nc.sbuf_tensor(n, sh, dt))
            z = csb("z", [128, NCH, OWN], BF16)
            cuo = csb("cuo", [128, NCH, 256], F32)
            wt_c = csb("wt_c", [128, 2, NCH, 128], BF16)
            ysl = csb("ysl", [128, NCH, 256], F32)
            ybf_c = csb("ybf_c", [128, NCH, 256], BF16)
            tmp_c = csb("tmp_c", [128, 256], F32)

            with nc.psum_tensor("psC_su", [1, 2048], F32) as psu, \
                 nc.psum_tensor("psC_sq", [1, 2048], F32) as psq:
                for sl in range(8):
                    @blk.sync
                    def _(s, sl=sl):
                        P.w(s, "dve", "pe")
                        P.inc(s.dma_start(out=ysl[:, :, :],
                                          in_=ypark_r[:, :, sl * 256:(sl + 1) * 256]), "ld", 16)
                    @blk.vector
                    def _(v, sl=sl):
                        P.w(v, "ld", "pe")
                        v.tensor_copy(ybf_c[:], ysl[:, :, :])
                    @blk.tensor
                    def _(t, sl=sl):
                        P.w(t, "dve")
                        for c in range(NCH):
                            i = t.matmul(psu[:, sl * 256:(sl + 1) * 256], ones_bf[:],
                                         ybf_c[:, c, :], start=(c == 0), stop=(c == NCH - 1))
                        P.inc(i, "pe")
                    @blk.vector
                    def _(v, sl=sl):
                        P.w(v, "pe")
                        v.tensor_mul(ybf_c[:], ysl[:, :, :], ysl[:, :, :])
                    @blk.tensor
                    def _(t, sl=sl):
                        P.w(t, "dve")
                        for c in range(NCH):
                            i = t.matmul(psq[:, sl * 256:(sl + 1) * 256], ones_bf[:],
                                         ybf_c[:, c, :], start=(c == 0), stop=(c == NCH - 1))
                        P.inc(i, "pe")
                @blk.scalar
                def _(sc):
                    P.w(sc, "pe")
                    sc.mul(rw1[:], psu[:], 1.0 / D)
                    sc.mul(rw2[:], psq[:], 1.0 / D)
            @blk.vector
            def _(v):
                P.w(v, "act")
                for sl in range(8):
                    v.tensor_mul(tmp_c[0:1, 0:256], rw1[:, sl * 256:(sl + 1) * 256],
                                 rw1[:, sl * 256:(sl + 1) * 256])
                    v.tensor_sub(rw2[:, sl * 256:(sl + 1) * 256],
                                 rw2[:, sl * 256:(sl + 1) * 256], tmp_c[0:1, 0:256])
            @blk.scalar
            def _(sc):
                P.w(sc, "dve")
                sc.activation(rw2[:], rw2[:], AF.Ln, bias=epsc[:])
                sc.activation(rw2[:], rw2[:], AF.Exp, scale=-0.5)
            @blk.gpsimd
            def _(g):
                P.w(g, "act", "gp")
                P.inc(g.dma_start(out=d_rowb.ap(), in_=rw1[:]), "gp", 16)
                g.wait_ge(P.sem["gp"], P.cnt["gp"])
                P.inc(g.dma_start(out=d_recb.ap()[0:1, :], in_=rw2[:]), "gp", 16)
            with nc.sbuf_tensor("mu2bc", [128, OWN], F32) as mu2bc, \
                 nc.sbuf_tensor("r2bc", [128, OWN], F32) as r2bc:
                @blk.gpsimd
                def _(g):
                    P.w(g, "gp")
                    P.inc(g.dma_start(out=mu2bc[:],
                                      in_=d_rowb.ap().partition_broadcast(128)[:, 0, :]),
                          "gp", 16)
                    P.inc(g.dma_start(out=r2bc[:],
                                      in_=bass.AP(d_recb, 0, [[0, 128], [1, OWN]])), "gp", 16)
                for sl in range(8):
                    @blk.sync
                    def _(s, sl=sl):
                        P.w(s, "dve", "pe")
                        P.inc(s.dma_start(out=ysl[:, :, :],
                                          in_=ypark_r[:, :, sl * 256:(sl + 1) * 256]), "ld", 16)
                    @blk.vector
                    def _(v, sl=sl):
                        P.w(v, "gp", "ld")
                        mub = bass.AP(mu2bc, sl * 256, [list(mu2bc.ap().ap[0]), [0, NCH], [1, 256]])
                        rbb = bass.AP(r2bc, sl * 256, [list(r2bc.ap().ap[0]), [0, NCH], [1, 256]])
                        v.tensor_sub(ysl[:, :, :], ysl[:, :, :], mub)
                        v.tensor_mul(z[:, :, sl * 256:(sl + 1) * 256], ysl[:, :, :], rbb)
            rctx = ExitStack()
            wres = {w: rctx.enter_context(nc.sbuf_tensor("w_" + w, [128, NCH, D], BF16))
                    for w in ("wf", "wi", "wc", "uf", "ui", "uc")}
            cst = rctx.enter_context(nc.sbuf_tensor("cst", [128, NCH, 256], F32))
            cbf = rctx.enter_context(nc.sbuf_tensor("cbf", [128, NCH, 256], BF16))
            gates = {g: rctx.enter_context(nc.sbuf_tensor("g_" + g, [128, NCH, 256], F32))
                     for g in ("f", "i", "c")}

            @blk.sync
            def _(s):
                P.w(s, "dve")
                for wi_, w8 in enumerate(("wf", "wi", "wc", "uf", "ui", "uc")):
                    P.inc(s.dma_start(out=wres[w8][:],
                                      in_=wffnbf_r[:, :, wi_ * 1024:(wi_ + 1) * 1024]),
                          "ld", 16)
            for c in range(NCH):
                @blk.vector
                def _(v, c=c):
                    P.w(v, "gp")
                    src = bass.AP(bv["incv"], c, [list(bv["incv"].ap().ap[0]), [0, 256]])
                    P.inc(v.tensor_copy(cst[:, c, :], src), "dve")
            @blk.vector
            def _(v):
                P.w(v, "dve", "pool")
                P.inc(v.tensor_copy(cbf[:], cst[:]), "dve")
            gnames = (("f", "wf", "uf", "bfv", AF.Sigmoid),
                      ("i", "wi", "ui", "biv", AF.Sigmoid),
                      ("c", "wc", "uc", "bcv", AF.Tanh))
            with nc.psum_tensor("psC_g0", [128, 256], F32) as pg0, \
                 nc.psum_tensor("psC_g1", [128, 256], F32) as pg1, \
                 nc.psum_tensor("psC_g2", [128, 256], F32) as pg2, \
                 nc.psum_tensor("psC_g3", [128, 256], F32) as pg3:
                pgs = (pg0, pg1, pg2, pg3)
                pguse = [0, 0, 0, 0]
                gi = 0
                for t_ in range(B):
                    for (gn, wn, un, bn, af) in gnames:
                        for mc in range(NCH):
                            bi = gi % 4
                            gi += 1
                            pg = pgs[bi]
                            @blk.tensor
                            def _(t, t_=t_, wn=wn, un=un, mc=mc, pg=pg, bi=bi):
                                P.wv(t, "act", pguse[bi])
                                if mc == 0 and gn == "f":
                                    P.w(t, "dve", "pool", "ld")
                                for ec in range(NCH):
                                    t.matmul(pg[:], wres[wn][:, ec, mc * 128:(mc + 1) * 128],
                                             z[:, ec, t_ * 256:(t_ + 1) * 256],
                                             start=(ec == 0), stop=False)
                                for ec in range(NCH):
                                    i = t.matmul(pg[:], wres[un][:, ec, mc * 128:(mc + 1) * 128],
                                                 cbf[:, ec, :],
                                                 start=False, stop=(ec == NCH - 1))
                                P.inc(i, "pe")
                            @blk.scalar
                            def _(sc, gn=gn, bn=bn, af=af, mc=mc, pg=pg, bi=bi):
                                P.w(sc, "pe")
                                sc.activation(gates[gn][:, mc, :], pg[:], af,
                                              bias=bv[bn][:, mc:mc + 1])
                                pguse[bi] = P.cnt["act"]
                    for mc in range(NCH):
                        dec = blk.vector if mc % 2 == 0 else blk.gpsimd
                        nm = "dve" if mc % 2 == 0 else "pool"
                        @dec
                        def _(e, mc=mc, nm=nm):
                            P.w(e, "act")
                            scratch = tmp_c[:, :] if nm == "dve" else ysl[:, 0, :]
                            e.tensor_mul(cst[:, mc, :], cst[:, mc, :], gates["f"][:, mc, :])
                            e.tensor_mul(scratch, gates["i"][:, mc, :], gates["c"][:, mc, :])
                            e.tensor_add(cst[:, mc, :], cst[:, mc, :], scratch)
                    @blk.vector
                    def _(v, t_=t_):
                        P.w(v, "dve", "pool")
                        v.tensor_copy(cbf[:], cst[:])
            wcuse = [0, 0]
            for mc in range(NCH):
                @blk.sync
                def _(s, mc=mc):
                    P.wv(s, "pe", wcuse[mc % 2])
                    if mc == 0:
                        P.w(s, "dve")
                    P.inc(s.dma_start(out=wt_c[:, mc % 2],
                                      in_=wffnbf_r[:, :, 7 * 1024 + mc * 128:
                                                   7 * 1024 + (mc + 1) * 128]), "ld", 16)
                with nc.psum_tensor(f"psC_uo_{mc}", [128, 256], F32) as pu:
                    @blk.tensor
                    def _(t, mc=mc, pu=pu):
                        P.w(t, "ld", "act", "dve", "pool")
                        for ec in range(NCH):
                            i = t.matmul(pu[:], wt_c[:, mc % 2, ec, :], cbf[:, ec, :],
                                         start=(ec == 0), stop=(ec == NCH - 1))
                        P.inc(i, "pe")
                    wcuse[mc % 2] = P.cnt["pe"]
                    @blk.vector
                    def _(v, mc=mc, pu=pu):
                        P.w(v, "pe")
                        v.tensor_copy(cuo[:, mc, :], pu[:])
            rctx.close()
            with ExitStack() as octx:
                wo_res = octx.enter_context(nc.sbuf_tensor("wo_res", [128, NCH, D], BF16))
                ystage = octx.enter_context(nc.sbuf_tensor("ystage", [128, NCH, 256], F32))
                yo_r = d_yo.rearrange("(c p) n -> p c n", p=128)
                @blk.sync
                def _(s):
                    P.w(s, "pe")
                    P.inc(s.dma_start(out=wo_res[:],
                                      in_=wffnbf_r[:, :, 6 * 1024:7 * 1024]), "ld", 16)
                for bt in range(B):
                    @blk.sync
                    def _(s, bt=bt):
                        P.w(s, "dve", "st")
                        P.inc(s.dma_start(out=ysl[:, :, :],
                                          in_=ypark_r[:, :, bt * 256:(bt + 1) * 256]),
                              "ld", 16)
                    for mc in range(NCH):
                        with nc.psum_tensor(f"psC_o_{bt}_{mc}", [128, 256], F32) as po:
                            @blk.tensor
                            def _(t, mc=mc, bt=bt, po=po):
                                P.w(t, "ld")
                                P.w(t, "dve")
                                for ec in range(NCH):
                                    i = t.matmul(po[:], wo_res[:, ec, mc * 128:(mc + 1) * 128],
                                                 z[:, ec, bt * 256:(bt + 1) * 256],
                                                 start=(ec == 0), stop=(ec == NCH - 1))
                                P.inc(i, "pe")
                            @blk.vector
                            def _(v, mc=mc, po=po):
                                P.w(v, "pe", "act")
                                P.inc(v.tensor_add(tmp_c[:], po[:], cuo[:, mc, :]), "dve")
                            @blk.scalar
                            def _(sc, mc=mc):
                                P.w(sc, "dve")
                                P.inc(sc.activation(tmp_c[:], tmp_c[:], AF.Tanh,
                                                    bias=bv["bov"][:, mc:mc + 1]), "act")
                            @blk.vector
                            def _(v, mc=mc):
                                P.w(v, "act", "ld")
                                P.inc(v.tensor_add(ystage[:, mc, :], tmp_c[:], ysl[:, mc, :]),
                                      "dve")
                    @blk.sync
                    def _(s, bt=bt):
                        P.w(s, "dve")
                        P.inc(s.dma_start(out=yo_r[:, :, bt * 256:(bt + 1) * 256],
                                          in_=ystage[:, :, :]), "st", 16)
        @blk.sync
        def _(s):
            P.wall(s)
    return nc


_NC_CACHE = {}


def kernel(L0_attn_query, L0_attn_key, L0_attn_value, L0_attn_in_proj_weight,
           L0_attn_out_proj_weight, L0_ffn_wf, L0_ffn_uf, L0_ffn_bf, L0_ffn_wi,
           L0_ffn_ui, L0_ffn_bi, L0_ffn_wc, L0_ffn_uc, L0_ffn_bc, L0_ffn_wo,
           L0_ffn_uo, L0_ffn_bo, L0_ffn_in_c):
    x = np.asarray(L0_attn_query, np.float32)
    xf = np.ascontiguousarray(x.transpose(2, 0, 1).reshape(D, B * T))
    wqkv = np.ascontiguousarray(np.asarray(L0_attn_in_proj_weight, np.float32).T)
    wout = np.ascontiguousarray(np.asarray(L0_attn_out_proj_weight, np.float32).T)
    wmats = {k: np.ascontiguousarray(np.asarray(v, np.float32).T) for k, v in
             dict(wf=L0_ffn_wf, wi=L0_ffn_wi, wc=L0_ffn_wc, uf=L0_ffn_uf,
                  ui=L0_ffn_ui, uc=L0_ffn_uc, wo=L0_ffn_wo, uo=L0_ffn_uo).items()}
    bvecs = {k: np.ascontiguousarray(np.asarray(v, np.float32).reshape(NCH, 128).T)
             for k, v in dict(bfv=L0_ffn_bf, biv=L0_ffn_bi, bcv=L0_ffn_bc,
                              bov=L0_ffn_bo, incv=L0_ffn_in_c).items()}
    jj = np.arange(128)
    trimask = np.where(jj[:, None] > jj[None, :], np.float32(NEG), np.float32(0.0))
    trimask = np.ascontiguousarray(trimask.astype(np.float32))

    in_maps = []
    for core in range(8):
        blks = (core, 8 + core)
        xo = np.empty((D, OWN), np.float32)
        for b in range(B):
            for s_, blkid in enumerate(blks):
                xo[:, b * 256 + s_ * 128: b * 256 + (s_ + 1) * 128] = \
                    x[b, blkid * 128:(blkid + 1) * 128, :].T
        deny = np.zeros((128, 32), np.float32)
        for s_, blkid in enumerate(blks):
            for k in range(16):
                if k >= blkid:
                    deny[:, s_ * 16 + k] = NEG
        m = dict(xf=xf, xo=np.ascontiguousarray(xo), wqkv=wqkv, wout=wout,
                 deny=deny, trimask=trimask, iden=np.eye(128, dtype=np.float32))
        m.update(wmats)
        m.update(bvecs)
        in_maps.append(m)

    dbg = bool(_NC_CACHE.get("debug"))
    key = ("nc", dbg)
    if key not in _NC_CACHE:
        _NC_CACHE[key] = build_nc(debug=dbg)
    nc = _NC_CACHE[key]
    res = run_bass_kernel_spmd(nc, in_maps, core_ids=list(range(8)))

    if dbg:
        _NC_CACHE["results"] = res.results
    out = np.empty((B, T, D), np.float32)
    for core in range(8):
        yo = res.results[core]["yo"]
        for b in range(B):
            for s_, blkid in enumerate((core, 8 + core)):
                out[b, blkid * 128:(blkid + 1) * 128, :] = \
                    yo[:, b * 256 + s_ * 128: b * 256 + (s_ + 1) * 128].T
    return out
